# version 09
# baseline (speedup 1.0000x reference)
import math
import os as _os
from contextlib import ExitStack
A_STAGE = _os.environ.get("KERNEL_A_STAGE", "all")

import numpy as np
import ml_dtypes

import concourse.bass as bass
import concourse.mybir as mybir
from concourse.bass_utils import run_bass_kernel_spmd

F32 = mybir.dt.float32
BF16 = mybir.dt.bfloat16
AF = mybir.ActivationFunctionType
ALU = mybir.AluOpType
AX = mybir.AxisListType

B, S, D = 4, 8192, 1024
H3 = 3 * D
NCORES = 8
RPC = S // 2
NCHUNK = 16
SEG = 256
NH, DH, NPAIR = 16, 64, 8
FF = 4096
EXP_BIAS = 12.0
LN_EPS = 1e-5
LR = 0.001

ENGINES = ("sync", "tensor", "vector", "scalar", "gpsimd")


class Prog:
    def __init__(self, nc):
        self.nc = nc
        self.ops = {e: [] for e in ENGINES}
        self.cnt = {e: 0 for e in ENGINES}
        self.waited = {e: set() for e in ENGINES}
        self.dma_cnt = {}
        self.dma_streams = {}
        self._es = None
        self.sems = {}

    def init_sems(self, es):
        for e in ENGINES:
            self.sems[e] = es.enter_context(self.nc.semaphore(f"sem_{e}"))
        self._es = es

    def dma_stream(self, name, queue_engine):
        if name not in self.dma_streams:
            sem = self._es.enter_context(self.nc.semaphore(f"dma_{name}"))
            self.dma_streams[name] = (queue_engine, sem)
            self.dma_cnt[name] = 0
        return name

    def op(self, engine, fn):
        self.cnt[engine] += 1
        tok = (engine, self.cnt[engine])
        self.ops[engine].append(("op", fn, tok))
        return tok

    def tensor(self, fn):
        return self.op("tensor", fn)

    def vector(self, fn):
        return self.op("vector", fn)

    def scalar(self, fn):
        return self.op("scalar", fn)

    def gpsimd(self, fn):
        return self.op("gpsimd", fn)

    def sync_op(self, fn):
        return self.op("sync", fn)

    def dma(self, stream, out, in_, **kw):
        qe, _sem = self.dma_streams[stream]
        self.dma_cnt[stream] += 16
        fn = lambda eng, out=out, in_=in_, kw=kw: eng.dma_start(out=out, in_=in_, **kw)
        self.ops[qe].append(("dma", fn, stream))
        return self.dma_cnt[stream]

    def wait(self, engine, tok):
        if tok is None:
            return
        (pe, t) = tok
        if pe == engine:
            return
        self.waited[pe].add(t)
        self.ops[engine].append(("wait", "op", pe, t))

    def cc(self, fn):
        if not hasattr(self, "cc_sem"):
            self.cc_sem = self._es.enter_context(self.nc.semaphore("cc_sem"))
            self.cc_cnt = 0
        self.cc_cnt += 1
        self.ops["gpsimd"].append(("cc", fn, self.cc_cnt))
        return self.cc_cnt

    def wait_cc(self, engine):
        if getattr(self, "cc_cnt", 0) == 0:
            return
        self.ops[engine].append(("wait", "cc", None, self.cc_cnt))

    def wait_dma(self, engine, stream, upto=None):
        val = self.dma_cnt[stream] if upto is None else upto
        if val == 0:
            return
        self.ops[engine].append(("wait", "dma", stream, val))

    def barrier(self):
        toks = {e: self.cnt[e] for e in ENGINES if self.cnt[e] > 0}
        dmav = dict(self.dma_cnt)
        for e in ENGINES:
            for pe, t in toks.items():
                if pe != e:
                    self.waited[pe].add(t)
                    self.ops[e].append(("wait", "op", pe, t))
            for s, v in dmav.items():
                if v > 0:
                    self.ops[e].append(("wait", "dma", s, v))
            self.wait_cc(e)

    def emit_all(self, block):
        rank = {}
        for e in ENGINES:
            w = sorted(self.waited[e])
            rank[e] = {t: i + 1 for i, t in enumerate(w)}

        def run_engine(ename):
            def body(eng):
                for entry in self.ops[ename]:
                    kind = entry[0]
                    if kind == "op":
                        _, fn, tok = entry
                        inst = fn(eng)
                        if tok[1] in self.waited[ename]:
                            inst.then_inc(self.sems[ename], 1)
                    elif kind == "dma":
                        _, fn, stream = entry
                        fn(eng).then_inc(self.dma_streams[stream][1], 16)
                    elif kind == "cc":
                        _, fn, _n = entry
                        fn(eng).then_inc(self.cc_sem, 1)
                    else:
                        _, wkind, key, val = entry
                        if wkind == "op":
                            eng.wait_ge(self.sems[key], rank[key][val])
                        elif wkind == "cc":
                            eng.wait_ge(self.cc_sem, val)
                        else:
                            eng.wait_ge(self.dma_streams[key][1], val)
            return body

        block.sync(run_engine("sync"))
        block.tensor(run_engine("tensor"))
        block.vector(run_engine("vector"))
        block.scalar(run_engine("scalar"))
        block.gpsimd(run_engine("gpsimd"))


class Ctx:
    def __init__(self, nc, prog, es):
        self.nc = nc
        self.p = prog
        self.es = es
        self.scopes = [es]
        self.g = {}

    def sb(self, name, shape, dt=F32):
        return self.scopes[-1].enter_context(self.nc.sbuf_tensor(name, shape, dt))

    def psum(self, name, shape, dt=F32):
        return self.scopes[-1].enter_context(self.nc.psum_tensor(name, shape, dt))

    def scope(self):
        ctx = self

        class _Scope:
            def __enter__(self_s):
                self_s.es = ExitStack()
                ctx.scopes.append(self_s.es)
                return ctx

            def __exit__(self_s, *a):
                if a[0] is None:
                    ctx.p.barrier()
                ctx.scopes.pop()
                return self_s.es.__exit__(*a)

        return _Scope()


def bias_chunks(ctx, dram_ap, n, tag, scale=1.0):
    p = ctx.p
    out = ctx.sb(f"bchunks_{tag}", [128, n], F32)
    with ctx.scope():
        row = ctx.sb(f"brow_{tag}", [1, n * 128], F32)
        ps = ctx.psum(f"psb_{tag}", [128, 1], F32)
        p.dma("ld", row[:], dram_ap)
        p.wait_dma("tensor", "ld")
        for j in range(n):
            tmm = p.tensor(lambda e, j=j: e.matmul(
                ps[:], row[0:1, j * 128:(j + 1) * 128], ctx.one11[:], start=True, stop=True))
            p.wait("vector", tmm)
            if scale == 1.0:
                last = p.vector(lambda e, j=j: e.tensor_copy(out[:, j:j + 1], ps[:]))
            else:
                last = p.vector(lambda e, j=j: e.tensor_scalar_mul(out[:, j:j + 1], ps[:], scale))
            p.wait("tensor", last)
    return out


def emit_ln(ctx, lnt, ycol, N, store_xf, store_xb, ps_stat, ps_rep, st_stream="st"):
    p = ctx.p
    y2t, srow, mrow, vrow, rrow, mr2, rep, xbt = lnt

    ps_s, ps_q = ps_stat
    mm_q = None
    for c in range(8):
        p.wait("vector", mm_q)
        tsq = p.vector(lambda e, c=c: e.tensor_mul(
            y2t[:], ycol[:, c * N:(c + 1) * N], ycol[:, c * N:(c + 1) * N]))
        p.tensor(lambda e, c=c: e.matmul(
            ps_s[:], ctx.ones_col[:], ycol[:, c * N:(c + 1) * N],
            start=(c == 0), stop=(c == 7)))
        p.wait("tensor", tsq)
        mm_q = p.tensor(lambda e, c=c: e.matmul(
            ps_q[:], ctx.ones_col[:], y2t[:],
            start=(c == 0), stop=(c == 7)))
    p.wait("vector", mm_q)
    p.vector(lambda e: e.tensor_copy(srow[0:1, 0:N], ps_s[:]))
    tcp = p.vector(lambda e: e.tensor_copy(srow[0:1, N:2 * N], ps_q[:]))
    p.wait("tensor", tcp)
    p.vector(lambda e: e.tensor_scalar_mul(mrow[:], srow[0:1, 0:N], 1.0 / D))
    p.vector(lambda e: e.tensor_scalar_mul(vrow[:], srow[0:1, N:2 * N], 1.0 / D))
    p.vector(lambda e: e.tensor_mul(mr2[:], mrow[:], mrow[:]))
    tv = p.vector(lambda e: e.tensor_sub(vrow[:], vrow[:], mr2[:]))
    p.wait("scalar", tv)
    p.scalar(lambda e: e.activation(rrow[:], vrow[:], AF.Ln, bias=ctx.eps1[:], scale=1.0))
    tr = p.scalar(lambda e: e.activation(rrow[:], rrow[:], AF.Exp, bias=ctx.zero1[:], scale=-0.5))
    p.wait("vector", tr)
    tmr = p.vector(lambda e: e.tensor_mul(mr2[:], mrow[:], rrow[:]))
    p.wait("tensor", tmr)
    bm1 = p.tensor(lambda e: e.matmul(ps_rep[:, 0:N], ctx.ones_row[:], rrow[:], start=True, stop=True))
    bm2 = p.tensor(lambda e: e.matmul(ps_rep[:, N:2 * N], ctx.ones_row[:], mr2[:], start=True, stop=True))
    p.wait("vector", bm1)
    p.wait("vector", bm2)
    tcr = p.vector(lambda e: e.tensor_copy(rep[:], ps_rep[:, 0:2 * N]))
    p.wait("tensor", tcr)
    for c in range(8):
        p.vector(lambda e, c=c: e.tensor_mul(ycol[:, c * N:(c + 1) * N],
                                             ycol[:, c * N:(c + 1) * N], rep[:, 0:N]))
        tfc = p.vector(lambda e, c=c: e.tensor_sub(ycol[:, c * N:(c + 1) * N],
                                                   ycol[:, c * N:(c + 1) * N], rep[:, N:2 * N]))
        p.wait("gpsimd" if st_stream == "st" else "sync", tfc)
        store_xf(c, ycol[:, c * N:(c + 1) * N])
        if store_xb is not None:
            p.wait_dma("vector", st_stream)
            tbc = p.vector(lambda e, c=c: e.tensor_copy(xbt[:], ycol[:, c * N:(c + 1) * N]))
            p.wait("gpsimd" if st_stream == "st" else "sync", tbc)
            store_xb(c, xbt[:])
    p.wait_dma("vector", st_stream)
    return tfc


def make_ln_tiles(ctx, N, tag):
    return (
        ctx.sb(f"ln_y2_{tag}", [128, N], F32),
        ctx.sb(f"ln_srow_{tag}", [1, 2 * N], F32),
        ctx.sb(f"ln_mrow_{tag}", [1, N], F32),
        ctx.sb(f"ln_vrow_{tag}", [1, N], F32),
        ctx.sb(f"ln_rrow_{tag}", [1, N], F32),
        ctx.sb(f"ln_mr2_{tag}", [1, N], F32),
        ctx.sb(f"ln_rep_{tag}", [128, 2 * N], F32),
        ctx.sb(f"ln_xb_{tag}", [128, N], BF16),
    )


def emit_rope(ctx, tmp, ct, st, out_bf, rop1, rop2, N):
    p = ctx.p
    p.vector(lambda e: e.tensor_mul(rop1[:], tmp[:], ct[:]))
    for q in range(4):
        a, b = q * 32, (q ^ 1) * 32
        p.vector(lambda e, a=a, b=b: e.tensor_copy(rop2[a:a + 32, :], tmp[b:b + 32, :]))
    p.vector(lambda e: e.tensor_mul(rop2[:], rop2[:], st[:]))
    return p.vector(lambda e: e.tensor_add(out_bf[:], rop1[:], rop2[:]))


def emit_phase_a(ctx_outer):
    with ctx_outer.scope() as ctx:
        _emit_phase_a(ctx)


def _emit_phase_a(ctx):
    p, g, nc = ctx.p, ctx.g, ctx.nc
    NBLK = 16

    wqkv_b = ctx.sb("wqkv_b", [128, 8 * H3], BF16)
    wout_b = ctx.sb("wout_b", [128, 8 * D], BF16)
    bqk = bias_chunks(ctx, g["b_qkv"][0:1, 0:2048], 16, "bqk")
    bo = bias_chunks(ctx, g["b_out"][0:1, :], 8, "bo")
    if _os.environ.get("KERNEL_DEBUG_BIAS") == "1":
        dbg = nc.dram_tensor("bqk_dbg", [128, 16], F32, kind="ExternalOutput").ap()
        p.barrier()
        p.dma("st", dbg[:], bqk[:])
    bv_b = ctx.sb("bv_b", [1, D], BF16)
    o_all = ctx.sb("o_all", [128, NPAIR * RPC], BF16)
    with ctx.scope():
        wtmp = ctx.sb("wtmp_a", [128, H3], F32)
        for dc in range(8):
            p.dma("ld", wtmp[:], g["w_qkv"][dc * 128:(dc + 1) * 128, :])
            p.wait_dma("vector", "ld")
            tcast = p.vector(lambda e, dc=dc: e.tensor_copy(wqkv_b[:, dc * H3:(dc + 1) * H3], wtmp[:]))
            p.wait("sync", tcast)
        for dc in range(8):
            p.dma("ld", wtmp[:, 0:D], g["w_out"][dc * 128:(dc + 1) * 128, :])
            p.wait_dma("vector", "ld")
            tcast = p.vector(lambda e, dc=dc: e.tensor_copy(wout_b[:, dc * D:(dc + 1) * D], wtmp[:, 0:D]))
            p.wait("sync", tcast)
        p.dma("ld", wtmp[0:1, 0:D], g["b_qkv"][0:1, 2048:3072])
        p.wait_dma("vector", "ld")
        p.vector(lambda e: e.tensor_copy(bv_b[:], wtmp[0:1, 0:D]))

    with ctx.scope():
        sb_ = ctx.sb
        ps_ = ctx.psum
        xrow = sb_("xrow", [128, D])
        xTf = sb_("xTf", [128, 8 * 512])
        xTb = sb_("xTb", [128, 8 * 512], BF16)
        ct = sb_("ct", [128, 512])
        st_t = sb_("st_t", [128, 512])
        ctq = sb_("ctq", [128, 512])
        stq = sb_("stq", [128, 512])
        tmpe = sb_("tmpe", [128, 512])
        rop1 = sb_("rop1", [128, 512])
        rop2 = sb_("rop2", [128, 512])
        kout = sb_("kout", [128, 512], BF16)
        qout = sb_("qout", [128, 512], BF16)
        vout = sb_("vout", [128, 512], BF16)
        psT = ps_("psT", [128, 1024])
        psQK = [ps_("psQK0", [128, 512]), ps_("psQK1", [128, 512])]
        psV = ps_("psV", [128, 512])

        last_rope = None
        tlast_tr = None
        for blk in range(NBLK):
            own = blk < 8
            p.wait_dma("vector", "st")
            tprev_cast = None
            for t in range(4):
                if tlast_tr is not None:
                    p.wait("sync", tlast_tr)
                p.dma("ld", xrow[:], g["xb"][blk * 512 + t * 128: blk * 512 + (t + 1) * 128, :])
                p.wait_dma("tensor", "ld")
                ttr = None
                for dc in range(8):
                    slot = (dc % 2) * 512
                    ttr = p.tensor(lambda e, dc=dc, slot=slot: e.transpose(
                        psT[:, slot:slot + 128], xrow[:, dc * 128:(dc + 1) * 128], ctx.ident_f[:]))
                    p.wait("vector", ttr)
                    tev = p.vector(lambda e, dc=dc, t=t, slot=slot: e.tensor_copy(
                        xTf[:, dc * 512 + t * 128: dc * 512 + (t + 1) * 128],
                        psT[:, slot:slot + 128]))
                    p.wait("tensor", tev)
                tlast_tr = tev
                tcast = p.vector(lambda e, t=t: e.tensor_copy(
                    xTb.rearrange("p (c n) -> p c n", c=8)[:, :, t * 128:(t + 1) * 128],
                    xTf.rearrange("p (c n) -> p c n", c=8)[:, :, t * 128:(t + 1) * 128]))
                tprev_cast = tcast
            if own:
                p.wait("gpsimd", tprev_cast)
                for dc in range(8):
                    p.dma("st", g["XR0"][dc * 128:(dc + 1) * 128, blk * 512:(blk + 1) * 512],
                          xTf[:, dc * 512:(dc + 1) * 512])
            if A_STAGE == "tr":
                continue
            p.wait("scalar", last_rope)
            p.dma("ld2", ct[:], g["cos_k"][:, blk * 512:(blk + 1) * 512])
            p.dma("ld2", st_t[:], g["sin_k"][:, blk * 512:(blk + 1) * 512])
            if own:
                p.dma("ld2", ctq[:], g["cos_q"][:, blk * 512:(blk + 1) * 512])
                p.dma("ld2", stq[:], g["sin_q"][:, blk * 512:(blk + 1) * 512])
            p.wait("tensor", tprev_cast)
            p.wait_dma("vector", "ld2")

            for kc in range(8):
                bank = psQK[kc % 2]
                mm = None
                for dc in range(8):
                    mm = p.tensor(lambda e, kc=kc, dc=dc, bank=bank: e.matmul(
                        bank[:], wqkv_b[:, dc * H3 + D + kc * 128: dc * H3 + D + (kc + 1) * 128],
                        xTb[:, dc * 512:(dc + 1) * 512],
                        start=(dc == 0), stop=(dc == 7)))
                p.wait("vector", mm)
                tb_ = p.vector(lambda e, kc=kc, bank=bank: e.tensor_scalar_add(tmpe[:], bank[:], bqk[:, 8 + kc:9 + kc]))
                p.wait("tensor", tb_)
                tlast = emit_rope(ctx, tmpe, ct, st_t, kout, rop1, rop2, 512)
                last_rope = tlast
                p.wait("gpsimd", tlast)
                p.dma("st", g["KT_s"][kc * 128:(kc + 1) * 128, blk * 512:(blk + 1) * 512], kout[:])
                p.wait_dma("vector", "st")
            if own:
                for qc in range(8):
                    bank = psQK[qc % 2]
                    mm = None
                    for dc in range(8):
                        mm = p.tensor(lambda e, qc=qc, dc=dc, bank=bank: e.matmul(
                            bank[:], wqkv_b[:, dc * H3 + qc * 128: dc * H3 + (qc + 1) * 128],
                            xTb[:, dc * 512:(dc + 1) * 512],
                            start=(dc == 0), stop=(dc == 7)))
                    p.wait("vector", mm)
                    tb_ = p.vector(lambda e, qc=qc, bank=bank: e.tensor_scalar_add(
                        tmpe[:], bank[:], bqk[:, qc:qc + 1]))
                    p.wait("tensor", tb_)
                    tlast = emit_rope(ctx, tmpe, ctq, stq, qout, rop1, rop2, 512)
                    last_rope = tlast
                    p.wait("gpsimd", tlast)
                    p.dma("st", g["QT_s"][qc * 128:(qc + 1) * 128, blk * 512:(blk + 1) * 512], qout[:])
                    p.wait_dma("vector", "st")
            for t in range(4):
                for vc in range(2):
                    mm = None
                    for dc in range(8):
                        mm = p.tensor(lambda e, t=t, vc=vc, dc=dc: e.matmul(
                            psV[:], xTb[:, dc * 512 + t * 128: dc * 512 + (t + 1) * 128],
                            wqkv_b[:, dc * H3 + 2048 + vc * 512: dc * H3 + 2048 + (vc + 1) * 512],
                            start=(dc == 0), stop=False))
                    mm = p.tensor(lambda e, t=t, vc=vc: e.matmul(
                        psV[:], ctx.ones_brow[:], bv_b[0:1, vc * 512:(vc + 1) * 512],
                        start=False, stop=True))
                    p.wait("vector", mm)
                    tev = p.vector(lambda e: e.tensor_copy(vout[:], psV[:]))
                    p.wait("tensor", tev)
                    p.wait("gpsimd", tev)
                    p.dma("st", g["V_s"][blk * 512 + t * 128: blk * 512 + (t + 1) * 128,
                                         vc * 512:(vc + 1) * 512], vout[:])
                    p.wait_dma("vector", "st")

    p.barrier()
    if A_STAGE in ("tr", "qkv"):
        return

    with ctx.scope():
        sb_ = ctx.sb
        ps_ = ctx.psum
        qT = sb_("qT", [128, RPC], BF16)
        kT = sb_("kT", [128, S], BF16)
        vpA = sb_("vpA", [128, 64 * 65], BF16)
        vpB = sb_("vpB", [128, 64 * 65], BF16)
        PP = sb_("pp", [128, 4 * 1024], BF16)
        osA = sb_("osA", [65, 512])
        osB = sb_("osB", [65, 512])
        zrec = sb_("zrec", [1, 1024])
        psS2 = [ps_("psSa", [128, 1024]), ps_("psSb", [128, 1024])]
        poA = ps_("poA", [65, 512])
        poB = ps_("poB", [65, 512])
        psR = ps_("psR", [128, 1024])
        mb = sb_("mb", [128, 1])
        p.vector(lambda e: e.memset(mb[:], -EXP_BIAS))

        for pair in range(NPAIR):
            p.dma("ld", qT[:], g["QT_s"][pair * 128:(pair + 1) * 128, :])
            p.dma("ld2", kT[:], g["KT_s"][pair * 128:(pair + 1) * 128, :])
            p.vector(lambda e: e.memset(vpA.rearrange("p (n c) -> p n c", c=65)[:, :, 64], 1.0))
            tms = p.vector(lambda e: e.memset(vpB.rearrange("p (n c) -> p n c", c=65)[:, :, 64], 1.0))
            p.dma("ld3", vpA.rearrange("p (n c) -> p n c", c=65)[:, :, 0:64],
                  g["V_s"].rearrange("(n p) c -> p n c", p=128)[:, :, pair * 128: pair * 128 + 64])
            p.dma("ld", vpB.rearrange("p (n c) -> p n c", c=65)[:, :, 0:64],
                  g["V_s"].rearrange("(n p) c -> p n c", p=128)[:, :, pair * 128 + 64: pair * 128 + 128])
            p.wait_dma("tensor", "ld")
            p.wait_dma("tensor", "ld2")
            p.wait_dma("tensor", "ld3")
            p.wait("tensor", tms)

            exp_hist = {}
            pv_hist = {}

            def emit_pv(kt, gk):
                p.wait("tensor", exp_hist[gk])
                base = (gk % 4) * 1024
                pv = p.tensor(lambda e, kt=kt, base=base: e.matmul(
                    poA[:], vpA[:, kt * 65:(kt + 1) * 65],
                    PP[:, base:base + 512],
                    start=(kt == 0), stop=(kt == 63)))
                pv = p.tensor(lambda e, kt=kt, base=base: e.matmul(
                    poB[:], vpB[:, kt * 65:(kt + 1) * 65],
                    PP[:, base + 512:base + 1024],
                    start=(kt == 0), stop=(kt == 63)))
                pv_hist[gk] = pv

            prev_tmo = None
            for qc in range(8):
                for kt in range(64):
                    gk = qc * 64 + kt
                    if gk >= 2:
                        p.wait("tensor", exp_hist[gk - 2])
                    bank = psS2[gk % 2]
                    smm = p.tensor(lambda e, kt=kt, qc=qc, bank=bank: e.matmul(
                        bank[:, 0:512],
                        kT[0:64, kt * 128:(kt + 1) * 128], qT[0:64, qc * 512:(qc + 1) * 512],
                        start=True, stop=True, tile_position=(0, 0)))
                    smm = p.tensor(lambda e, kt=kt, qc=qc, bank=bank: e.matmul(
                        bank[:, 512:1024],
                        kT[64:128, kt * 128:(kt + 1) * 128], qT[64:128, qc * 512:(qc + 1) * 512],
                        start=True, stop=True, tile_position=(64, 0)))
                    p.wait("scalar", smm)
                    if gk >= 4:
                        p.wait("scalar", pv_hist[gk - 4])
                    texp = p.scalar(lambda e, gk=gk, bank=bank: e.activation(
                        PP[:, (gk % 4) * 1024:(gk % 4 + 1) * 1024], bank[:],
                        AF.Exp, bias=mb[:], scale=1.0))
                    exp_hist[gk] = texp
                    if kt > 0:
                        emit_pv(kt - 1, gk - 1)
                emit_pv(63, qc * 64 + 63)
                p.wait("vector", pv_hist[qc * 64 + 63])
                p.vector(lambda e: e.tensor_copy(osA[:], poA[:]))
                tcb = p.vector(lambda e: e.tensor_copy(osB[:], poB[:]))
                p.vector(lambda e: e.reciprocal(zrec[0:1, 0:512], osA[64:65, :]))
                trc = p.vector(lambda e: e.reciprocal(zrec[0:1, 512:1024], osB[64:65, :]))
                p.wait("tensor", tcb)
                p.wait("tensor", trc)
                p.wait("tensor", prev_tmo)
                z1 = p.tensor(lambda e: e.matmul(psR[0:64, 0:512], ctx.ones_row[0:1, 0:64],
                                                 zrec[0:1, 0:512], start=True, stop=True))
                z2 = p.tensor(lambda e: e.matmul(psR[64:128, 512:1024], ctx.ones_row[0:1, 0:64],
                                                 zrec[0:1, 512:1024], start=True, stop=True))
                p.wait("vector", z1)
                p.wait("vector", z2)
                p.vector(lambda e, pair=pair, qc=qc: e.tensor_mul(
                    o_all[0:64, pair * RPC + qc * 512: pair * RPC + (qc + 1) * 512],
                    osA[0:64, :], psR[0:64, 0:512]))
                tmo = p.vector(lambda e, pair=pair, qc=qc: e.tensor_mul(
                    o_all[64:128, pair * RPC + qc * 512: pair * RPC + (qc + 1) * 512],
                    osB[0:64, :], psR[64:128, 512:1024]))
                prev_tmo = tmo
            p.barrier()

    if A_STAGE == "flash":
        return
    with ctx.scope():
        sb_ = ctx.sb
        ps_ = ctx.psum
        ycol = sb_("ycol", [128, 8 * 512])
        xrt_all = sb_("xrt_all", [128, 8 * 512])
        psOb = [ps_("psOa3", [128, 512]), ps_("psOb3", [128, 512])]
        ps_stat = (ps_("ps_stat_a_s", [1, 512]), ps_("ps_stat_a_q", [1, 512]))
        ps_rep = ps_("ps_rep_a", [128, 1024])
        lnt = make_ln_tiles(ctx, 512, "a")
        t_evac = {0: None, 1: None}
        last_res = None
        for qc in range(8):
            p.wait("sync", last_res)
            for dm in range(8):
                p.dma("ld", xrt_all[:, dm * 512:(dm + 1) * 512],
                      g["XR0"][dm * 128:(dm + 1) * 128, qc * 512:(qc + 1) * 512])
            for dm in range(8):
                bank = psOb[dm % 2]
                p.wait("tensor", t_evac[dm % 2])
                mm = None
                for pr in range(8):
                    mm = p.tensor(lambda e, qc=qc, dm=dm, pr=pr, bank=bank: e.matmul(
                        bank[:], wout_b[:, pr * D + dm * 128: pr * D + (dm + 1) * 128],
                        o_all[:, pr * RPC + qc * 512: pr * RPC + (qc + 1) * 512],
                        start=(pr == 0), stop=(pr == 7)))
                p.wait("vector", mm)
                if dm == 0:
                    p.wait_dma("vector", "ld")
                tadd = p.vector(lambda e, dm=dm, bank=bank: e.tensor_scalar_add(
                    ycol[:, dm * 512:(dm + 1) * 512], bank[:], bo[:, dm:dm + 1]))
                tres = p.vector(lambda e, dm=dm: e.tensor_add(
                    ycol[:, dm * 512:(dm + 1) * 512], ycol[:, dm * 512:(dm + 1) * 512],
                    xrt_all[:, dm * 512:(dm + 1) * 512]))
                t_evac[dm % 2] = tadd
                last_res = tres
            emit_ln(ctx, lnt, ycol, 512,
                    lambda c, ap, qc=qc: p.dma("st", g["xf_A"][c * 128:(c + 1) * 128,
                                                              qc * 512:(qc + 1) * 512], ap),
                    lambda c, ap, qc=qc: p.dma("st", g["xb_A"][c * 128:(c + 1) * 128,
                                                              qc * 512:(qc + 1) * 512], ap),
                    ps_stat, ps_rep)



def emit_phase_b(ctx_outer, src):
    with ctx_outer.scope() as ctx:
        _emit_phase_b(ctx, src)


def _emit_phase_b(ctx, src):
    p, g, nc = ctx.p, ctx.g, ctx.nc
    xf_src, xb_src = g[src[0]], g[src[1]]

    w1t = ctx.sb("w1t", [128, 8 * FF], BF16)
    w2t = ctx.sb("w2t", [128, 32 * D], BF16)
    b1c = bias_chunks(ctx, g["b1"][0:1, :], 32, "b1")
    b2c = bias_chunks(ctx, g["b2"][0:1, :], 8, "b2")
    er1 = ctx.sb("er1", [1, 4 * 1024], BF16)
    oddc1 = ctx.sb("oddc1", [128, 8 * 4], BF16)
    oddc2 = ctx.sb("oddc2", [128, 8 * 4], BF16)
    lsc = ctx.sb("lsc", [1, 1], F32)
    p.dma("ld", lsc[:], g["lscale"][:])

    evs4 = ctx.sb("evs4", [4, 8 * 128], BF16)

    with ctx.scope():
        wrow = ctx.sb("wrow_b", [128, FF])
        wrow_bf = ctx.sb("wrowbf_b", [128, FF], BF16)
        psTb = ctx.psum("psT_b", [128, 1024], BF16)
        tprev = None
        for hc in range(32):
            p.wait("sync", tprev)
            p.dma("ld", wrow[:, 0:D], g["W1"][hc * 128:(hc + 1) * 128, :])
            p.wait_dma("vector", "ld")
            tc = p.vector(lambda e: e.tensor_copy(wrow_bf[:, 0:D], wrow[:, 0:D]))
            tprev = tc
            p.wait("tensor", tc)
            for dc in range(8):
                slot = (dc % 2) * 512
                tt = p.tensor(lambda e, dc=dc, slot=slot: e.transpose(
                    psTb[:, slot:slot + 128], wrow_bf[:, dc * 128:(dc + 1) * 128], ctx.ident_b[:]))
                p.wait("vector", tt)
                tev = p.vector(lambda e, dc=dc, hc=hc, slot=slot: e.tensor_copy(
                    w1t[:, dc * FF + hc * 128: dc * FF + (hc + 1) * 128], psTb[:, slot:slot + 128]))
                p.wait("tensor", tev)
                tprev = tev
        for dc in range(8):
            p.wait("sync", tprev)
            p.dma("ld", wrow[:], g["W2"][dc * 128:(dc + 1) * 128, :])
            p.wait_dma("vector", "ld")
            tc = p.vector(lambda e: e.tensor_copy(wrow_bf[:], wrow[:]))
            tprev = tc
            p.wait("tensor", tc)
            for hc in range(32):
                slot = (hc % 2) * 512
                tt = p.tensor(lambda e, hc=hc, slot=slot: e.transpose(
                    psTb[:, slot:slot + 128], wrow_bf[:, hc * 128:(hc + 1) * 128], ctx.ident_b[:]))
                p.wait("vector", tt)
                tev = p.vector(lambda e, dc=dc, hc=hc, slot=slot: e.tensor_copy(
                    w2t[:, hc * D + dc * 128: hc * D + (dc + 1) * 128], psTb[:, slot:slot + 128]))
                p.wait("tensor", tev)
                tprev = tev

    with ctx.scope():
        ltcols = ctx.sb("ltcols", [128, 16])
        lt_tok = ctx.sb("lt_tok", [16, 1024])
        psLT = ctx.psum("psLT", [128, 128])
        tprev = None
        for dc in range(8):
            p.wait("sync", tprev)
            p.dma("ld", ltcols[:], xf_src[dc * 128:(dc + 1) * 128, 1:17])
            p.wait_dma("tensor", "ld")
            tt = p.tensor(lambda e, dc=dc: e.transpose(psLT[0:16, :], ltcols[:], ctx.ident_f[:]))
            p.wait("vector", tt)
            tev = p.vector(lambda e, dc=dc: e.tensor_copy(lt_tok[:, dc * 128:(dc + 1) * 128],
                                                          psLT[0:16, :]))
            tprev = tev
            p.wait("tensor", tev)
        p.wait("gpsimd", tprev)
        p.dma("cc", g["ag_lt_in"][:], lt_tok[:])
        p.wait_dma("gpsimd", "cc")
        p.cc(lambda e: e.collective_compute(
            "AllGather", ALU.bypass, replica_groups=[list(range(NCORES))],
            ins=[g["ag_lt_in"].opt()], outs=[g["ag_lt_out"].opt()]))
        p.wait_cc("gpsimd")
        agt = ctx.sb("agt", [16, 8 * 1024])
        bselt = ctx.sb("bselt", [16, 8])
        ltmine = ctx.sb("ltmine", [16, 1024])
        p.dma("ld", bselt[:], g["bsel"][:])
        for j in range(NCORES):
            p.dma("cc", agt[:, j * 1024:(j + 1) * 1024], g["ag_lt_out"][j * 16:(j + 1) * 16, :])
        p.wait_dma("vector", "cc")
        p.wait_dma("vector", "ld")
        p.vector(lambda e: e.tensor_scalar_mul(ltmine[:], agt[:, 0:1024], bselt[:, 0:1]))
        tcomb = None
        for j in range(1, NCORES):
            tcomb = p.vector(lambda e, j=j: e.scalar_tensor_tensor(
                ltmine[:], agt[:, j * 1024:(j + 1) * 1024], bselt[:, j:j + 1], ltmine[:],
                ALU.mult, ALU.add))
        p.wait("gpsimd", tcomb)
        p.dma("st", g["lt_s"][:], ltmine[:])
        ltfeat = ctx.sb("ltfeat", [128, 8 * 16])
        p.wait("tensor", tcomb)
        tprev = None
        for dc in range(8):
            tt = p.tensor(lambda e, dc=dc: e.transpose(
                psLT[:, 0:16], ltmine[:, dc * 128:(dc + 1) * 128], ctx.ident_f[0:16, 0:16]))
            p.wait("vector", tt)
            tev = p.vector(lambda e, dc=dc: e.tensor_copy(ltfeat[:, dc * 16:(dc + 1) * 16],
                                                          psLT[:, 0:16]))
            p.wait("tensor", tev)
            tprev = tev
        if _os.environ.get("KERNEL_DEBUG_B") == "1":
            dbg_lf = nc.dram_tensor("dbg_lf", [128, 128], F32, kind="ExternalOutput").ap()
            p.barrier()
            p.dma("st", dbg_lf[:], ltfeat[:])
            p.barrier()
        for dc in range(8):
            p.vector(lambda e, dc=dc: e.tensor_copy(
                oddc1[:, dc * 4:(dc + 1) * 4],
                ltfeat.rearrange("p (c s) -> p c s", s=16)[:, dc, 1:9:2]))
            p.vector(lambda e, dc=dc: e.tensor_copy(
                oddc2[:, dc * 4:(dc + 1) * 4],
                ltfeat.rearrange("p (c s) -> p c s", s=16)[:, dc, 9:16:2]))
        erow = ctx.sb("erow", [1, 1024])
        er2f = ctx.sb("er2f", [1, 1024])
        evrow = ctx.sb("evrow", [1, 256])
        p.wait_dma("sync", "st")
        p.wait_dma("vector", "ld")
        tprevdma = None
        for j in range(4):
            p.wait("sync", tprevdma)
            p.dma("ld", erow[:], g["lt_s"][2 * j:2 * j + 1, :])
            p.wait_dma("vector", "ld")
            t1 = p.vector(lambda e, j=j: e.tensor_scalar_mul(
                er1[0:1, j * 1024:(j + 1) * 1024], erow[:], lsc[0:1, 0:1]))
            tprevdma = t1
        for j in range(4):
            p.wait("sync", tprevdma)
            p.dma("ld", erow[:], g["lt_s"][8 + 2 * j:9 + 2 * j, :])
            p.wait_dma("vector", "ld")
            t2 = p.vector(lambda e: e.tensor_scalar_mul(er2f[:], erow[:], lsc[0:1, 0:1]))
            p.wait("gpsimd", t2)
            for r in range(4):
                tv = p.vector(lambda e, r=r: e.tensor_copy(evrow[:], er2f[0:1, r:1024:4]))
                p.wait("gpsimd", tv)
                p.dma("st", g["evs_s"][4 * j + r:4 * j + r + 1, :], evrow[:])
                p.wait_dma("vector", "st")
            tprevdma = t2
        evs4f = ctx.sb("evs4f", [4, 8 * 128])
        p.wait_dma("sync", "st")
        for j in range(4):
            for half in range(2):
                p.dma("ld", evs4f[:, (j * 2 + half) * 128:(j * 2 + half + 1) * 128],
                      g["evs_s"].rearrange("a (h b) -> a h b", h=2)[4 * j:4 * j + 4, half, :])
        p.wait_dma("vector", "ld")
        tcv = p.vector(lambda e: e.tensor_copy(evs4[:], evs4f[:]))
        p.wait("tensor", tcv)

    if _os.environ.get("KERNEL_DEBUG_B") == "1":
        dbg_oc = nc.dram_tensor("dbg_oc", [128, 32], BF16, kind="ExternalOutput").ap()
        dbg_oc2 = nc.dram_tensor("dbg_oc2", [128, 32], BF16, kind="ExternalOutput").ap()
        p.barrier()
        p.dma("st", dbg_oc[:], oddc1[:])
        p.dma("st", dbg_oc2[:], oddc2[:])
        dbg_e4 = nc.dram_tensor("dbg_e4", [4, 1024], BF16, kind="ExternalOutput").ap()
        p.dma("st", dbg_e4[:], evs4[:])
        p.barrier()
    TBW = 256
    NTB = RPC // TBW
    with ctx.scope():
        xcolb2 = [ctx.sb("xcolb_b0", [128, 8 * TBW], BF16),
                  ctx.sb("xcolb_b1", [128, 8 * TBW], BF16)]
        xrt = ctx.sb("xrt_b", [128, TBW])
        htile = ctx.sb("htile_b", [128, 32 * TBW], BF16)
        ab = ctx.sb("ab_b", [1, 4 * TBW], BF16)
        c4 = ctx.sb("c4_b", [4, 4 * TBW], BF16)
        crow = ctx.sb("crow_b", [4, TBW], BF16)
        ycol = ctx.sb("ycol_b", [128, 8 * TBW])
        lnt = make_ln_tiles(ctx, TBW, "b")
        psAa = ctx.psum("psAa", [1, TBW])
        bankA = ctx.psum("bankA_b", [128, TBW])
        bankB = ctx.psum("bankB_b", [128, TBW])
        psC = ctx.psum("psC_b", [4, TBW])
        ps_stat = (ctx.psum("ps_stat_b_s", [1, TBW]), ctx.psum("ps_stat_b_q", [1, TBW]))
        ps_rep = ctx.psum("ps_rep_b", [128, 2 * TBW])

        tlast_res = None
        pending_ln = None
        prev_tbh = {}
        buf_use = {0: None, 1: None}
        for tb in range(NTB):
            cols = (tb * TBW, (tb + 1) * TBW)
            xcolb = xcolb2[tb % 2]
            if tb == 0:
                for dc in range(8):
                    p.dma("ld", xcolb[:, dc * TBW:(dc + 1) * TBW],
                          xb_src[dc * 128:(dc + 1) * 128, cols[0]:cols[1]])
            p.wait_dma("tensor", "ld")
            for j in range(4):
                mm = None
                for dc in range(8):
                    mm = p.tensor(lambda e, j=j, dc=dc, xcolb=xcolb: e.matmul(
                        psAa[:], oddc1[:, dc * 4 + j: dc * 4 + j + 1],
                        xcolb[:, dc * TBW:(dc + 1) * TBW],
                        start=(dc == 0), stop=(dc == 7)))
                p.wait("vector", mm)
                tev = p.vector(lambda e, j=j: e.tensor_copy(ab[0:1, j * TBW:(j + 1) * TBW], psAa[:]))
                p.wait("tensor", tev)
            if tb + 1 < NTB:
                p.wait("sync", buf_use[(tb + 1) % 2])
                for dc in range(8):
                    p.dma("ld", xcolb2[(tb + 1) % 2][:, dc * TBW:(dc + 1) * TBW],
                          xb_src[dc * 128:(dc + 1) * 128, (tb + 1) * TBW:(tb + 2) * TBW])
            tg_hist = {}
            tev_c = None
            for hc in range(32):
                bank = bankA if hc % 2 == 0 else bankB
                j = hc // 8
                if hc >= 2:
                    p.wait("tensor", tg_hist[hc - 2])
                else:
                    p.wait("tensor", prev_tbh.get(6 + hc))
                for dc in range(8):
                    p.tensor(lambda e, hc=hc, dc=dc, bank=bank, xcolb=xcolb: e.matmul(
                        bank[:], w1t[:, dc * FF + hc * 128: dc * FF + (hc + 1) * 128],
                        xcolb[:, dc * TBW:(dc + 1) * TBW],
                        start=(dc == 0), stop=False))
                mm = p.tensor(lambda e, hc=hc, j=j, bank=bank: e.matmul(
                    bank[:], er1[0:1, j * 1024 + (hc % 8) * 128: j * 1024 + ((hc % 8) + 1) * 128],
                    ab[0:1, j * TBW:(j + 1) * TBW],
                    start=False, stop=True))
                p.wait("scalar", mm)
                tg = p.scalar(lambda e, hc=hc, bank=bank: e.activation(
                    htile[:, hc * TBW:(hc + 1) * TBW], bank[:], AF.Gelu,
                    bias=b1c[:, hc:hc + 1], scale=1.0))
                tg_hist[hc] = tg
                if hc % 8 == 7:
                    r = hc // 8
                    p.wait("tensor", tg)
                    p.wait("tensor", tev_c)
                    mmc = None
                    for qc in range(8):
                        mmc = p.tensor(lambda e, r=r, qc=qc: e.matmul(
                            psC[:], oddc2[:, qc * 4:(qc + 1) * 4],
                            htile[:, (r * 8 + qc) * TBW:(r * 8 + qc + 1) * TBW],
                            start=(qc == 0), stop=(qc == 7)))
                    p.wait("vector", mmc)
                    p.wait_dma("vector", "st")
                    tev_c = p.vector(lambda e, r=r: e.tensor_copy(crow[:], psC[:]))
                    p.wait("gpsimd", tev_c)
                    p.dma("st", g["cscr"][r, :, :], crow[:])
            buf_use[tb % 2] = mm
            p.wait_dma("scalar", "st")
            for j in range(4):
                p.dma("ld2", c4[:, j * TBW:(j + 1) * TBW], g["cscr"][:, j, :])
            if pending_ln is not None:
                pending_ln()
                pending_ln = None
            tbh = {}
            for oc in range(8):
                bank = bankA if oc % 2 == 0 else bankB
                j, half = oc // 2, oc % 2
                if oc >= 2:
                    p.wait("tensor", tbh[oc - 2])
                for hc in range(32):
                    p.tensor(lambda e, oc=oc, hc=hc, bank=bank: e.matmul(
                        bank[:], w2t[:, hc * D + oc * 128: hc * D + (oc + 1) * 128],
                        htile[:, hc * TBW:(hc + 1) * TBW],
                        start=(hc == 0), stop=False))
                if oc == 0:
                    p.wait_dma("tensor", "ld2")
                mm = p.tensor(lambda e, oc=oc, j=j, half=half, bank=bank: e.matmul(
                    bank[:], evs4[:, (j * 2 + half) * 128:(j * 2 + half + 1) * 128],
                    c4[:, j * TBW:(j + 1) * TBW],
                    start=False, stop=True))
                if oc > 0 or tb > 0:
                    p.wait("scalar", tlast_res)
                p.dma("ld2", xrt[:], xf_src[oc * 128:(oc + 1) * 128, cols[0]:cols[1]])
                p.wait("vector", mm)
                tb_ = p.vector(lambda e, oc=oc, bank=bank: e.tensor_scalar_add(
                    ycol[:, oc * TBW:(oc + 1) * TBW], bank[:], b2c[:, oc:oc + 1]))
                p.wait_dma("vector", "ld2")
                tres = p.vector(lambda e, oc=oc: e.tensor_add(
                    ycol[:, oc * TBW:(oc + 1) * TBW], ycol[:, oc * TBW:(oc + 1) * TBW],
                    xrt[:]))
                tbh[oc] = tb_
                tlast_use = mm
                tlast_res = tres
            prev_tbh = tbh
            if tb == 0 and _os.environ.get("KERNEL_DEBUG_B") == "1":
                dbg_h = nc.dram_tensor("dbg_h", [128, 32 * TBW], BF16, kind="ExternalOutput").ap()
                dbg_ab = nc.dram_tensor("dbg_ab", [1, 4 * TBW], BF16, kind="ExternalOutput").ap()
                dbg_y = nc.dram_tensor("dbg_y", [128, 8 * TBW], F32, kind="ExternalOutput").ap()
                p.barrier()
                p.dma("st", dbg_h[:], htile[:])
                p.dma("st", dbg_ab[:], ab[:])
                p.dma("st", dbg_y[:], ycol[:])
                p.barrier()
            def _mk_ln(tb=tb):
                def _f():
                    emit_ln(ctx, lnt, ycol, TBW,
                            lambda c, ap, tb=tb: p.dma("st", g["xf_B"][c * 128:(c + 1) * 128,
                                                                       tb * TBW:(tb + 1) * TBW], ap),
                            lambda c, ap, tb=tb: p.dma("st", g["xb_B"][c * 128:(c + 1) * 128,
                                                                       tb * TBW:(tb + 1) * TBW], ap),
                            ps_stat, ps_rep)
                return _f
            pending_ln = _mk_ln()
        pending_ln()

def emit_phase_c(ctx_outer, src):
    with ctx_outer.scope() as ctx:
        _emit_phase_c(ctx, src)


def _emit_phase_c(ctx, src):
    p, g, nc = ctx.p, ctx.g, ctx.nc
    xf_src, xb_src = g[src[0]], g[src[1]]
    TOT = float(2048 * 1024)

    w0b = ctx.sb("w0b", [128, 8 * FF], BF16)
    w1b = ctx.sb("w1b", [128, 32 * D], BF16)
    Dsc = ctx.sb("Dsc", [1, 1])
    Dvec = ctx.sb("Dvec", [128, 1])
    D2vec = ctx.sb("D2vec", [128, 1])
    with ctx.scope():
        wtmp = ctx.sb("wtmp_c", [128, FF])
        wbtmp = ctx.sb("wbtmp_c", [128, FF], BF16)
        tprev = None
        for dc in range(8):
            p.wait("sync", tprev)
            p.dma("ld", wtmp[:], g["w0"][dc * 128:(dc + 1) * 128, :])
            p.wait_dma("vector", "ld")
            tprev = p.vector(lambda e, dc=dc: e.tensor_copy(w0b[:, dc * FF:(dc + 1) * FF], wtmp[:]))
        for fc in range(32):
            p.wait("sync", tprev)
            p.dma("ld", wtmp[:, 0:D], g["w1f"][fc * 128:(fc + 1) * 128, :])
            p.wait_dma("vector", "ld")
            tprev = p.vector(lambda e, fc=fc: e.tensor_copy(w1b[:, fc * D:(fc + 1) * D], wtmp[:, 0:D]))
        for dc in range(8):
            p.wait("sync", tprev)
            p.dma("ld", wtmp[:], g["w2"][dc * 128:(dc + 1) * 128, :])
            p.wait_dma("vector", "ld")
            tc = p.vector(lambda e: e.tensor_copy(wbtmp[:], wtmp[:]))
            tprev = tc
            p.wait("gpsimd", tc)
            p.dma("st", g["w2b_s"][dc * 128:(dc + 1) * 128, :], wbtmp[:])
            p.wait_dma("vector", "st")
        p.vector(lambda e: e.memset(Dsc[:], 1.0))
        p.vector(lambda e: e.memset(Dvec[:], 1.0))
        p.vector(lambda e: e.memset(D2vec[:], 1.0))
        zrow = ctx.sb("zrow_c", [1, 8])
        tzz = p.vector(lambda e: e.memset(zrow[:], 0.0))
        p.wait("gpsimd", tzz)
        p.dma("cc", g["cc_in"][:], zrow[:])
        p.wait_dma("gpsimd", "cc")

    with ctx.scope():
        TBW = 256
        xcb = ctx.sb("xcb_c", [128, 8 * TBW], BF16)
        xcf = ctx.sb("xcf_c", [128, 8 * TBW])
        w2fc = [ctx.sb("w2fc0", [128, 8 * 128], BF16), ctx.sb("w2fc1", [128, 8 * 128], BF16)]
        htile = ctx.sb("htile_c", [128, 32 * TBW], BF16)
        stile = ctx.sb("stile_c", [128, TBW])
        vtile = ctx.sb("vtile_c", [128, TBW])
        ocol = ctx.sb("ocol_c", [128, 8 * TBW])
        dtile = ctx.sb("dtile_c", [128, TBW])
        lrow = ctx.sb("lrow_c", [1, TBW])
        ls = ctx.sb("ls_c", [1, 1])
        lg = ctx.sb("lg_c", [1, 1])
        dec = ctx.sb("dec_c", [1, 1])
        lnt = make_ln_tiles(ctx, TBW, "c")
        psU = ctx.psum("psU_c", [128, TBW])
        psV2 = ctx.psum("psV_c", [128, TBW])
        psO2 = [ctx.psum("psO2a_c", [128, TBW]), ctx.psum("psO2b_c", [128, TBW])]
        psM = ctx.psum("psM_c", [128, TBW])
        ps_stat = (ctx.psum("ps_st_c_s", [1, TBW]), ctx.psum("ps_st_c_q", [1, TBW]))
        ps_rep = ctx.psum("ps_rep_c", [128, 2 * TBW])

        t_u_evac = None
        t_v_evac = None
        t_x_use = None
        t_res = None
        last_th = None
        t_w2_use = {0: None, 1: None}
        for c in range(NCHUNK):
            cols = (c * TBW, (c + 1) * TBW)
            p.wait("sync", t_x_use)
            p.wait("sync", t_res)
            for dc in range(8):
                p.dma("ld", xcb[:, dc * TBW:(dc + 1) * TBW],
                      xb_src[dc * 128:(dc + 1) * 128, cols[0]:cols[1]])
                p.dma("ld", xcf[:, dc * TBW:(dc + 1) * TBW],
                      xf_src[dc * 128:(dc + 1) * 128, cols[0]:cols[1]])
            p.wait_dma("tensor", "ld")
            p.wait_dma("vector", "ld")
            for fc in range(32):
                buf = fc % 2
                p.wait("scalar", t_w2_use[buf])
                p.wait("gpsimd", t_w2_use[buf])
                p.dma("ld2", w2fc[buf][:, 0:512],
                      g["w2b_s"].rearrange("(cc pp) f -> pp cc f", pp=128)[:, 0:4, fc * 128:(fc + 1) * 128])
                p.dma("ld3", w2fc[buf][:, 512:1024],
                      g["w2b_s"].rearrange("(cc pp) f -> pp cc f", pp=128)[:, 4:8, fc * 128:(fc + 1) * 128])
                mmu = None
                p.wait("tensor", t_u_evac)
                for dc in range(8):
                    mmu = p.tensor(lambda e, fc=fc, dc=dc: e.matmul(
                        psU[:], w0b[:, dc * FF + fc * 128: dc * FF + (fc + 1) * 128],
                        xcb[:, dc * TBW:(dc + 1) * TBW],
                        start=(dc == 0), stop=(dc == 7)))
                p.wait_dma("tensor", "ld2")
                p.wait_dma("tensor", "ld3")
                mmv = None
                p.wait("tensor", t_v_evac)
                for dc in range(8):
                    mmv = p.tensor(lambda e, fc=fc, dc=dc, buf=buf: e.matmul(
                        psV2[:], w2fc[buf][:, dc * 128:(dc + 1) * 128],
                        xcb[:, dc * TBW:(dc + 1) * TBW],
                        start=(dc == 0), stop=(dc == 7)))
                t_w2_use[buf] = mmv
                t_x_use = mmv
                p.wait("scalar", mmu)
                p.wait("scalar", last_th)
                tsl = p.scalar(lambda e: e.activation(stile[:], psU[:], AF.Silu,
                                                      bias=ctx.zero128[:], scale=Dvec[:]))
                t_u_evac = tsl
                p.wait("vector", mmv)
                tvc = p.vector(lambda e: e.tensor_copy(vtile[:], psV2[:]))
                t_v_evac = tvc
                p.wait("vector", tsl)
                th = p.vector(lambda e, fc=fc: e.tensor_mul(
                    htile[:, fc * TBW:(fc + 1) * TBW], stile[:], vtile[:]))
                last_th = th
            p.wait("tensor", last_th)
            t_o_evac = {0: None, 1: None}
            for dm in range(8):
                bank = psO2[dm % 2]
                mm = None
                p.wait("tensor", t_o_evac[dm % 2])
                for fc in range(32):
                    mm = p.tensor(lambda e, dm=dm, fc=fc, bank=bank: e.matmul(
                        bank[:], w1b[:, fc * D + dm * 128: fc * D + (dm + 1) * 128],
                        htile[:, fc * TBW:(fc + 1) * TBW],
                        start=(fc == 0), stop=(fc == 31)))
                p.wait("vector", mm)
                tev = p.vector(lambda e, dm=dm, bank=bank: e.tensor_scalar(
                    ocol[:, dm * TBW:(dm + 1) * TBW], bank[:], D2vec[:], None, ALU.mult))
                t_o_evac[dm % 2] = tev
            mml = None
            for dm in range(8):
                tsub = p.vector(lambda e, dm=dm: e.tensor_sub(
                    dtile[:], ocol[:, dm * TBW:(dm + 1) * TBW], xcf[:, dm * TBW:(dm + 1) * TBW]))
                tsq = p.vector(lambda e: e.tensor_mul(dtile[:], dtile[:], dtile[:]))
                p.wait("tensor", tsq)
                mml = p.tensor(lambda e, dm=dm: e.matmul(
                    psM[0:1, 0:TBW], ctx.ones_col[:], dtile[:],
                    start=(dm == 0), stop=(dm == 7)))
                if dm < 7:
                    p.wait("vector", mml)
            p.wait("vector", mml)
            tlr = p.vector(lambda e: e.tensor_copy(lrow[:], psM[0:1, 0:TBW]))
            p.wait("tensor", tlr)
            tls = p.vector(lambda e: e.reduce_sum(ls[:], lrow[:], AX.X))
            p.wait("gpsimd", tls)
            p.dma("cc", g["cc_in"][0:1, 0:1], ls[:])
            p.wait_dma("gpsimd", "cc")
            p.cc(lambda e: e.collective_compute(
                "AllReduce", ALU.add, replica_groups=[list(range(NCORES))],
                ins=[g["cc_in"].opt()], outs=[g["cc_out"].opt()]))
            p.wait_cc("gpsimd")
            p.dma("cc", lg[:], g["cc_out"][0:1, 0:1])
            for dm in range(8):
                t_res = p.vector(lambda e, dm=dm: e.tensor_add(
                    ocol[:, dm * TBW:(dm + 1) * TBW], ocol[:, dm * TBW:(dm + 1) * TBW],
                    xcf[:, dm * TBW:(dm + 1) * TBW]))
            emit_ln(ctx, lnt, ocol, TBW,
                    lambda ch, ap, c=c: p.dma("st2", g["xf_C"][ch * 128:(ch + 1) * 128,
                                                               c * TBW:(c + 1) * TBW], ap),
                    lambda ch, ap, c=c: p.dma("st2", g["xb_C"][ch * 128:(ch + 1) * 128,
                                                               c * TBW:(c + 1) * TBW], ap),
                    ps_stat, ps_rep, st_stream="st2")
            p.wait_dma("vector", "cc")
            if _os.environ.get("KERNEL_DEBUG_C") == "1":
                if "dbg_loss" not in g:
                    g["dbg_loss"] = nc.dram_tensor("dbg_loss", [1, NCHUNK], F32,
                                                   kind="ExternalOutput").ap()
                    g["dbg_o"] = nc.dram_tensor("dbg_o", [128, 8 * 256], F32,
                                                kind="ExternalOutput").ap()
                p.wait_dma("gpsimd", "cc")
                p.dma("st", g["dbg_loss"][0:1, c:c + 1], lg[:])
            tdec = p.vector(lambda e: e.tensor_scalar(dec[:], lg[:], -LR / TOT, 1.0,
                                                      ALU.mult, ALU.add))
            p.wait("vector", t_u_evac)
            tds = p.vector(lambda e: e.tensor_mul(Dsc[:], Dsc[:], dec[:]))
            p.wait("tensor", tds)
            mmd = p.tensor(lambda e: e.matmul(psM[:, 0:1], ctx.ones_row[:], Dsc[:],
                                              start=True, stop=True))
            p.wait("vector", mmd)
            tdv = p.vector(lambda e: e.tensor_copy(Dvec[:], psM[:, 0:1]))
            p.wait("tensor", tdv)
            td2 = p.vector(lambda e: e.tensor_mul(D2vec[:], Dvec[:], Dvec[:]))
            p.wait("scalar", td2)
            if _os.environ.get("KERNEL_DEBUG_C") == "1" and c == NCHUNK - 1:
                p.barrier()
                p.dma("st", g["dbg_o"][:], ocol[:])
                p.barrier()



def emit_phase_d(ctx_outer, src):
    with ctx_outer.scope() as ctx:
        _emit_phase_d(ctx, src)


def _emit_phase_d(ctx, src):
    p, g, nc = ctx.p, ctx.g, ctx.nc
    xf_src = g[src[0]]

    wrt = ctx.sb("wrt", [128, 8 * D], BF16)
    wkt = ctx.sb("wkt", [128, 8 * D], BF16)
    wvt = ctx.sb("wvt", [128, 8 * D], BF16)
    murc = bias_chunks(ctx, g["mu_r"][0:1, :], 8, "mur")
    mukc = bias_chunks(ctx, g["mu_k"][0:1, :], 8, "muk")
    mur1c = ctx.sb("mur1c", [128, 8])
    muk1c = ctx.sb("muk1c", [128, 8])
    bd_use = ctx.sb("bd_use", [128, 128])
    p.vector(lambda e: e.tensor_scalar(mur1c[:], murc[:], -1.0, 1.0, ALU.mult, ALU.add))
    p.vector(lambda e: e.tensor_scalar(muk1c[:], mukc[:], -1.0, 1.0, ALU.mult, ALU.add))

    with ctx.scope():
        wrow = ctx.sb("wrow_d", [128, D])
        wrow_bf = ctx.sb("wrowbf_d", [128, D], BF16)
        psTd0 = ctx.psum("psTd0", [128, 1024], BF16)
        tprev = None
        for wname, wt in (("W_r", wrt), ("W_k", wkt), ("W_v", wvt)):
            for oc in range(8):
                p.wait("sync", tprev)
                p.dma("ld", wrow[:], g[wname][oc * 128:(oc + 1) * 128, :])
                p.wait_dma("vector", "ld")
                tc = p.vector(lambda e: e.tensor_copy(wrow_bf[:], wrow[:]))
                tprev = tc
                p.wait("tensor", tc)
                for cc in range(8):
                    slot = (cc % 2) * 512
                    tt = p.tensor(lambda e, cc=cc, slot=slot: e.transpose(
                        psTd0[:, slot:slot + 128], wrow_bf[:, cc * 128:(cc + 1) * 128],
                        ctx.ident_b[:]))
                    p.wait("vector", tt)
                    tev = p.vector(lambda e, cc=cc, oc=oc, wt=wt, slot=slot: e.tensor_copy(
                        wt[:, cc * D + oc * 128: cc * D + (oc + 1) * 128],
                        psTd0[:, slot:slot + 128]))
                    p.wait("tensor", tev)
                    tprev = tev

    with ctx.scope():
        bd = ctx.sb("bd", [128, 128])
        agt = ctx.sb("agt_d", [128, 8 * 128])
        pflagt = ctx.sb("pflagt", [128, 2])
        pbd1 = ctx.sb("pbd1", [128, 128])
        for dc in range(8):
            p.dma("ld", bd[:, dc * 16:(dc + 1) * 16],
                  xf_src[dc * 128:(dc + 1) * 128, 255:RPC:256],
                  allow_slow_non_contiguous=True)
        p.dma("ld", pflagt[:], g["pflag"][:])
        p.wait_dma("gpsimd", "ld")
        p.dma("cc", g["ag_bd_in"][:], bd[:])
        p.wait_dma("gpsimd", "cc")
        p.cc(lambda e: e.collective_compute(
            "AllGather", ALU.bypass, replica_groups=[list(range(NCORES))],
            ins=[g["ag_bd_in"].opt()], outs=[g["ag_bd_out"].opt()]))
        p.wait_cc("gpsimd")
        for j in range(NCORES):
            p.dma("cc", agt[:, j * 128:(j + 1) * 128], g["ag_bd_out"][j * 128:(j + 1) * 128, :])
        p.wait_dma("vector", "cc")
        tm0 = p.vector(lambda e: e.tensor_scalar_mul(bd_use[:], agt[:, 0:128], ctx.psel128[:, 0:1]))
        tcomb = tm0
        for j in range(1, NCORES):
            tcomb = p.vector(lambda e, j=j: e.scalar_tensor_tensor(
                bd_use[:], agt[:, j * 128:(j + 1) * 128], ctx.psel128[:, j:j + 1], bd_use[:],
                ALU.mult, ALU.add))
        for dc in range(8):
            p.vector(lambda e, dc=dc: e.memset(pbd1[:, dc * 16:dc * 16 + 1], 0.0))
            p.vector(lambda e, dc=dc: e.tensor_copy(
                pbd1[:, dc * 16 + 1:(dc + 1) * 16], bd_use[:, dc * 16:(dc + 1) * 16 - 1]))
        p.vector(lambda e: e.tensor_scalar_mul(bd_use[:], bd_use[:], pflagt[:, 0:1]))
        tbd = p.vector(lambda e: e.scalar_tensor_tensor(
            bd_use[:], pbd1[:], pflagt[:, 1:2], bd_use[:], ALU.mult, ALU.add))
        p.wait("tensor", tbd)
        p.wait("sync", tbd)

    with ctx.scope():
        TBW = 256
        xseg2 = [ctx.sb("xseg_d0", [128, 8 * 257]), ctx.sb("xseg_d1", [128, 8 * 257])]
        xr_b = ctx.sb("xr_d", [128, 8 * TBW], BF16)
        xk_b = ctx.sb("xk_d", [128, 8 * TBW], BF16)
        rsig = ctx.sb("rsig_d", [128, 8 * TBW])
        kk = ctx.sb("kk_d", [128, TBW])
        kact = ctx.sb("kact_d", [128, 8 * TBW], BF16)
        ycol = ctx.sb("ycol_d", [128, 8 * TBW])
        otok = ctx.sb("otok_d", [128, 8 * 128])
        tmpd = ctx.sb("tmp_d", [128, TBW])
        lnt = make_ln_tiles(ctx, TBW, "d")
        bankA = ctx.psum("bankA_d", [128, TBW])
        bankB = ctx.psum("bankB_d", [128, TBW])
        psTd = ctx.psum("psTd", [128, 1024])
        ps_stat = (ctx.psum("ps_st_d_s", [1, TBW]), ctx.psum("ps_st_d_q", [1, TBW]))
        ps_rep = ctx.psum("ps_rep_d", [128, 2 * TBW])

        def emit_loadmix(sg, wait_toks):
            xseg = xseg2[sg % 2]
            for dc in range(8):
                p.dma("ld", xseg[:, dc * 257 + 1:(dc + 1) * 257],
                      xf_src[dc * 128:(dc + 1) * 128, sg * TBW:(sg + 1) * TBW])
            p.wait_dma("vector", "ld")
            p.vector(lambda e, sg=sg, xseg=xseg: e.tensor_copy(
                xseg.rearrange("p (c s) -> p c s", s=257)[:, :, 0:1],
                bd_use.rearrange("p (c s) -> p c s", s=16)[:, :, sg:sg + 1]))
            for tok in wait_toks:
                p.wait("vector", tok)
            tmix = None
            for dc in range(8):
                p.vector(lambda e, dc=dc, xseg=xseg: e.tensor_scalar_mul(
                    tmpd[:], xseg[:, dc * 257 + 1:(dc + 1) * 257], murc[:, dc:dc + 1]))
                p.vector(lambda e, dc=dc, xseg=xseg: e.scalar_tensor_tensor(
                    xr_b[:, dc * TBW:(dc + 1) * TBW], xseg[:, dc * 257:(dc + 1) * 257 - 1],
                    mur1c[:, dc:dc + 1], tmpd[:], ALU.mult, ALU.add))
                p.vector(lambda e, dc=dc, xseg=xseg: e.tensor_scalar_mul(
                    tmpd[:], xseg[:, dc * 257 + 1:(dc + 1) * 257], mukc[:, dc:dc + 1]))
                tmix = p.vector(lambda e, dc=dc, xseg=xseg: e.scalar_tensor_tensor(
                    xk_b[:, dc * TBW:(dc + 1) * TBW], xseg[:, dc * 257:(dc + 1) * 257 - 1],
                    muk1c[:, dc:dc + 1], tmpd[:], ALU.mult, ALU.add))
            return tmix

        res_buf = {0: None, 1: None}
        tmix_next = None
        for sg in range(NCHUNK):
            xseg = xseg2[sg % 2]
            tmix = emit_loadmix(0, ()) if sg == 0 else tmix_next
            p.wait("tensor", tmix)
            t_ev = {0: None, 1: None}
            for dd in range(8):
                bank = bankA if dd % 2 == 0 else bankB
                p.wait("tensor", t_ev[dd % 2])
                mm = None
                for cc in range(8):
                    mm = p.tensor(lambda e, dd=dd, cc=cc, bank=bank: e.matmul(
                        bank[:], wrt[:, cc * D + dd * 128: cc * D + (dd + 1) * 128],
                        xr_b[:, cc * TBW:(cc + 1) * TBW],
                        start=(cc == 0), stop=(cc == 7)))
                p.wait("scalar", mm)
                tsg = p.scalar(lambda e, dd=dd, bank=bank: e.activation(
                    rsig[:, dd * TBW:(dd + 1) * TBW], bank[:], AF.Sigmoid,
                    bias=ctx.zero128[:], scale=1.0))
                t_ev[dd % 2] = tsg
            mm_r_last = mm
            for dd in range(8):
                bank = bankA if dd % 2 == 0 else bankB
                p.wait("tensor", t_ev[dd % 2])
                mm = None
                for cc in range(8):
                    mm = p.tensor(lambda e, dd=dd, cc=cc, bank=bank: e.matmul(
                        bank[:], wkt[:, cc * D + dd * 128: cc * D + (dd + 1) * 128],
                        xk_b[:, cc * TBW:(cc + 1) * TBW],
                        start=(cc == 0), stop=(cc == 7)))
                p.wait("scalar", mm)
                trl = p.scalar(lambda e, dd=dd, bank=bank: e.activation(
                    kk[:], bank[:], AF.Relu, bias=ctx.zero128[:], scale=1.0))
                p.wait("vector", trl)
                tk = p.vector(lambda e, dd=dd: e.tensor_mul(
                    kact[:, dd * TBW:(dd + 1) * TBW], kk[:], kk[:]))
                p.wait("scalar", tk)
                t_ev[dd % 2] = tk
            mm_k_last = mm
            if sg + 1 < NCHUNK:
                p.wait("sync", res_buf[(sg + 1) % 2])
                tmix_next = emit_loadmix(sg + 1, (mm_r_last, mm_k_last))
            p.wait("tensor", tk)
            t_evv = {0: None, 1: None}
            for dd in range(8):
                bank = bankA if dd % 2 == 0 else bankB
                p.wait("tensor", t_evv[dd % 2])
                mm = None
                for cc in range(8):
                    mm = p.tensor(lambda e, dd=dd, cc=cc, bank=bank: e.matmul(
                        bank[:], wvt[:, cc * D + dd * 128: cc * D + (dd + 1) * 128],
                        kact[:, cc * TBW:(cc + 1) * TBW],
                        start=(cc == 0), stop=(cc == 7)))
                p.wait("vector", mm)
                tmul = p.vector(lambda e, dd=dd, bank=bank: e.tensor_tensor(
                    ycol[:, dd * TBW:(dd + 1) * TBW], rsig[:, dd * TBW:(dd + 1) * TBW],
                    bank[:], ALU.mult))
                tres = p.vector(lambda e, dd=dd, xseg=xseg: e.tensor_add(
                    ycol[:, dd * TBW:(dd + 1) * TBW], ycol[:, dd * TBW:(dd + 1) * TBW],
                    xseg[:, dd * 257 + 1:(dd + 1) * 257]))
                t_evv[dd % 2] = tres
                res_buf[sg % 2] = tres
            tln = emit_ln(ctx, lnt, ycol, TBW,
                          lambda ch, ap: None, None, ps_stat, ps_rep)
            p.wait("tensor", tln)
            for t in range(2):
                ttr = None
                for dc in range(8):
                    ttr = p.tensor(lambda e, t=t, dc=dc: e.transpose(
                        psTd[:, dc * 128:(dc + 1) * 128],
                        ycol[:, dc * TBW + t * 128: dc * TBW + (t + 1) * 128],
                        ctx.ident_f[:]))
                p.wait("vector", ttr)
                p.wait_dma("vector", "st")
                tev = p.vector(lambda e: e.tensor_copy(otok[:], psTd[:]))
                p.wait("tensor", tev)
                p.wait("gpsimd", tev)
                p.dma("st", g["out_tok"][sg * TBW + t * 128: sg * TBW + (t + 1) * 128, :],
                      otok[:])

def build_program(phases=("A", "B", "C", "D"), debug_outputs=()):
    nc = bass.Bass("TRN2", target_bir_lowering=False)
    p = Prog(nc)
    es = ExitStack()
    ctx = Ctx(nc, p, es)
    g = ctx.g

    def dram(name, shape, dt=F32, kind="Internal"):
        if name in debug_outputs and kind == "Internal":
            kind = "ExternalOutput"
        g[name] = nc.dram_tensor(name, shape, dt, kind=kind).ap()
        return g[name]

    first, last = phases[0], phases[-1]

    if "A" in phases:
        dram("xb", [S, D], F32, kind="ExternalInput")
        dram("w_qkv", [D, H3], F32, kind="ExternalInput")
        dram("b_qkv", [1, H3], F32, kind="ExternalInput")
        dram("w_out", [D, D], F32, kind="ExternalInput")
        dram("b_out", [1, D], F32, kind="ExternalInput")
        dram("cos_q", [128, RPC], F32, kind="ExternalInput")
        dram("sin_q", [128, RPC], F32, kind="ExternalInput")
        dram("cos_k", [128, S], F32, kind="ExternalInput")
        dram("sin_k", [128, S], F32, kind="ExternalInput")
        dram("QT_s", [D, RPC], BF16)
        dram("KT_s", [D, S], BF16)
        dram("V_s", [S, D], BF16)
        dram("XR0", [D, RPC], F32)
    dram("ident_bf", [128, 128], BF16, kind="ExternalInput")
    dram("ident_f32", [128, 128], F32, kind="ExternalInput")
    if "B" in phases:
        dram("W1", [FF, D], F32, kind="ExternalInput")
        dram("b1", [1, FF], F32, kind="ExternalInput")
        dram("W2", [D, FF], F32, kind="ExternalInput")
        dram("b2", [1, D], F32, kind="ExternalInput")
        dram("lscale", [1, 1], F32, kind="ExternalInput")
        dram("bsel", [16, 8], F32, kind="ExternalInput")
        dram("ag_lt_in", [16, D], F32)
        dram("ag_lt_out", [128, D], F32)
        dram("lt_s", [16, D], F32)
        dram("evs_s", [16, 256], F32)
        dram("cscr", [4, 4, 256], BF16)
    if "C" in phases:
        dram("w0", [D, FF], F32, kind="ExternalInput")
        dram("w1f", [FF, D], F32, kind="ExternalInput")
        dram("w2", [D, FF], F32, kind="ExternalInput")
        dram("w2b_s", [D, FF], BF16)
        dram("cc_in", [1, 8], F32)
        dram("cc_out", [1, 8], F32)
    if "D" in phases:
        dram("W_r", [D, D], F32, kind="ExternalInput")
        dram("W_k", [D, D], F32, kind="ExternalInput")
        dram("W_v", [D, D], F32, kind="ExternalInput")
        dram("mu_r", [1, D], F32, kind="ExternalInput")
        dram("mu_k", [1, D], F32, kind="ExternalInput")
        dram("psel", [128, 8], F32, kind="ExternalInput")
        dram("pflag", [128, 2], F32, kind="ExternalInput")
        dram("ag_bd_in", [128, 128], F32)
        dram("ag_bd_out", [1024, 128], F32)
    if first != "A":
        dram("xT_in_f", [D, RPC], F32, kind="ExternalInput")
        dram("xT_in_b", [D, RPC], BF16, kind="ExternalInput")
    for ph in phases:
        if ph == "D":
            continue
        kind = "ExternalOutput" if ph == last else "Internal"
        dram(f"xf_{ph}", [D, RPC], F32, kind=kind)
        dram(f"xb_{ph}", [D, RPC], BF16, kind=kind)
    if "D" in phases:
        dram("out_tok", [RPC, D], F32, kind="ExternalOutput")

    with es:
        p.init_sems(es)
        p.dma_stream("ld", "sync")
        p.dma_stream("ld2", "scalar")
        p.dma_stream("ld3", "gpsimd")
        p.dma_stream("st2", "sync")
        p.dma_stream("st", "gpsimd")
        p.dma_stream("cc", "gpsimd")

        ctx.ident_b = ctx.sb("identb", [128, 128], BF16)
        ctx.ident_f = ctx.sb("identf", [128, 128], F32)
        ctx.ones_col = ctx.sb("ones_col", [128, 1], F32)
        ctx.ones_row = ctx.sb("ones_row", [1, 128], F32)
        ctx.ones_brow = ctx.sb("ones_brow", [1, 128], BF16)
        ctx.one11 = ctx.sb("one11", [1, 1], F32)
        ctx.eps1 = ctx.sb("eps1", [1, 1], F32)
        ctx.zero1 = ctx.sb("zero1", [1, 1], F32)
        ctx.zero128 = ctx.sb("zero128", [128, 1], F32)
        p.dma("ld", ctx.ident_b[:], g["ident_bf"][:])
        p.dma("ld", ctx.ident_f[:], g["ident_f32"][:])
        p.vector(lambda e: e.memset(ctx.ones_col[:], 1.0))
        p.vector(lambda e: e.memset(ctx.ones_row[:], 1.0))
        p.vector(lambda e: e.memset(ctx.ones_brow[:], 1.0))
        p.vector(lambda e: e.memset(ctx.one11[:], 1.0))
        if "D" in phases:
            ctx.psel128 = ctx.sb("psel128", [128, 8], F32)
            p.dma("ld", ctx.psel128[:], g["psel"][:])
        p.vector(lambda e: e.memset(ctx.eps1[:], LN_EPS))
        p.vector(lambda e: e.memset(ctx.zero1[:], 0.0))
        tz = p.vector(lambda e: e.memset(ctx.zero128[:], 0.0))
        p.wait("tensor", tz)
        p.wait("scalar", tz)
        p.wait_dma("tensor", "ld")

        for ph in phases:
            if ph == "A":
                emit_phase_a(ctx)
            elif ph == "B":
                emit_phase_b(ctx, src=("xf_A", "xb_A") if "A" in phases else ("xT_in_f", "xT_in_b"))
            elif ph == "C":
                idx = phases.index("C")
                src = (f"xf_{phases[idx-1]}", f"xb_{phases[idx-1]}") if idx > 0 else ("xT_in_f", "xT_in_b")
                emit_phase_c(ctx, src=src)
            elif ph == "D":
                idx = phases.index("D")
                src = (f"xf_{phases[idx-1]}", f"xb_{phases[idx-1]}") if idx > 0 else ("xT_in_f", "xT_in_b")
                emit_phase_d(ctx, src=src)
            p.barrier()

        with nc.Block() as block:
            p.emit_all(block)
    return nc


def host_constants(core):
    b, pp = core // 2, core % 2
    own_pos = np.concatenate(
        [np.arange(c * 512 + pp * 256, c * 512 + pp * 256 + 256) for c in range(NCHUNK)])
    other_pos = np.concatenate(
        [np.arange(c * 512 + (1 - pp) * 256, c * 512 + (1 - pp) * 256 + 256) for c in range(NCHUNK)])
    perm = np.concatenate([own_pos, other_pos])
    half = 32
    freqs = 1.0 / (500.0 ** (np.arange(half, dtype=np.float64) / half))

    def tables(pos, scale):
        ang = pos[None, :].astype(np.float64) * freqs[:, None]
        cos = np.cos(ang) * scale
        sin = np.sin(ang) * scale
        cos128 = np.tile(cos, (4, 1))
        sinS = np.concatenate([-sin, sin, -sin, sin], axis=0)
        return cos128.astype(np.float32), sinS.astype(np.float32)

    kscale = 1.0 / math.sqrt(DH)
    cq, sq = tables(own_pos, 1.0)
    ck, sk = tables(perm, kscale)
    bsel = np.zeros((16, 8), dtype=np.float32)
    bsel[:, 2 * b] = 1.0
    psel = np.zeros((128, 8), dtype=np.float32)
    psel[:, 2 * b + (1 - pp)] = 1.0
    pflag = np.zeros((128, 2), dtype=np.float32)
    pflag[:, 0] = 1.0 if pp == 1 else 0.0
    pflag[:, 1] = 1.0 if pp == 0 else 0.0
    return dict(own_pos=own_pos, perm=perm, cos_q=cq, sin_q=sq, cos_k=ck, sin_k=sk,
                bsel=bsel, psel=psel, pflag=pflag,
                ident_bf=np.eye(128, dtype=ml_dtypes.bfloat16),
                ident_f32=np.eye(128, dtype=np.float32))


def host_inputs_for_phases(inputs, phases, core, const, stream_f=None, stream_b=None):
    b = core // 2
    d = {}
    d["ident_bf"] = const["ident_bf"]
    d["ident_f32"] = const["ident_f32"]
    if "A" in phases:
        x = np.asarray(inputs["L0_attn_x"])
        d["xb"] = np.ascontiguousarray(x[b][const["perm"]])
        d["w_qkv"] = np.asarray(inputs["L0_attn_w_qkv"])
        d["b_qkv"] = np.asarray(inputs["L0_attn_b_qkv"]).reshape(1, -1)
        d["w_out"] = np.asarray(inputs["L0_attn_w_out"])
        d["b_out"] = np.asarray(inputs["L0_attn_b_out"]).reshape(1, -1)
        d["cos_q"], d["sin_q"] = const["cos_q"], const["sin_q"]
        d["cos_k"], d["sin_k"] = const["cos_k"], const["sin_k"]
    else:
        d["xT_in_f"] = stream_f[core]
        d["xT_in_b"] = stream_b[core]
    if "B" in phases:
        d["W1"] = np.asarray(inputs["L0_ffn_W1"])
        d["b1"] = np.asarray(inputs["L0_ffn_b1"]).reshape(1, -1)
        d["W2"] = np.asarray(inputs["L0_ffn_W2"])
        d["b2"] = np.asarray(inputs["L0_ffn_b2"]).reshape(1, -1)
        d["lscale"] = np.asarray(inputs["L0_ffn_scale"]).reshape(1, 1)
        d["bsel"] = const["bsel"]
    if "C" in phases:
        d["w0"] = np.asarray(inputs["L1_attn_w0"])
        d["w1f"] = np.asarray(inputs["L1_attn_w1"])
        d["w2"] = np.asarray(inputs["L1_attn_w2"])
    if "D" in phases:
        d["W_r"] = np.asarray(inputs["L1_ffn_W_r"])
        d["W_k"] = np.asarray(inputs["L1_ffn_W_k"])
        d["W_v"] = np.asarray(inputs["L1_ffn_W_v"])
        d["mu_r"] = np.asarray(inputs["L1_ffn_mu_r"]).reshape(1, -1)
        d["mu_k"] = np.asarray(inputs["L1_ffn_mu_k"]).reshape(1, -1)
        d["psel"] = const["psel"]
        d["pflag"] = const["pflag"]
    return d


def run_launch(inputs, phases, stream_f=None, stream_b=None, debug_outputs=()):
    nc = build_program(phases=tuple(phases), debug_outputs=debug_outputs)
    consts = [host_constants(c) for c in range(NCORES)]
    in_maps = [host_inputs_for_phases(inputs, phases, c, consts[c], stream_f, stream_b)
               for c in range(NCORES)]
    res = run_bass_kernel_spmd(nc, in_maps, core_ids=list(range(NCORES)))
    return res.results


def kernel(**inputs):
    consts = [host_constants(c) for c in range(NCORES)]
    if _os.environ.get("KERNEL_UNFUSED") == "1":
        groups = [("A",), ("B",), ("C",), ("D",)]
    else:
        groups = [("A", "B", "C", "D")]
    stream_f = stream_b = None
    out = None
    for phases in groups:
        results = run_launch(inputs, phases, stream_f, stream_b)
        last = phases[-1]
        if last != "D":
            stream_f = [np.asarray(results[c][f"xf_{last}"]) for c in range(NCORES)]
            stream_b = [np.asarray(results[c][f"xb_{last}"]) for c in range(NCORES)]
        else:
            out = np.zeros((B, S, D), dtype=np.float32)
            for c in range(NCORES):
                b = c // 2
                out[b][consts[c]["own_pos"]] = np.asarray(results[c]["out_tok"])
    return out


# version 10 (final)
# speedup vs baseline: 1.0054x; 1.0054x over previous
import math
import os as _os
from contextlib import ExitStack
A_STAGE = _os.environ.get("KERNEL_A_STAGE", "all")

import numpy as np
import ml_dtypes

import concourse.bass as bass
import concourse.mybir as mybir
from concourse.bass_utils import run_bass_kernel_spmd

F32 = mybir.dt.float32
BF16 = mybir.dt.bfloat16
AF = mybir.ActivationFunctionType
ALU = mybir.AluOpType
AX = mybir.AxisListType

B, S, D = 4, 8192, 1024
H3 = 3 * D
NCORES = 8
RPC = S // 2
NCHUNK = 16
SEG = 256
NH, DH, NPAIR = 16, 64, 8
FF = 4096
EXP_BIAS = 12.0
LN_EPS = 1e-5
LR = 0.001

ENGINES = ("sync", "tensor", "vector", "scalar", "gpsimd")


class Prog:
    def __init__(self, nc):
        self.nc = nc
        self.ops = {e: [] for e in ENGINES}
        self.cnt = {e: 0 for e in ENGINES}
        self.waited = {e: set() for e in ENGINES}
        self.dma_cnt = {}
        self.dma_streams = {}
        self._es = None
        self.sems = {}

    def init_sems(self, es):
        for e in ENGINES:
            self.sems[e] = es.enter_context(self.nc.semaphore(f"sem_{e}"))
        self._es = es

    def dma_stream(self, name, queue_engine):
        if name not in self.dma_streams:
            sem = self._es.enter_context(self.nc.semaphore(f"dma_{name}"))
            self.dma_streams[name] = (queue_engine, sem)
            self.dma_cnt[name] = 0
        return name

    def op(self, engine, fn):
        self.cnt[engine] += 1
        tok = (engine, self.cnt[engine])
        self.ops[engine].append(("op", fn, tok))
        return tok

    def tensor(self, fn):
        return self.op("tensor", fn)

    def vector(self, fn):
        return self.op("vector", fn)

    def scalar(self, fn):
        return self.op("scalar", fn)

    def gpsimd(self, fn):
        return self.op("gpsimd", fn)

    def sync_op(self, fn):
        return self.op("sync", fn)

    def dma(self, stream, out, in_, **kw):
        qe, _sem = self.dma_streams[stream]
        self.dma_cnt[stream] += 16
        fn = lambda eng, out=out, in_=in_, kw=kw: eng.dma_start(out=out, in_=in_, **kw)
        self.ops[qe].append(("dma", fn, stream))
        return self.dma_cnt[stream]

    def wait(self, engine, tok):
        if tok is None:
            return
        (pe, t) = tok
        if pe == engine:
            return
        self.waited[pe].add(t)
        self.ops[engine].append(("wait", "op", pe, t))

    def cc(self, fn):
        if not hasattr(self, "cc_sem"):
            self.cc_sem = self._es.enter_context(self.nc.semaphore("cc_sem"))
            self.cc_cnt = 0
        self.cc_cnt += 1
        self.ops["gpsimd"].append(("cc", fn, self.cc_cnt))
        return self.cc_cnt

    def wait_cc(self, engine):
        if getattr(self, "cc_cnt", 0) == 0:
            return
        self.ops[engine].append(("wait", "cc", None, self.cc_cnt))

    def wait_dma(self, engine, stream, upto=None):
        val = self.dma_cnt[stream] if upto is None else upto
        if val == 0:
            return
        self.ops[engine].append(("wait", "dma", stream, val))

    def barrier(self):
        toks = {e: self.cnt[e] for e in ENGINES if self.cnt[e] > 0}
        dmav = dict(self.dma_cnt)
        for e in ENGINES:
            for pe, t in toks.items():
                if pe != e:
                    self.waited[pe].add(t)
                    self.ops[e].append(("wait", "op", pe, t))
            for s, v in dmav.items():
                if v > 0:
                    self.ops[e].append(("wait", "dma", s, v))
            self.wait_cc(e)

    def emit_all(self, block):
        rank = {}
        for e in ENGINES:
            w = sorted(self.waited[e])
            rank[e] = {t: i + 1 for i, t in enumerate(w)}

        def run_engine(ename):
            def body(eng):
                for entry in self.ops[ename]:
                    kind = entry[0]
                    if kind == "op":
                        _, fn, tok = entry
                        inst = fn(eng)
                        if tok[1] in self.waited[ename]:
                            inst.then_inc(self.sems[ename], 1)
                    elif kind == "dma":
                        _, fn, stream = entry
                        fn(eng).then_inc(self.dma_streams[stream][1], 16)
                    elif kind == "cc":
                        _, fn, _n = entry
                        fn(eng).then_inc(self.cc_sem, 1)
                    else:
                        _, wkind, key, val = entry
                        if wkind == "op":
                            eng.wait_ge(self.sems[key], rank[key][val])
                        elif wkind == "cc":
                            eng.wait_ge(self.cc_sem, val)
                        else:
                            eng.wait_ge(self.dma_streams[key][1], val)
            return body

        block.sync(run_engine("sync"))
        block.tensor(run_engine("tensor"))
        block.vector(run_engine("vector"))
        block.scalar(run_engine("scalar"))
        block.gpsimd(run_engine("gpsimd"))


class Ctx:
    def __init__(self, nc, prog, es):
        self.nc = nc
        self.p = prog
        self.es = es
        self.scopes = [es]
        self.g = {}

    def sb(self, name, shape, dt=F32):
        return self.scopes[-1].enter_context(self.nc.sbuf_tensor(name, shape, dt))

    def psum(self, name, shape, dt=F32):
        return self.scopes[-1].enter_context(self.nc.psum_tensor(name, shape, dt))

    def scope(self):
        ctx = self

        class _Scope:
            def __enter__(self_s):
                self_s.es = ExitStack()
                ctx.scopes.append(self_s.es)
                return ctx

            def __exit__(self_s, *a):
                if a[0] is None:
                    ctx.p.barrier()
                ctx.scopes.pop()
                return self_s.es.__exit__(*a)

        return _Scope()


def bias_chunks(ctx, dram_ap, n, tag, scale=1.0):
    p = ctx.p
    out = ctx.sb(f"bchunks_{tag}", [128, n], F32)
    with ctx.scope():
        row = ctx.sb(f"brow_{tag}", [1, n * 128], F32)
        ps = ctx.psum(f"psb_{tag}", [128, 1], F32)
        p.dma("ld", row[:], dram_ap)
        p.wait_dma("tensor", "ld")
        for j in range(n):
            tmm = p.tensor(lambda e, j=j: e.matmul(
                ps[:], row[0:1, j * 128:(j + 1) * 128], ctx.one11[:], start=True, stop=True))
            p.wait("vector", tmm)
            if scale == 1.0:
                last = p.vector(lambda e, j=j: e.tensor_copy(out[:, j:j + 1], ps[:]))
            else:
                last = p.vector(lambda e, j=j: e.tensor_scalar_mul(out[:, j:j + 1], ps[:], scale))
            p.wait("tensor", last)
    return out


def emit_ln(ctx, lnt, ycol, N, store_xf, store_xb, ps_stat, ps_rep, st_stream="st"):
    p = ctx.p
    y2t, srow, mrow, vrow, rrow, mr2, rep, xbt = lnt

    ps_s, ps_q = ps_stat
    mm_q = None
    for c in range(8):
        p.wait("vector", mm_q)
        tsq = p.vector(lambda e, c=c: e.tensor_mul(
            y2t[:], ycol[:, c * N:(c + 1) * N], ycol[:, c * N:(c + 1) * N]))
        p.tensor(lambda e, c=c: e.matmul(
            ps_s[:], ctx.ones_col[:], ycol[:, c * N:(c + 1) * N],
            start=(c == 0), stop=(c == 7)))
        p.wait("tensor", tsq)
        mm_q = p.tensor(lambda e, c=c: e.matmul(
            ps_q[:], ctx.ones_col[:], y2t[:],
            start=(c == 0), stop=(c == 7)))
    p.wait("vector", mm_q)
    p.vector(lambda e: e.tensor_copy(srow[0:1, 0:N], ps_s[:]))
    tcp = p.vector(lambda e: e.tensor_copy(srow[0:1, N:2 * N], ps_q[:]))
    p.wait("tensor", tcp)
    p.vector(lambda e: e.tensor_scalar_mul(mrow[:], srow[0:1, 0:N], 1.0 / D))
    p.vector(lambda e: e.tensor_scalar_mul(vrow[:], srow[0:1, N:2 * N], 1.0 / D))
    p.vector(lambda e: e.tensor_mul(mr2[:], mrow[:], mrow[:]))
    tv = p.vector(lambda e: e.tensor_sub(vrow[:], vrow[:], mr2[:]))
    p.wait("scalar", tv)
    p.scalar(lambda e: e.activation(rrow[:], vrow[:], AF.Ln, bias=ctx.eps1[:], scale=1.0))
    tr = p.scalar(lambda e: e.activation(rrow[:], rrow[:], AF.Exp, bias=ctx.zero1[:], scale=-0.5))
    p.wait("vector", tr)
    tmr = p.vector(lambda e: e.tensor_mul(mr2[:], mrow[:], rrow[:]))
    p.wait("tensor", tmr)
    bm1 = p.tensor(lambda e: e.matmul(ps_rep[:, 0:N], ctx.ones_row[:], rrow[:], start=True, stop=True))
    bm2 = p.tensor(lambda e: e.matmul(ps_rep[:, N:2 * N], ctx.ones_row[:], mr2[:], start=True, stop=True))
    p.wait("vector", bm1)
    p.wait("vector", bm2)
    tcr = p.vector(lambda e: e.tensor_copy(rep[:], ps_rep[:, 0:2 * N]))
    p.wait("tensor", tcr)
    for c in range(8):
        p.vector(lambda e, c=c: e.tensor_mul(ycol[:, c * N:(c + 1) * N],
                                             ycol[:, c * N:(c + 1) * N], rep[:, 0:N]))
        tfc = p.vector(lambda e, c=c: e.tensor_sub(ycol[:, c * N:(c + 1) * N],
                                                   ycol[:, c * N:(c + 1) * N], rep[:, N:2 * N]))
        p.wait("gpsimd" if st_stream == "st" else "sync", tfc)
        store_xf(c, ycol[:, c * N:(c + 1) * N])
        if store_xb is not None:
            p.wait_dma("vector", st_stream)
            tbc = p.vector(lambda e, c=c: e.tensor_copy(xbt[:], ycol[:, c * N:(c + 1) * N]))
            p.wait("gpsimd" if st_stream == "st" else "sync", tbc)
            store_xb(c, xbt[:])
    p.wait_dma("vector", st_stream)
    return tfc


def make_ln_tiles(ctx, N, tag):
    return (
        ctx.sb(f"ln_y2_{tag}", [128, N], F32),
        ctx.sb(f"ln_srow_{tag}", [1, 2 * N], F32),
        ctx.sb(f"ln_mrow_{tag}", [1, N], F32),
        ctx.sb(f"ln_vrow_{tag}", [1, N], F32),
        ctx.sb(f"ln_rrow_{tag}", [1, N], F32),
        ctx.sb(f"ln_mr2_{tag}", [1, N], F32),
        ctx.sb(f"ln_rep_{tag}", [128, 2 * N], F32),
        ctx.sb(f"ln_xb_{tag}", [128, N], BF16),
    )


def emit_rope(ctx, tmp, ct, st, out_bf, rop1, rop2, N):
    p = ctx.p
    p.vector(lambda e: e.tensor_mul(rop1[:], tmp[:], ct[:]))
    for q in range(4):
        a, b = q * 32, (q ^ 1) * 32
        p.vector(lambda e, a=a, b=b: e.tensor_copy(rop2[a:a + 32, :], tmp[b:b + 32, :]))
    p.vector(lambda e: e.tensor_mul(rop2[:], rop2[:], st[:]))
    return p.vector(lambda e: e.tensor_add(out_bf[:], rop1[:], rop2[:]))


def emit_phase_a(ctx_outer):
    with ctx_outer.scope() as ctx:
        _emit_phase_a(ctx)


def _emit_phase_a(ctx):
    p, g, nc = ctx.p, ctx.g, ctx.nc
    NBLK = 16

    wqkv_b = ctx.sb("wqkv_b", [128, 8 * H3], BF16)
    wout_b = ctx.sb("wout_b", [128, 8 * D], BF16)
    bqk = bias_chunks(ctx, g["b_qkv"][0:1, 0:2048], 16, "bqk")
    bo = bias_chunks(ctx, g["b_out"][0:1, :], 8, "bo")
    if _os.environ.get("KERNEL_DEBUG_BIAS") == "1":
        dbg = nc.dram_tensor("bqk_dbg", [128, 16], F32, kind="ExternalOutput").ap()
        p.barrier()
        p.dma("st", dbg[:], bqk[:])
    bv_b = ctx.sb("bv_b", [1, D], BF16)
    o_all = ctx.sb("o_all", [128, NPAIR * RPC], BF16)
    with ctx.scope():
        wtmp = ctx.sb("wtmp_a", [128, H3], F32)
        for dc in range(8):
            p.dma("ld", wtmp[:], g["w_qkv"][dc * 128:(dc + 1) * 128, :])
            p.wait_dma("vector", "ld")
            tcast = p.vector(lambda e, dc=dc: e.tensor_copy(wqkv_b[:, dc * H3:(dc + 1) * H3], wtmp[:]))
            p.wait("sync", tcast)
        for dc in range(8):
            p.dma("ld", wtmp[:, 0:D], g["w_out"][dc * 128:(dc + 1) * 128, :])
            p.wait_dma("vector", "ld")
            tcast = p.vector(lambda e, dc=dc: e.tensor_copy(wout_b[:, dc * D:(dc + 1) * D], wtmp[:, 0:D]))
            p.wait("sync", tcast)
        p.dma("ld", wtmp[0:1, 0:D], g["b_qkv"][0:1, 2048:3072])
        p.wait_dma("vector", "ld")
        p.vector(lambda e: e.tensor_copy(bv_b[:], wtmp[0:1, 0:D]))

    with ctx.scope():
        sb_ = ctx.sb
        ps_ = ctx.psum
        xrow = sb_("xrow", [128, D])
        xTf = sb_("xTf", [128, 8 * 512])
        xTb = sb_("xTb", [128, 8 * 512], BF16)
        ct = sb_("ct", [128, 512])
        st_t = sb_("st_t", [128, 512])
        ctq = sb_("ctq", [128, 512])
        stq = sb_("stq", [128, 512])
        tmpe = sb_("tmpe", [128, 512])
        rop1 = sb_("rop1", [128, 512])
        rop2 = sb_("rop2", [128, 512])
        kout = sb_("kout", [128, 512], BF16)
        qout = sb_("qout", [128, 512], BF16)
        vout = sb_("vout", [128, 512], BF16)
        psT = ps_("psT", [128, 1024])
        psQK = [ps_("psQK0", [128, 512]), ps_("psQK1", [128, 512])]
        psV = ps_("psV", [128, 512])

        last_rope = None
        tlast_tr = None
        for blk in range(NBLK):
            own = blk < 8
            p.wait_dma("vector", "st")
            tprev_cast = None
            for t in range(4):
                if tlast_tr is not None:
                    p.wait("sync", tlast_tr)
                p.dma("ld", xrow[:], g["xb"][blk * 512 + t * 128: blk * 512 + (t + 1) * 128, :])
                p.wait_dma("tensor", "ld")
                ttr = None
                for dc in range(8):
                    slot = (dc % 2) * 512
                    ttr = p.tensor(lambda e, dc=dc, slot=slot: e.transpose(
                        psT[:, slot:slot + 128], xrow[:, dc * 128:(dc + 1) * 128], ctx.ident_f[:]))
                    p.wait("vector", ttr)
                    tev = p.vector(lambda e, dc=dc, t=t, slot=slot: e.tensor_copy(
                        xTf[:, dc * 512 + t * 128: dc * 512 + (t + 1) * 128],
                        psT[:, slot:slot + 128]))
                    p.wait("tensor", tev)
                tlast_tr = tev
                tcast = p.vector(lambda e, t=t: e.tensor_copy(
                    xTb.rearrange("p (c n) -> p c n", c=8)[:, :, t * 128:(t + 1) * 128],
                    xTf.rearrange("p (c n) -> p c n", c=8)[:, :, t * 128:(t + 1) * 128]))
                tprev_cast = tcast
            if own:
                p.wait("gpsimd", tprev_cast)
                for dc in range(8):
                    p.dma("st", g["XR0"][dc * 128:(dc + 1) * 128, blk * 512:(blk + 1) * 512],
                          xTf[:, dc * 512:(dc + 1) * 512])
            if A_STAGE == "tr":
                continue
            p.wait("scalar", last_rope)
            p.dma("ld2", ct[:], g["cos_k"][:, blk * 512:(blk + 1) * 512])
            p.dma("ld2", st_t[:], g["sin_k"][:, blk * 512:(blk + 1) * 512])
            if own:
                p.dma("ld2", ctq[:], g["cos_q"][:, blk * 512:(blk + 1) * 512])
                p.dma("ld2", stq[:], g["sin_q"][:, blk * 512:(blk + 1) * 512])
            p.wait("tensor", tprev_cast)
            p.wait_dma("vector", "ld2")

            for kc in range(8):
                bank = psQK[kc % 2]
                mm = None
                for dc in range(8):
                    mm = p.tensor(lambda e, kc=kc, dc=dc, bank=bank: e.matmul(
                        bank[:], wqkv_b[:, dc * H3 + D + kc * 128: dc * H3 + D + (kc + 1) * 128],
                        xTb[:, dc * 512:(dc + 1) * 512],
                        start=(dc == 0), stop=(dc == 7)))
                p.wait("vector", mm)
                tb_ = p.vector(lambda e, kc=kc, bank=bank: e.tensor_scalar_add(tmpe[:], bank[:], bqk[:, 8 + kc:9 + kc]))
                p.wait("tensor", tb_)
                tlast = emit_rope(ctx, tmpe, ct, st_t, kout, rop1, rop2, 512)
                last_rope = tlast
                p.wait("gpsimd", tlast)
                p.dma("st", g["KT_s"][kc * 128:(kc + 1) * 128, blk * 512:(blk + 1) * 512], kout[:])
                p.wait_dma("vector", "st")
            if own:
                for qc in range(8):
                    bank = psQK[qc % 2]
                    mm = None
                    for dc in range(8):
                        mm = p.tensor(lambda e, qc=qc, dc=dc, bank=bank: e.matmul(
                            bank[:], wqkv_b[:, dc * H3 + qc * 128: dc * H3 + (qc + 1) * 128],
                            xTb[:, dc * 512:(dc + 1) * 512],
                            start=(dc == 0), stop=(dc == 7)))
                    p.wait("vector", mm)
                    tb_ = p.vector(lambda e, qc=qc, bank=bank: e.tensor_scalar_add(
                        tmpe[:], bank[:], bqk[:, qc:qc + 1]))
                    p.wait("tensor", tb_)
                    tlast = emit_rope(ctx, tmpe, ctq, stq, qout, rop1, rop2, 512)
                    last_rope = tlast
                    p.wait("gpsimd", tlast)
                    p.dma("st", g["QT_s"][qc * 128:(qc + 1) * 128, blk * 512:(blk + 1) * 512], qout[:])
                    p.wait_dma("vector", "st")
            for t in range(4):
                for vc in range(2):
                    mm = None
                    for dc in range(8):
                        mm = p.tensor(lambda e, t=t, vc=vc, dc=dc: e.matmul(
                            psV[:], xTb[:, dc * 512 + t * 128: dc * 512 + (t + 1) * 128],
                            wqkv_b[:, dc * H3 + 2048 + vc * 512: dc * H3 + 2048 + (vc + 1) * 512],
                            start=(dc == 0), stop=False))
                    mm = p.tensor(lambda e, t=t, vc=vc: e.matmul(
                        psV[:], ctx.ones_brow[:], bv_b[0:1, vc * 512:(vc + 1) * 512],
                        start=False, stop=True))
                    p.wait("vector", mm)
                    tev = p.vector(lambda e: e.tensor_copy(vout[:], psV[:]))
                    p.wait("tensor", tev)
                    p.wait("gpsimd", tev)
                    p.dma("st", g["V_s"][blk * 512 + t * 128: blk * 512 + (t + 1) * 128,
                                         vc * 512:(vc + 1) * 512], vout[:])
                    p.wait_dma("vector", "st")

    p.barrier()
    if A_STAGE in ("tr", "qkv"):
        return

    with ctx.scope():
        sb_ = ctx.sb
        ps_ = ctx.psum
        qT = sb_("qT", [128, RPC], BF16)
        kT = sb_("kT", [128, S], BF16)
        vpA = sb_("vpA", [128, 64 * 65], BF16)
        vpB = sb_("vpB", [128, 64 * 65], BF16)
        PP = sb_("pp", [128, 4 * 1024], BF16)
        osA = sb_("osA", [65, 512])
        osB = sb_("osB", [65, 512])
        zrec = sb_("zrec", [1, 1024])
        psS2 = [ps_("psSa", [128, 1024]), ps_("psSb", [128, 1024])]
        poA = ps_("poA", [65, 512])
        poB = ps_("poB", [65, 512])
        psR = ps_("psR", [128, 1024])
        mb = sb_("mb", [128, 1])
        p.vector(lambda e: e.memset(mb[:], -EXP_BIAS))

        prev_smm = None
        prev_pv = None
        prev_tmo = None
        for pair in range(NPAIR):
            if pair > 0:
                for engq in ("sync", "scalar", "gpsimd"):
                    p.wait(engq, prev_smm)
                    p.wait(engq, prev_pv)
                p.wait("vector", prev_pv)
            p.dma("ld", qT[:], g["QT_s"][pair * 128:(pair + 1) * 128, :])
            p.dma("ld2", kT[:], g["KT_s"][pair * 128:(pair + 1) * 128, :])
            p.vector(lambda e: e.memset(vpA.rearrange("p (n c) -> p n c", c=65)[:, :, 64], 1.0))
            tms = p.vector(lambda e: e.memset(vpB.rearrange("p (n c) -> p n c", c=65)[:, :, 64], 1.0))
            p.dma("ld3", vpA.rearrange("p (n c) -> p n c", c=65)[:, :, 0:64],
                  g["V_s"].rearrange("(n p) c -> p n c", p=128)[:, :, pair * 128: pair * 128 + 64])
            p.dma("ld", vpB.rearrange("p (n c) -> p n c", c=65)[:, :, 0:64],
                  g["V_s"].rearrange("(n p) c -> p n c", p=128)[:, :, pair * 128 + 64: pair * 128 + 128])
            p.wait_dma("tensor", "ld")
            p.wait_dma("tensor", "ld2")
            p.wait_dma("tensor", "ld3")
            p.wait("tensor", tms)

            exp_hist = {}
            pv_hist = {}

            def emit_pv(kt, gk):
                p.wait("tensor", exp_hist[gk])
                base = (gk % 4) * 1024
                pv = p.tensor(lambda e, kt=kt, base=base: e.matmul(
                    poA[:], vpA[:, kt * 65:(kt + 1) * 65],
                    PP[:, base:base + 512],
                    start=(kt == 0), stop=(kt == 63)))
                pv = p.tensor(lambda e, kt=kt, base=base: e.matmul(
                    poB[:], vpB[:, kt * 65:(kt + 1) * 65],
                    PP[:, base + 512:base + 1024],
                    start=(kt == 0), stop=(kt == 63)))
                pv_hist[gk] = pv

            for qc in range(8):
                for kt in range(64):
                    gk = qc * 64 + kt
                    if gk >= 2:
                        p.wait("tensor", exp_hist[gk - 2])
                    bank = psS2[gk % 2]
                    smm = p.tensor(lambda e, kt=kt, qc=qc, bank=bank: e.matmul(
                        bank[:, 0:512],
                        kT[0:64, kt * 128:(kt + 1) * 128], qT[0:64, qc * 512:(qc + 1) * 512],
                        start=True, stop=True, tile_position=(0, 0)))
                    smm = p.tensor(lambda e, kt=kt, qc=qc, bank=bank: e.matmul(
                        bank[:, 512:1024],
                        kT[64:128, kt * 128:(kt + 1) * 128], qT[64:128, qc * 512:(qc + 1) * 512],
                        start=True, stop=True, tile_position=(64, 0)))
                    p.wait("scalar", smm)
                    if gk >= 4:
                        p.wait("scalar", pv_hist[gk - 4])
                    texp = p.scalar(lambda e, gk=gk, bank=bank: e.activation(
                        PP[:, (gk % 4) * 1024:(gk % 4 + 1) * 1024], bank[:],
                        AF.Exp, bias=mb[:], scale=1.0))
                    exp_hist[gk] = texp
                    if kt > 0:
                        emit_pv(kt - 1, gk - 1)
                emit_pv(63, qc * 64 + 63)
                p.wait("vector", pv_hist[qc * 64 + 63])
                p.vector(lambda e: e.tensor_copy(osA[:], poA[:]))
                tcb = p.vector(lambda e: e.tensor_copy(osB[:], poB[:]))
                p.vector(lambda e: e.reciprocal(zrec[0:1, 0:512], osA[64:65, :]))
                trc = p.vector(lambda e: e.reciprocal(zrec[0:1, 512:1024], osB[64:65, :]))
                p.wait("tensor", tcb)
                p.wait("tensor", trc)
                p.wait("tensor", prev_tmo)
                z1 = p.tensor(lambda e: e.matmul(psR[0:64, 0:512], ctx.ones_row[0:1, 0:64],
                                                 zrec[0:1, 0:512], start=True, stop=True))
                z2 = p.tensor(lambda e: e.matmul(psR[64:128, 512:1024], ctx.ones_row[0:1, 0:64],
                                                 zrec[0:1, 512:1024], start=True, stop=True))
                p.wait("vector", z1)
                p.wait("vector", z2)
                p.vector(lambda e, pair=pair, qc=qc: e.tensor_mul(
                    o_all[0:64, pair * RPC + qc * 512: pair * RPC + (qc + 1) * 512],
                    osA[0:64, :], psR[0:64, 0:512]))
                tmo = p.vector(lambda e, pair=pair, qc=qc: e.tensor_mul(
                    o_all[64:128, pair * RPC + qc * 512: pair * RPC + (qc + 1) * 512],
                    osB[0:64, :], psR[64:128, 512:1024]))
                prev_tmo = tmo
            prev_smm = smm
            prev_pv = pv_hist[7 * 64 + 63]

    if A_STAGE == "flash":
        return
    with ctx.scope():
        sb_ = ctx.sb
        ps_ = ctx.psum
        ycol = sb_("ycol", [128, 8 * 512])
        xrt_all = sb_("xrt_all", [128, 8 * 512])
        psOb = [ps_("psOa3", [128, 512]), ps_("psOb3", [128, 512])]
        ps_stat = (ps_("ps_stat_a_s", [1, 512]), ps_("ps_stat_a_q", [1, 512]))
        ps_rep = ps_("ps_rep_a", [128, 1024])
        lnt = make_ln_tiles(ctx, 512, "a")
        t_evac = {0: None, 1: None}
        last_res = None
        for qc in range(8):
            p.wait("sync", last_res)
            for dm in range(8):
                p.dma("ld", xrt_all[:, dm * 512:(dm + 1) * 512],
                      g["XR0"][dm * 128:(dm + 1) * 128, qc * 512:(qc + 1) * 512])
            for dm in range(8):
                bank = psOb[dm % 2]
                p.wait("tensor", t_evac[dm % 2])
                mm = None
                for pr in range(8):
                    mm = p.tensor(lambda e, qc=qc, dm=dm, pr=pr, bank=bank: e.matmul(
                        bank[:], wout_b[:, pr * D + dm * 128: pr * D + (dm + 1) * 128],
                        o_all[:, pr * RPC + qc * 512: pr * RPC + (qc + 1) * 512],
                        start=(pr == 0), stop=(pr == 7)))
                p.wait("vector", mm)
                if dm == 0:
                    p.wait_dma("vector", "ld")
                tadd = p.vector(lambda e, dm=dm, bank=bank: e.tensor_scalar_add(
                    ycol[:, dm * 512:(dm + 1) * 512], bank[:], bo[:, dm:dm + 1]))
                tres = p.vector(lambda e, dm=dm: e.tensor_add(
                    ycol[:, dm * 512:(dm + 1) * 512], ycol[:, dm * 512:(dm + 1) * 512],
                    xrt_all[:, dm * 512:(dm + 1) * 512]))
                t_evac[dm % 2] = tadd
                last_res = tres
            emit_ln(ctx, lnt, ycol, 512,
                    lambda c, ap, qc=qc: p.dma("st", g["xf_A"][c * 128:(c + 1) * 128,
                                                              qc * 512:(qc + 1) * 512], ap),
                    lambda c, ap, qc=qc: p.dma("st", g["xb_A"][c * 128:(c + 1) * 128,
                                                              qc * 512:(qc + 1) * 512], ap),
                    ps_stat, ps_rep)



def emit_phase_b(ctx_outer, src):
    with ctx_outer.scope() as ctx:
        _emit_phase_b(ctx, src)


def _emit_phase_b(ctx, src):
    p, g, nc = ctx.p, ctx.g, ctx.nc
    xf_src, xb_src = g[src[0]], g[src[1]]

    w1t = ctx.sb("w1t", [128, 8 * FF], BF16)
    w2t = ctx.sb("w2t", [128, 32 * D], BF16)
    b1c = bias_chunks(ctx, g["b1"][0:1, :], 32, "b1")
    b2c = bias_chunks(ctx, g["b2"][0:1, :], 8, "b2")
    er1 = ctx.sb("er1", [1, 4 * 1024], BF16)
    oddc1 = ctx.sb("oddc1", [128, 8 * 4], BF16)
    oddc2 = ctx.sb("oddc2", [128, 8 * 4], BF16)
    lsc = ctx.sb("lsc", [1, 1], F32)
    p.dma("ld", lsc[:], g["lscale"][:])

    evs4 = ctx.sb("evs4", [4, 8 * 128], BF16)

    with ctx.scope():
        wrow = ctx.sb("wrow_b", [128, FF])
        wrow_bf = ctx.sb("wrowbf_b", [128, FF], BF16)
        psTb = ctx.psum("psT_b", [128, 1024], BF16)
        tprev = None
        for hc in range(32):
            p.wait("sync", tprev)
            p.dma("ld", wrow[:, 0:D], g["W1"][hc * 128:(hc + 1) * 128, :])
            p.wait_dma("vector", "ld")
            tc = p.vector(lambda e: e.tensor_copy(wrow_bf[:, 0:D], wrow[:, 0:D]))
            tprev = tc
            p.wait("tensor", tc)
            for dc in range(8):
                slot = (dc % 2) * 512
                tt = p.tensor(lambda e, dc=dc, slot=slot: e.transpose(
                    psTb[:, slot:slot + 128], wrow_bf[:, dc * 128:(dc + 1) * 128], ctx.ident_b[:]))
                p.wait("vector", tt)
                tev = p.vector(lambda e, dc=dc, hc=hc, slot=slot: e.tensor_copy(
                    w1t[:, dc * FF + hc * 128: dc * FF + (hc + 1) * 128], psTb[:, slot:slot + 128]))
                p.wait("tensor", tev)
                tprev = tev
        for dc in range(8):
            p.wait("sync", tprev)
            p.dma("ld", wrow[:], g["W2"][dc * 128:(dc + 1) * 128, :])
            p.wait_dma("vector", "ld")
            tc = p.vector(lambda e: e.tensor_copy(wrow_bf[:], wrow[:]))
            tprev = tc
            p.wait("tensor", tc)
            for hc in range(32):
                slot = (hc % 2) * 512
                tt = p.tensor(lambda e, hc=hc, slot=slot: e.transpose(
                    psTb[:, slot:slot + 128], wrow_bf[:, hc * 128:(hc + 1) * 128], ctx.ident_b[:]))
                p.wait("vector", tt)
                tev = p.vector(lambda e, dc=dc, hc=hc, slot=slot: e.tensor_copy(
                    w2t[:, hc * D + dc * 128: hc * D + (dc + 1) * 128], psTb[:, slot:slot + 128]))
                p.wait("tensor", tev)
                tprev = tev

    with ctx.scope():
        ltcols = ctx.sb("ltcols", [128, 16])
        lt_tok = ctx.sb("lt_tok", [16, 1024])
        psLT = ctx.psum("psLT", [128, 128])
        tprev = None
        for dc in range(8):
            p.wait("sync", tprev)
            p.dma("ld", ltcols[:], xf_src[dc * 128:(dc + 1) * 128, 1:17])
            p.wait_dma("tensor", "ld")
            tt = p.tensor(lambda e, dc=dc: e.transpose(psLT[0:16, :], ltcols[:], ctx.ident_f[:]))
            p.wait("vector", tt)
            tev = p.vector(lambda e, dc=dc: e.tensor_copy(lt_tok[:, dc * 128:(dc + 1) * 128],
                                                          psLT[0:16, :]))
            tprev = tev
            p.wait("tensor", tev)
        p.wait("gpsimd", tprev)
        p.dma("cc", g["ag_lt_in"][:], lt_tok[:])
        p.wait_dma("gpsimd", "cc")
        p.cc(lambda e: e.collective_compute(
            "AllGather", ALU.bypass, replica_groups=[list(range(NCORES))],
            ins=[g["ag_lt_in"].opt()], outs=[g["ag_lt_out"].opt()]))
        p.wait_cc("gpsimd")
        agt = ctx.sb("agt", [16, 8 * 1024])
        bselt = ctx.sb("bselt", [16, 8])
        ltmine = ctx.sb("ltmine", [16, 1024])
        p.dma("ld", bselt[:], g["bsel"][:])
        for j in range(NCORES):
            p.dma("cc", agt[:, j * 1024:(j + 1) * 1024], g["ag_lt_out"][j * 16:(j + 1) * 16, :])
        p.wait_dma("vector", "cc")
        p.wait_dma("vector", "ld")
        p.vector(lambda e: e.tensor_scalar_mul(ltmine[:], agt[:, 0:1024], bselt[:, 0:1]))
        tcomb = None
        for j in range(1, NCORES):
            tcomb = p.vector(lambda e, j=j: e.scalar_tensor_tensor(
                ltmine[:], agt[:, j * 1024:(j + 1) * 1024], bselt[:, j:j + 1], ltmine[:],
                ALU.mult, ALU.add))
        p.wait("gpsimd", tcomb)
        p.dma("st", g["lt_s"][:], ltmine[:])
        ltfeat = ctx.sb("ltfeat", [128, 8 * 16])
        p.wait("tensor", tcomb)
        tprev = None
        for dc in range(8):
            tt = p.tensor(lambda e, dc=dc: e.transpose(
                psLT[:, 0:16], ltmine[:, dc * 128:(dc + 1) * 128], ctx.ident_f[0:16, 0:16]))
            p.wait("vector", tt)
            tev = p.vector(lambda e, dc=dc: e.tensor_copy(ltfeat[:, dc * 16:(dc + 1) * 16],
                                                          psLT[:, 0:16]))
            p.wait("tensor", tev)
            tprev = tev
        if _os.environ.get("KERNEL_DEBUG_B") == "1":
            dbg_lf = nc.dram_tensor("dbg_lf", [128, 128], F32, kind="ExternalOutput").ap()
            p.barrier()
            p.dma("st", dbg_lf[:], ltfeat[:])
            p.barrier()
        for dc in range(8):
            p.vector(lambda e, dc=dc: e.tensor_copy(
                oddc1[:, dc * 4:(dc + 1) * 4],
                ltfeat.rearrange("p (c s) -> p c s", s=16)[:, dc, 1:9:2]))
            p.vector(lambda e, dc=dc: e.tensor_copy(
                oddc2[:, dc * 4:(dc + 1) * 4],
                ltfeat.rearrange("p (c s) -> p c s", s=16)[:, dc, 9:16:2]))
        erow = ctx.sb("erow", [1, 1024])
        er2f = ctx.sb("er2f", [1, 1024])
        evrow = ctx.sb("evrow", [1, 256])
        p.wait_dma("sync", "st")
        p.wait_dma("vector", "ld")
        tprevdma = None
        for j in range(4):
            p.wait("sync", tprevdma)
            p.dma("ld", erow[:], g["lt_s"][2 * j:2 * j + 1, :])
            p.wait_dma("vector", "ld")
            t1 = p.vector(lambda e, j=j: e.tensor_scalar_mul(
                er1[0:1, j * 1024:(j + 1) * 1024], erow[:], lsc[0:1, 0:1]))
            tprevdma = t1
        for j in range(4):
            p.wait("sync", tprevdma)
            p.dma("ld", erow[:], g["lt_s"][8 + 2 * j:9 + 2 * j, :])
            p.wait_dma("vector", "ld")
            t2 = p.vector(lambda e: e.tensor_scalar_mul(er2f[:], erow[:], lsc[0:1, 0:1]))
            p.wait("gpsimd", t2)
            for r in range(4):
                tv = p.vector(lambda e, r=r: e.tensor_copy(evrow[:], er2f[0:1, r:1024:4]))
                p.wait("gpsimd", tv)
                p.dma("st", g["evs_s"][4 * j + r:4 * j + r + 1, :], evrow[:])
                p.wait_dma("vector", "st")
            tprevdma = t2
        evs4f = ctx.sb("evs4f", [4, 8 * 128])
        p.wait_dma("sync", "st")
        for j in range(4):
            for half in range(2):
                p.dma("ld", evs4f[:, (j * 2 + half) * 128:(j * 2 + half + 1) * 128],
                      g["evs_s"].rearrange("a (h b) -> a h b", h=2)[4 * j:4 * j + 4, half, :])
        p.wait_dma("vector", "ld")
        tcv = p.vector(lambda e: e.tensor_copy(evs4[:], evs4f[:]))
        p.wait("tensor", tcv)

    if _os.environ.get("KERNEL_DEBUG_B") == "1":
        dbg_oc = nc.dram_tensor("dbg_oc", [128, 32], BF16, kind="ExternalOutput").ap()
        dbg_oc2 = nc.dram_tensor("dbg_oc2", [128, 32], BF16, kind="ExternalOutput").ap()
        p.barrier()
        p.dma("st", dbg_oc[:], oddc1[:])
        p.dma("st", dbg_oc2[:], oddc2[:])
        dbg_e4 = nc.dram_tensor("dbg_e4", [4, 1024], BF16, kind="ExternalOutput").ap()
        p.dma("st", dbg_e4[:], evs4[:])
        p.barrier()
    TBW = 256
    NTB = RPC // TBW
    with ctx.scope():
        xcolb2 = [ctx.sb("xcolb_b0", [128, 8 * TBW], BF16),
                  ctx.sb("xcolb_b1", [128, 8 * TBW], BF16)]
        xrt = ctx.sb("xrt_b", [128, TBW])
        htile = ctx.sb("htile_b", [128, 32 * TBW], BF16)
        ab = ctx.sb("ab_b", [1, 4 * TBW], BF16)
        c4 = ctx.sb("c4_b", [4, 4 * TBW], BF16)
        crow = ctx.sb("crow_b", [4, TBW], BF16)
        ycol = ctx.sb("ycol_b", [128, 8 * TBW])
        lnt = make_ln_tiles(ctx, TBW, "b")
        psAa = ctx.psum("psAa", [1, TBW])
        bankA = ctx.psum("bankA_b", [128, TBW])
        bankB = ctx.psum("bankB_b", [128, TBW])
        psC = ctx.psum("psC_b", [4, TBW])
        ps_stat = (ctx.psum("ps_stat_b_s", [1, TBW]), ctx.psum("ps_stat_b_q", [1, TBW]))
        ps_rep = ctx.psum("ps_rep_b", [128, 2 * TBW])

        tlast_res = None
        pending_ln = None
        prev_tbh = {}
        buf_use = {0: None, 1: None}
        for tb in range(NTB):
            cols = (tb * TBW, (tb + 1) * TBW)
            xcolb = xcolb2[tb % 2]
            if tb == 0:
                for dc in range(8):
                    p.dma("ld", xcolb[:, dc * TBW:(dc + 1) * TBW],
                          xb_src[dc * 128:(dc + 1) * 128, cols[0]:cols[1]])
            p.wait_dma("tensor", "ld")
            for j in range(4):
                mm = None
                for dc in range(8):
                    mm = p.tensor(lambda e, j=j, dc=dc, xcolb=xcolb: e.matmul(
                        psAa[:], oddc1[:, dc * 4 + j: dc * 4 + j + 1],
                        xcolb[:, dc * TBW:(dc + 1) * TBW],
                        start=(dc == 0), stop=(dc == 7)))
                p.wait("vector", mm)
                tev = p.vector(lambda e, j=j: e.tensor_copy(ab[0:1, j * TBW:(j + 1) * TBW], psAa[:]))
                p.wait("tensor", tev)
            if tb + 1 < NTB:
                p.wait("sync", buf_use[(tb + 1) % 2])
                for dc in range(8):
                    p.dma("ld", xcolb2[(tb + 1) % 2][:, dc * TBW:(dc + 1) * TBW],
                          xb_src[dc * 128:(dc + 1) * 128, (tb + 1) * TBW:(tb + 2) * TBW])
            tg_hist = {}
            tev_c = None
            for hc in range(32):
                bank = bankA if hc % 2 == 0 else bankB
                j = hc // 8
                if hc >= 2:
                    p.wait("tensor", tg_hist[hc - 2])
                else:
                    p.wait("tensor", prev_tbh.get(6 + hc))
                for dc in range(8):
                    p.tensor(lambda e, hc=hc, dc=dc, bank=bank, xcolb=xcolb: e.matmul(
                        bank[:], w1t[:, dc * FF + hc * 128: dc * FF + (hc + 1) * 128],
                        xcolb[:, dc * TBW:(dc + 1) * TBW],
                        start=(dc == 0), stop=False))
                mm = p.tensor(lambda e, hc=hc, j=j, bank=bank: e.matmul(
                    bank[:], er1[0:1, j * 1024 + (hc % 8) * 128: j * 1024 + ((hc % 8) + 1) * 128],
                    ab[0:1, j * TBW:(j + 1) * TBW],
                    start=False, stop=True))
                p.wait("scalar", mm)
                tg = p.scalar(lambda e, hc=hc, bank=bank: e.activation(
                    htile[:, hc * TBW:(hc + 1) * TBW], bank[:], AF.Gelu,
                    bias=b1c[:, hc:hc + 1], scale=1.0))
                tg_hist[hc] = tg
                if hc % 8 == 7:
                    r = hc // 8
                    p.wait("tensor", tg)
                    p.wait("tensor", tev_c)
                    mmc = None
                    for qc in range(8):
                        mmc = p.tensor(lambda e, r=r, qc=qc: e.matmul(
                            psC[:], oddc2[:, qc * 4:(qc + 1) * 4],
                            htile[:, (r * 8 + qc) * TBW:(r * 8 + qc + 1) * TBW],
                            start=(qc == 0), stop=(qc == 7)))
                    p.wait("vector", mmc)
                    p.wait_dma("vector", "st")
                    tev_c = p.vector(lambda e, r=r: e.tensor_copy(crow[:], psC[:]))
                    p.wait("gpsimd", tev_c)
                    p.dma("st", g["cscr"][r, :, :], crow[:])
            buf_use[tb % 2] = mm
            p.wait_dma("scalar", "st")
            for j in range(4):
                p.dma("ld2", c4[:, j * TBW:(j + 1) * TBW], g["cscr"][:, j, :])
            if pending_ln is not None:
                pending_ln()
                pending_ln = None
            tbh = {}
            for oc in range(8):
                bank = bankA if oc % 2 == 0 else bankB
                j, half = oc // 2, oc % 2
                if oc >= 2:
                    p.wait("tensor", tbh[oc - 2])
                for hc in range(32):
                    p.tensor(lambda e, oc=oc, hc=hc, bank=bank: e.matmul(
                        bank[:], w2t[:, hc * D + oc * 128: hc * D + (oc + 1) * 128],
                        htile[:, hc * TBW:(hc + 1) * TBW],
                        start=(hc == 0), stop=False))
                if oc == 0:
                    p.wait_dma("tensor", "ld2")
                mm = p.tensor(lambda e, oc=oc, j=j, half=half, bank=bank: e.matmul(
                    bank[:], evs4[:, (j * 2 + half) * 128:(j * 2 + half + 1) * 128],
                    c4[:, j * TBW:(j + 1) * TBW],
                    start=False, stop=True))
                if oc > 0 or tb > 0:
                    p.wait("scalar", tlast_res)
                p.dma("ld2", xrt[:], xf_src[oc * 128:(oc + 1) * 128, cols[0]:cols[1]])
                p.wait("vector", mm)
                tb_ = p.vector(lambda e, oc=oc, bank=bank: e.tensor_scalar_add(
                    ycol[:, oc * TBW:(oc + 1) * TBW], bank[:], b2c[:, oc:oc + 1]))
                p.wait_dma("vector", "ld2")
                tres = p.vector(lambda e, oc=oc: e.tensor_add(
                    ycol[:, oc * TBW:(oc + 1) * TBW], ycol[:, oc * TBW:(oc + 1) * TBW],
                    xrt[:]))
                tbh[oc] = tb_
                tlast_use = mm
                tlast_res = tres
            prev_tbh = tbh
            if tb == 0 and _os.environ.get("KERNEL_DEBUG_B") == "1":
                dbg_h = nc.dram_tensor("dbg_h", [128, 32 * TBW], BF16, kind="ExternalOutput").ap()
                dbg_ab = nc.dram_tensor("dbg_ab", [1, 4 * TBW], BF16, kind="ExternalOutput").ap()
                dbg_y = nc.dram_tensor("dbg_y", [128, 8 * TBW], F32, kind="ExternalOutput").ap()
                p.barrier()
                p.dma("st", dbg_h[:], htile[:])
                p.dma("st", dbg_ab[:], ab[:])
                p.dma("st", dbg_y[:], ycol[:])
                p.barrier()
            def _mk_ln(tb=tb):
                def _f():
                    emit_ln(ctx, lnt, ycol, TBW,
                            lambda c, ap, tb=tb: p.dma("st", g["xf_B"][c * 128:(c + 1) * 128,
                                                                       tb * TBW:(tb + 1) * TBW], ap),
                            lambda c, ap, tb=tb: p.dma("st", g["xb_B"][c * 128:(c + 1) * 128,
                                                                       tb * TBW:(tb + 1) * TBW], ap),
                            ps_stat, ps_rep)
                return _f
            pending_ln = _mk_ln()
        pending_ln()

def emit_phase_c(ctx_outer, src):
    with ctx_outer.scope() as ctx:
        _emit_phase_c(ctx, src)


def _emit_phase_c(ctx, src):
    p, g, nc = ctx.p, ctx.g, ctx.nc
    xf_src, xb_src = g[src[0]], g[src[1]]
    TOT = float(2048 * 1024)

    w0b = ctx.sb("w0b", [128, 8 * FF], BF16)
    w1b = ctx.sb("w1b", [128, 32 * D], BF16)
    Dsc = ctx.sb("Dsc", [1, 1])
    Dvec = ctx.sb("Dvec", [128, 1])
    D2vec = ctx.sb("D2vec", [128, 1])
    with ctx.scope():
        wtmp = ctx.sb("wtmp_c", [128, FF])
        wbtmp = ctx.sb("wbtmp_c", [128, FF], BF16)
        tprev = None
        for dc in range(8):
            p.wait("sync", tprev)
            p.dma("ld", wtmp[:], g["w0"][dc * 128:(dc + 1) * 128, :])
            p.wait_dma("vector", "ld")
            tprev = p.vector(lambda e, dc=dc: e.tensor_copy(w0b[:, dc * FF:(dc + 1) * FF], wtmp[:]))
        for fc in range(32):
            p.wait("sync", tprev)
            p.dma("ld", wtmp[:, 0:D], g["w1f"][fc * 128:(fc + 1) * 128, :])
            p.wait_dma("vector", "ld")
            tprev = p.vector(lambda e, fc=fc: e.tensor_copy(w1b[:, fc * D:(fc + 1) * D], wtmp[:, 0:D]))
        for dc in range(8):
            p.wait("sync", tprev)
            p.dma("ld", wtmp[:], g["w2"][dc * 128:(dc + 1) * 128, :])
            p.wait_dma("vector", "ld")
            tc = p.vector(lambda e: e.tensor_copy(wbtmp[:], wtmp[:]))
            tprev = tc
            p.wait("gpsimd", tc)
            p.dma("st", g["w2b_s"][dc * 128:(dc + 1) * 128, :], wbtmp[:])
            p.wait_dma("vector", "st")
        p.vector(lambda e: e.memset(Dsc[:], 1.0))
        p.vector(lambda e: e.memset(Dvec[:], 1.0))
        p.vector(lambda e: e.memset(D2vec[:], 1.0))
        zrow = ctx.sb("zrow_c", [1, 8])
        tzz = p.vector(lambda e: e.memset(zrow[:], 0.0))
        p.wait("gpsimd", tzz)
        p.dma("cc", g["cc_in"][:], zrow[:])
        p.wait_dma("gpsimd", "cc")

    with ctx.scope():
        TBW = 256
        xcb = ctx.sb("xcb_c", [128, 8 * TBW], BF16)
        xcf = ctx.sb("xcf_c", [128, 8 * TBW])
        w2fc = [ctx.sb("w2fc0", [128, 8 * 128], BF16), ctx.sb("w2fc1", [128, 8 * 128], BF16)]
        htile = ctx.sb("htile_c", [128, 32 * TBW], BF16)
        stile = ctx.sb("stile_c", [128, TBW])
        vtile = ctx.sb("vtile_c", [128, TBW])
        ocol = ctx.sb("ocol_c", [128, 8 * TBW])
        dtile = ctx.sb("dtile_c", [128, TBW])
        lrow = ctx.sb("lrow_c", [1, TBW])
        ls = ctx.sb("ls_c", [1, 1])
        lg = ctx.sb("lg_c", [1, 1])
        dec = ctx.sb("dec_c", [1, 1])
        lnt = make_ln_tiles(ctx, TBW, "c")
        psU = ctx.psum("psU_c", [128, TBW])
        psV2 = ctx.psum("psV_c", [128, TBW])
        psO2 = [ctx.psum("psO2a_c", [128, TBW]), ctx.psum("psO2b_c", [128, TBW])]
        psM = ctx.psum("psM_c", [128, TBW])
        ps_stat = (ctx.psum("ps_st_c_s", [1, TBW]), ctx.psum("ps_st_c_q", [1, TBW]))
        ps_rep = ctx.psum("ps_rep_c", [128, 2 * TBW])

        t_u_evac = None
        t_v_evac = None
        t_x_use = None
        t_res = None
        last_th = None
        t_w2_use = {0: None, 1: None}
        for c in range(NCHUNK):
            cols = (c * TBW, (c + 1) * TBW)
            p.wait("sync", t_x_use)
            p.wait("sync", t_res)
            for dc in range(8):
                p.dma("ld", xcb[:, dc * TBW:(dc + 1) * TBW],
                      xb_src[dc * 128:(dc + 1) * 128, cols[0]:cols[1]])
                p.dma("ld", xcf[:, dc * TBW:(dc + 1) * TBW],
                      xf_src[dc * 128:(dc + 1) * 128, cols[0]:cols[1]])
            p.wait_dma("tensor", "ld")
            p.wait_dma("vector", "ld")
            for fc in range(32):
                buf = fc % 2
                p.wait("scalar", t_w2_use[buf])
                p.wait("gpsimd", t_w2_use[buf])
                p.dma("ld2", w2fc[buf][:, 0:512],
                      g["w2b_s"].rearrange("(cc pp) f -> pp cc f", pp=128)[:, 0:4, fc * 128:(fc + 1) * 128])
                p.dma("ld3", w2fc[buf][:, 512:1024],
                      g["w2b_s"].rearrange("(cc pp) f -> pp cc f", pp=128)[:, 4:8, fc * 128:(fc + 1) * 128])
                mmu = None
                p.wait("tensor", t_u_evac)
                for dc in range(8):
                    mmu = p.tensor(lambda e, fc=fc, dc=dc: e.matmul(
                        psU[:], w0b[:, dc * FF + fc * 128: dc * FF + (fc + 1) * 128],
                        xcb[:, dc * TBW:(dc + 1) * TBW],
                        start=(dc == 0), stop=(dc == 7)))
                p.wait_dma("tensor", "ld2")
                p.wait_dma("tensor", "ld3")
                mmv = None
                p.wait("tensor", t_v_evac)
                for dc in range(8):
                    mmv = p.tensor(lambda e, fc=fc, dc=dc, buf=buf: e.matmul(
                        psV2[:], w2fc[buf][:, dc * 128:(dc + 1) * 128],
                        xcb[:, dc * TBW:(dc + 1) * TBW],
                        start=(dc == 0), stop=(dc == 7)))
                t_w2_use[buf] = mmv
                t_x_use = mmv
                p.wait("scalar", mmu)
                p.wait("scalar", last_th)
                tsl = p.scalar(lambda e: e.activation(stile[:], psU[:], AF.Silu,
                                                      bias=ctx.zero128[:], scale=Dvec[:]))
                t_u_evac = tsl
                p.wait("vector", mmv)
                tvc = p.vector(lambda e: e.tensor_copy(vtile[:], psV2[:]))
                t_v_evac = tvc
                p.wait("vector", tsl)
                th = p.vector(lambda e, fc=fc: e.tensor_mul(
                    htile[:, fc * TBW:(fc + 1) * TBW], stile[:], vtile[:]))
                last_th = th
            p.wait("tensor", last_th)
            t_o_evac = {0: None, 1: None}
            for dm in range(8):
                bank = psO2[dm % 2]
                mm = None
                p.wait("tensor", t_o_evac[dm % 2])
                for fc in range(32):
                    mm = p.tensor(lambda e, dm=dm, fc=fc, bank=bank: e.matmul(
                        bank[:], w1b[:, fc * D + dm * 128: fc * D + (dm + 1) * 128],
                        htile[:, fc * TBW:(fc + 1) * TBW],
                        start=(fc == 0), stop=(fc == 31)))
                p.wait("vector", mm)
                tev = p.vector(lambda e, dm=dm, bank=bank: e.tensor_scalar(
                    ocol[:, dm * TBW:(dm + 1) * TBW], bank[:], D2vec[:], None, ALU.mult))
                t_o_evac[dm % 2] = tev
            mml = None
            for dm in range(8):
                tsub = p.vector(lambda e, dm=dm: e.tensor_sub(
                    dtile[:], ocol[:, dm * TBW:(dm + 1) * TBW], xcf[:, dm * TBW:(dm + 1) * TBW]))
                tsq = p.vector(lambda e: e.tensor_mul(dtile[:], dtile[:], dtile[:]))
                p.wait("tensor", tsq)
                mml = p.tensor(lambda e, dm=dm: e.matmul(
                    psM[0:1, 0:TBW], ctx.ones_col[:], dtile[:],
                    start=(dm == 0), stop=(dm == 7)))
                if dm < 7:
                    p.wait("vector", mml)
            p.wait("vector", mml)
            tlr = p.vector(lambda e: e.tensor_copy(lrow[:], psM[0:1, 0:TBW]))
            p.wait("tensor", tlr)
            tls = p.vector(lambda e: e.reduce_sum(ls[:], lrow[:], AX.X))
            p.wait("gpsimd", tls)
            p.dma("cc", g["cc_in"][0:1, 0:1], ls[:])
            p.wait_dma("gpsimd", "cc")
            p.cc(lambda e: e.collective_compute(
                "AllReduce", ALU.add, replica_groups=[list(range(NCORES))],
                ins=[g["cc_in"].opt()], outs=[g["cc_out"].opt()]))
            p.wait_cc("gpsimd")
            p.dma("cc", lg[:], g["cc_out"][0:1, 0:1])
            for dm in range(8):
                t_res = p.vector(lambda e, dm=dm: e.tensor_add(
                    ocol[:, dm * TBW:(dm + 1) * TBW], ocol[:, dm * TBW:(dm + 1) * TBW],
                    xcf[:, dm * TBW:(dm + 1) * TBW]))
            emit_ln(ctx, lnt, ocol, TBW,
                    lambda ch, ap, c=c: p.dma("st2", g["xf_C"][ch * 128:(ch + 1) * 128,
                                                               c * TBW:(c + 1) * TBW], ap),
                    lambda ch, ap, c=c: p.dma("st2", g["xb_C"][ch * 128:(ch + 1) * 128,
                                                               c * TBW:(c + 1) * TBW], ap),
                    ps_stat, ps_rep, st_stream="st2")
            p.wait_dma("vector", "cc")
            if _os.environ.get("KERNEL_DEBUG_C") == "1":
                if "dbg_loss" not in g:
                    g["dbg_loss"] = nc.dram_tensor("dbg_loss", [1, NCHUNK], F32,
                                                   kind="ExternalOutput").ap()
                    g["dbg_o"] = nc.dram_tensor("dbg_o", [128, 8 * 256], F32,
                                                kind="ExternalOutput").ap()
                p.wait_dma("gpsimd", "cc")
                p.dma("st", g["dbg_loss"][0:1, c:c + 1], lg[:])
            tdec = p.vector(lambda e: e.tensor_scalar(dec[:], lg[:], -LR / TOT, 1.0,
                                                      ALU.mult, ALU.add))
            p.wait("vector", t_u_evac)
            tds = p.vector(lambda e: e.tensor_mul(Dsc[:], Dsc[:], dec[:]))
            p.wait("tensor", tds)
            mmd = p.tensor(lambda e: e.matmul(psM[:, 0:1], ctx.ones_row[:], Dsc[:],
                                              start=True, stop=True))
            p.wait("vector", mmd)
            tdv = p.vector(lambda e: e.tensor_copy(Dvec[:], psM[:, 0:1]))
            p.wait("tensor", tdv)
            td2 = p.vector(lambda e: e.tensor_mul(D2vec[:], Dvec[:], Dvec[:]))
            p.wait("scalar", td2)
            if _os.environ.get("KERNEL_DEBUG_C") == "1" and c == NCHUNK - 1:
                p.barrier()
                p.dma("st", g["dbg_o"][:], ocol[:])
                p.barrier()



def emit_phase_d(ctx_outer, src):
    with ctx_outer.scope() as ctx:
        _emit_phase_d(ctx, src)


def _emit_phase_d(ctx, src):
    p, g, nc = ctx.p, ctx.g, ctx.nc
    xf_src = g[src[0]]

    wrt = ctx.sb("wrt", [128, 8 * D], BF16)
    wkt = ctx.sb("wkt", [128, 8 * D], BF16)
    wvt = ctx.sb("wvt", [128, 8 * D], BF16)
    murc = bias_chunks(ctx, g["mu_r"][0:1, :], 8, "mur")
    mukc = bias_chunks(ctx, g["mu_k"][0:1, :], 8, "muk")
    mur1c = ctx.sb("mur1c", [128, 8])
    muk1c = ctx.sb("muk1c", [128, 8])
    bd_use = ctx.sb("bd_use", [128, 128])
    p.vector(lambda e: e.tensor_scalar(mur1c[:], murc[:], -1.0, 1.0, ALU.mult, ALU.add))
    p.vector(lambda e: e.tensor_scalar(muk1c[:], mukc[:], -1.0, 1.0, ALU.mult, ALU.add))

    with ctx.scope():
        wrow = ctx.sb("wrow_d", [128, D])
        wrow_bf = ctx.sb("wrowbf_d", [128, D], BF16)
        psTd0 = ctx.psum("psTd0", [128, 1024], BF16)
        tprev = None
        for wname, wt in (("W_r", wrt), ("W_k", wkt), ("W_v", wvt)):
            for oc in range(8):
                p.wait("sync", tprev)
                p.dma("ld", wrow[:], g[wname][oc * 128:(oc + 1) * 128, :])
                p.wait_dma("vector", "ld")
                tc = p.vector(lambda e: e.tensor_copy(wrow_bf[:], wrow[:]))
                tprev = tc
                p.wait("tensor", tc)
                for cc in range(8):
                    slot = (cc % 2) * 512
                    tt = p.tensor(lambda e, cc=cc, slot=slot: e.transpose(
                        psTd0[:, slot:slot + 128], wrow_bf[:, cc * 128:(cc + 1) * 128],
                        ctx.ident_b[:]))
                    p.wait("vector", tt)
                    tev = p.vector(lambda e, cc=cc, oc=oc, wt=wt, slot=slot: e.tensor_copy(
                        wt[:, cc * D + oc * 128: cc * D + (oc + 1) * 128],
                        psTd0[:, slot:slot + 128]))
                    p.wait("tensor", tev)
                    tprev = tev

    with ctx.scope():
        bd = ctx.sb("bd", [128, 128])
        agt = ctx.sb("agt_d", [128, 8 * 128])
        pflagt = ctx.sb("pflagt", [128, 2])
        pbd1 = ctx.sb("pbd1", [128, 128])
        for dc in range(8):
            p.dma("ld", bd[:, dc * 16:(dc + 1) * 16],
                  xf_src[dc * 128:(dc + 1) * 128, 255:RPC:256],
                  allow_slow_non_contiguous=True)
        p.dma("ld", pflagt[:], g["pflag"][:])
        p.wait_dma("gpsimd", "ld")
        p.dma("cc", g["ag_bd_in"][:], bd[:])
        p.wait_dma("gpsimd", "cc")
        p.cc(lambda e: e.collective_compute(
            "AllGather", ALU.bypass, replica_groups=[list(range(NCORES))],
            ins=[g["ag_bd_in"].opt()], outs=[g["ag_bd_out"].opt()]))
        p.wait_cc("gpsimd")
        for j in range(NCORES):
            p.dma("cc", agt[:, j * 128:(j + 1) * 128], g["ag_bd_out"][j * 128:(j + 1) * 128, :])
        p.wait_dma("vector", "cc")
        tm0 = p.vector(lambda e: e.tensor_scalar_mul(bd_use[:], agt[:, 0:128], ctx.psel128[:, 0:1]))
        tcomb = tm0
        for j in range(1, NCORES):
            tcomb = p.vector(lambda e, j=j: e.scalar_tensor_tensor(
                bd_use[:], agt[:, j * 128:(j + 1) * 128], ctx.psel128[:, j:j + 1], bd_use[:],
                ALU.mult, ALU.add))
        for dc in range(8):
            p.vector(lambda e, dc=dc: e.memset(pbd1[:, dc * 16:dc * 16 + 1], 0.0))
            p.vector(lambda e, dc=dc: e.tensor_copy(
                pbd1[:, dc * 16 + 1:(dc + 1) * 16], bd_use[:, dc * 16:(dc + 1) * 16 - 1]))
        p.vector(lambda e: e.tensor_scalar_mul(bd_use[:], bd_use[:], pflagt[:, 0:1]))
        tbd = p.vector(lambda e: e.scalar_tensor_tensor(
            bd_use[:], pbd1[:], pflagt[:, 1:2], bd_use[:], ALU.mult, ALU.add))
        p.wait("tensor", tbd)
        p.wait("sync", tbd)

    with ctx.scope():
        TBW = 256
        xseg2 = [ctx.sb("xseg_d0", [128, 8 * 257]), ctx.sb("xseg_d1", [128, 8 * 257])]
        xr_b = ctx.sb("xr_d", [128, 8 * TBW], BF16)
        xk_b = ctx.sb("xk_d", [128, 8 * TBW], BF16)
        rsig = ctx.sb("rsig_d", [128, 8 * TBW])
        kk = ctx.sb("kk_d", [128, TBW])
        kact = ctx.sb("kact_d", [128, 8 * TBW], BF16)
        ycol = ctx.sb("ycol_d", [128, 8 * TBW])
        otok = ctx.sb("otok_d", [128, 8 * 128])
        tmpd = ctx.sb("tmp_d", [128, TBW])
        lnt = make_ln_tiles(ctx, TBW, "d")
        bankA = ctx.psum("bankA_d", [128, TBW])
        bankB = ctx.psum("bankB_d", [128, TBW])
        psTd = ctx.psum("psTd", [128, 1024])
        ps_stat = (ctx.psum("ps_st_d_s", [1, TBW]), ctx.psum("ps_st_d_q", [1, TBW]))
        ps_rep = ctx.psum("ps_rep_d", [128, 2 * TBW])

        def emit_loadmix(sg, wait_toks):
            xseg = xseg2[sg % 2]
            for dc in range(8):
                p.dma("ld", xseg[:, dc * 257 + 1:(dc + 1) * 257],
                      xf_src[dc * 128:(dc + 1) * 128, sg * TBW:(sg + 1) * TBW])
            p.wait_dma("vector", "ld")
            p.vector(lambda e, sg=sg, xseg=xseg: e.tensor_copy(
                xseg.rearrange("p (c s) -> p c s", s=257)[:, :, 0:1],
                bd_use.rearrange("p (c s) -> p c s", s=16)[:, :, sg:sg + 1]))
            for tok in wait_toks:
                p.wait("vector", tok)
            tmix = None
            for dc in range(8):
                p.vector(lambda e, dc=dc, xseg=xseg: e.tensor_scalar_mul(
                    tmpd[:], xseg[:, dc * 257 + 1:(dc + 1) * 257], murc[:, dc:dc + 1]))
                p.vector(lambda e, dc=dc, xseg=xseg: e.scalar_tensor_tensor(
                    xr_b[:, dc * TBW:(dc + 1) * TBW], xseg[:, dc * 257:(dc + 1) * 257 - 1],
                    mur1c[:, dc:dc + 1], tmpd[:], ALU.mult, ALU.add))
                p.vector(lambda e, dc=dc, xseg=xseg: e.tensor_scalar_mul(
                    tmpd[:], xseg[:, dc * 257 + 1:(dc + 1) * 257], mukc[:, dc:dc + 1]))
                tmix = p.vector(lambda e, dc=dc, xseg=xseg: e.scalar_tensor_tensor(
                    xk_b[:, dc * TBW:(dc + 1) * TBW], xseg[:, dc * 257:(dc + 1) * 257 - 1],
                    muk1c[:, dc:dc + 1], tmpd[:], ALU.mult, ALU.add))
            return tmix

        res_buf = {0: None, 1: None}
        tmix_next = None
        for sg in range(NCHUNK):
            xseg = xseg2[sg % 2]
            tmix = emit_loadmix(0, ()) if sg == 0 else tmix_next
            p.wait("tensor", tmix)
            t_ev = {0: None, 1: None}
            for dd in range(8):
                bank = bankA if dd % 2 == 0 else bankB
                p.wait("tensor", t_ev[dd % 2])
                mm = None
                for cc in range(8):
                    mm = p.tensor(lambda e, dd=dd, cc=cc, bank=bank: e.matmul(
                        bank[:], wrt[:, cc * D + dd * 128: cc * D + (dd + 1) * 128],
                        xr_b[:, cc * TBW:(cc + 1) * TBW],
                        start=(cc == 0), stop=(cc == 7)))
                p.wait("scalar", mm)
                tsg = p.scalar(lambda e, dd=dd, bank=bank: e.activation(
                    rsig[:, dd * TBW:(dd + 1) * TBW], bank[:], AF.Sigmoid,
                    bias=ctx.zero128[:], scale=1.0))
                t_ev[dd % 2] = tsg
            mm_r_last = mm
            for dd in range(8):
                bank = bankA if dd % 2 == 0 else bankB
                p.wait("tensor", t_ev[dd % 2])
                mm = None
                for cc in range(8):
                    mm = p.tensor(lambda e, dd=dd, cc=cc, bank=bank: e.matmul(
                        bank[:], wkt[:, cc * D + dd * 128: cc * D + (dd + 1) * 128],
                        xk_b[:, cc * TBW:(cc + 1) * TBW],
                        start=(cc == 0), stop=(cc == 7)))
                p.wait("scalar", mm)
                trl = p.scalar(lambda e, dd=dd, bank=bank: e.activation(
                    kk[:], bank[:], AF.Relu, bias=ctx.zero128[:], scale=1.0))
                p.wait("vector", trl)
                tk = p.vector(lambda e, dd=dd: e.tensor_mul(
                    kact[:, dd * TBW:(dd + 1) * TBW], kk[:], kk[:]))
                p.wait("scalar", tk)
                t_ev[dd % 2] = tk
            mm_k_last = mm
            if sg + 1 < NCHUNK:
                p.wait("sync", res_buf[(sg + 1) % 2])
                tmix_next = emit_loadmix(sg + 1, (mm_r_last, mm_k_last))
            p.wait("tensor", tk)
            t_evv = {0: None, 1: None}
            for dd in range(8):
                bank = bankA if dd % 2 == 0 else bankB
                p.wait("tensor", t_evv[dd % 2])
                mm = None
                for cc in range(8):
                    mm = p.tensor(lambda e, dd=dd, cc=cc, bank=bank: e.matmul(
                        bank[:], wvt[:, cc * D + dd * 128: cc * D + (dd + 1) * 128],
                        kact[:, cc * TBW:(cc + 1) * TBW],
                        start=(cc == 0), stop=(cc == 7)))
                p.wait("vector", mm)
                tmul = p.vector(lambda e, dd=dd, bank=bank: e.tensor_tensor(
                    ycol[:, dd * TBW:(dd + 1) * TBW], rsig[:, dd * TBW:(dd + 1) * TBW],
                    bank[:], ALU.mult))
                tres = p.vector(lambda e, dd=dd, xseg=xseg: e.tensor_add(
                    ycol[:, dd * TBW:(dd + 1) * TBW], ycol[:, dd * TBW:(dd + 1) * TBW],
                    xseg[:, dd * 257 + 1:(dd + 1) * 257]))
                t_evv[dd % 2] = tres
                res_buf[sg % 2] = tres
            tln = emit_ln(ctx, lnt, ycol, TBW,
                          lambda ch, ap: None, None, ps_stat, ps_rep)
            p.wait("tensor", tln)
            for t in range(2):
                ttr = None
                for dc in range(8):
                    ttr = p.tensor(lambda e, t=t, dc=dc: e.transpose(
                        psTd[:, dc * 128:(dc + 1) * 128],
                        ycol[:, dc * TBW + t * 128: dc * TBW + (t + 1) * 128],
                        ctx.ident_f[:]))
                p.wait("vector", ttr)
                p.wait_dma("vector", "st")
                tev = p.vector(lambda e: e.tensor_copy(otok[:], psTd[:]))
                p.wait("tensor", tev)
                p.wait("gpsimd", tev)
                p.dma("st", g["out_tok"][sg * TBW + t * 128: sg * TBW + (t + 1) * 128, :],
                      otok[:])

def build_program(phases=("A", "B", "C", "D"), debug_outputs=()):
    nc = bass.Bass("TRN2", target_bir_lowering=False)
    p = Prog(nc)
    es = ExitStack()
    ctx = Ctx(nc, p, es)
    g = ctx.g

    def dram(name, shape, dt=F32, kind="Internal"):
        if name in debug_outputs and kind == "Internal":
            kind = "ExternalOutput"
        g[name] = nc.dram_tensor(name, shape, dt, kind=kind).ap()
        return g[name]

    first, last = phases[0], phases[-1]

    if "A" in phases:
        dram("xb", [S, D], F32, kind="ExternalInput")
        dram("w_qkv", [D, H3], F32, kind="ExternalInput")
        dram("b_qkv", [1, H3], F32, kind="ExternalInput")
        dram("w_out", [D, D], F32, kind="ExternalInput")
        dram("b_out", [1, D], F32, kind="ExternalInput")
        dram("cos_q", [128, RPC], F32, kind="ExternalInput")
        dram("sin_q", [128, RPC], F32, kind="ExternalInput")
        dram("cos_k", [128, S], F32, kind="ExternalInput")
        dram("sin_k", [128, S], F32, kind="ExternalInput")
        dram("QT_s", [D, RPC], BF16)
        dram("KT_s", [D, S], BF16)
        dram("V_s", [S, D], BF16)
        dram("XR0", [D, RPC], F32)
    dram("ident_bf", [128, 128], BF16, kind="ExternalInput")
    dram("ident_f32", [128, 128], F32, kind="ExternalInput")
    if "B" in phases:
        dram("W1", [FF, D], F32, kind="ExternalInput")
        dram("b1", [1, FF], F32, kind="ExternalInput")
        dram("W2", [D, FF], F32, kind="ExternalInput")
        dram("b2", [1, D], F32, kind="ExternalInput")
        dram("lscale", [1, 1], F32, kind="ExternalInput")
        dram("bsel", [16, 8], F32, kind="ExternalInput")
        dram("ag_lt_in", [16, D], F32)
        dram("ag_lt_out", [128, D], F32)
        dram("lt_s", [16, D], F32)
        dram("evs_s", [16, 256], F32)
        dram("cscr", [4, 4, 256], BF16)
    if "C" in phases:
        dram("w0", [D, FF], F32, kind="ExternalInput")
        dram("w1f", [FF, D], F32, kind="ExternalInput")
        dram("w2", [D, FF], F32, kind="ExternalInput")
        dram("w2b_s", [D, FF], BF16)
        dram("cc_in", [1, 8], F32)
        dram("cc_out", [1, 8], F32)
    if "D" in phases:
        dram("W_r", [D, D], F32, kind="ExternalInput")
        dram("W_k", [D, D], F32, kind="ExternalInput")
        dram("W_v", [D, D], F32, kind="ExternalInput")
        dram("mu_r", [1, D], F32, kind="ExternalInput")
        dram("mu_k", [1, D], F32, kind="ExternalInput")
        dram("psel", [128, 8], F32, kind="ExternalInput")
        dram("pflag", [128, 2], F32, kind="ExternalInput")
        dram("ag_bd_in", [128, 128], F32)
        dram("ag_bd_out", [1024, 128], F32)
    if first != "A":
        dram("xT_in_f", [D, RPC], F32, kind="ExternalInput")
        dram("xT_in_b", [D, RPC], BF16, kind="ExternalInput")
    for ph in phases:
        if ph == "D":
            continue
        kind = "ExternalOutput" if ph == last else "Internal"
        dram(f"xf_{ph}", [D, RPC], F32, kind=kind)
        dram(f"xb_{ph}", [D, RPC], BF16, kind=kind)
    if "D" in phases:
        dram("out_tok", [RPC, D], F32, kind="ExternalOutput")

    with es:
        p.init_sems(es)
        p.dma_stream("ld", "sync")
        p.dma_stream("ld2", "scalar")
        p.dma_stream("ld3", "gpsimd")
        p.dma_stream("st2", "sync")
        p.dma_stream("st", "gpsimd")
        p.dma_stream("cc", "gpsimd")

        ctx.ident_b = ctx.sb("identb", [128, 128], BF16)
        ctx.ident_f = ctx.sb("identf", [128, 128], F32)
        ctx.ones_col = ctx.sb("ones_col", [128, 1], F32)
        ctx.ones_row = ctx.sb("ones_row", [1, 128], F32)
        ctx.ones_brow = ctx.sb("ones_brow", [1, 128], BF16)
        ctx.one11 = ctx.sb("one11", [1, 1], F32)
        ctx.eps1 = ctx.sb("eps1", [1, 1], F32)
        ctx.zero1 = ctx.sb("zero1", [1, 1], F32)
        ctx.zero128 = ctx.sb("zero128", [128, 1], F32)
        p.dma("ld", ctx.ident_b[:], g["ident_bf"][:])
        p.dma("ld", ctx.ident_f[:], g["ident_f32"][:])
        p.vector(lambda e: e.memset(ctx.ones_col[:], 1.0))
        p.vector(lambda e: e.memset(ctx.ones_row[:], 1.0))
        p.vector(lambda e: e.memset(ctx.ones_brow[:], 1.0))
        p.vector(lambda e: e.memset(ctx.one11[:], 1.0))
        if "D" in phases:
            ctx.psel128 = ctx.sb("psel128", [128, 8], F32)
            p.dma("ld", ctx.psel128[:], g["psel"][:])
        p.vector(lambda e: e.memset(ctx.eps1[:], LN_EPS))
        p.vector(lambda e: e.memset(ctx.zero1[:], 0.0))
        tz = p.vector(lambda e: e.memset(ctx.zero128[:], 0.0))
        p.wait("tensor", tz)
        p.wait("scalar", tz)
        p.wait_dma("tensor", "ld")

        for ph in phases:
            if ph == "A":
                emit_phase_a(ctx)
            elif ph == "B":
                emit_phase_b(ctx, src=("xf_A", "xb_A") if "A" in phases else ("xT_in_f", "xT_in_b"))
            elif ph == "C":
                idx = phases.index("C")
                src = (f"xf_{phases[idx-1]}", f"xb_{phases[idx-1]}") if idx > 0 else ("xT_in_f", "xT_in_b")
                emit_phase_c(ctx, src=src)
            elif ph == "D":
                idx = phases.index("D")
                src = (f"xf_{phases[idx-1]}", f"xb_{phases[idx-1]}") if idx > 0 else ("xT_in_f", "xT_in_b")
                emit_phase_d(ctx, src=src)
            p.barrier()

        with nc.Block() as block:
            p.emit_all(block)
    return nc


def host_constants(core):
    b, pp = core // 2, core % 2
    own_pos = np.concatenate(
        [np.arange(c * 512 + pp * 256, c * 512 + pp * 256 + 256) for c in range(NCHUNK)])
    other_pos = np.concatenate(
        [np.arange(c * 512 + (1 - pp) * 256, c * 512 + (1 - pp) * 256 + 256) for c in range(NCHUNK)])
    perm = np.concatenate([own_pos, other_pos])
    half = 32
    freqs = 1.0 / (500.0 ** (np.arange(half, dtype=np.float64) / half))

    def tables(pos, scale):
        ang = pos[None, :].astype(np.float64) * freqs[:, None]
        cos = np.cos(ang) * scale
        sin = np.sin(ang) * scale
        cos128 = np.tile(cos, (4, 1))
        sinS = np.concatenate([-sin, sin, -sin, sin], axis=0)
        return cos128.astype(np.float32), sinS.astype(np.float32)

    kscale = 1.0 / math.sqrt(DH)
    cq, sq = tables(own_pos, 1.0)
    ck, sk = tables(perm, kscale)
    bsel = np.zeros((16, 8), dtype=np.float32)
    bsel[:, 2 * b] = 1.0
    psel = np.zeros((128, 8), dtype=np.float32)
    psel[:, 2 * b + (1 - pp)] = 1.0
    pflag = np.zeros((128, 2), dtype=np.float32)
    pflag[:, 0] = 1.0 if pp == 1 else 0.0
    pflag[:, 1] = 1.0 if pp == 0 else 0.0
    return dict(own_pos=own_pos, perm=perm, cos_q=cq, sin_q=sq, cos_k=ck, sin_k=sk,
                bsel=bsel, psel=psel, pflag=pflag,
                ident_bf=np.eye(128, dtype=ml_dtypes.bfloat16),
                ident_f32=np.eye(128, dtype=np.float32))


def host_inputs_for_phases(inputs, phases, core, const, stream_f=None, stream_b=None):
    b = core // 2
    d = {}
    d["ident_bf"] = const["ident_bf"]
    d["ident_f32"] = const["ident_f32"]
    if "A" in phases:
        x = np.asarray(inputs["L0_attn_x"])
        d["xb"] = np.ascontiguousarray(x[b][const["perm"]])
        d["w_qkv"] = np.asarray(inputs["L0_attn_w_qkv"])
        d["b_qkv"] = np.asarray(inputs["L0_attn_b_qkv"]).reshape(1, -1)
        d["w_out"] = np.asarray(inputs["L0_attn_w_out"])
        d["b_out"] = np.asarray(inputs["L0_attn_b_out"]).reshape(1, -1)
        d["cos_q"], d["sin_q"] = const["cos_q"], const["sin_q"]
        d["cos_k"], d["sin_k"] = const["cos_k"], const["sin_k"]
    else:
        d["xT_in_f"] = stream_f[core]
        d["xT_in_b"] = stream_b[core]
    if "B" in phases:
        d["W1"] = np.asarray(inputs["L0_ffn_W1"])
        d["b1"] = np.asarray(inputs["L0_ffn_b1"]).reshape(1, -1)
        d["W2"] = np.asarray(inputs["L0_ffn_W2"])
        d["b2"] = np.asarray(inputs["L0_ffn_b2"]).reshape(1, -1)
        d["lscale"] = np.asarray(inputs["L0_ffn_scale"]).reshape(1, 1)
        d["bsel"] = const["bsel"]
    if "C" in phases:
        d["w0"] = np.asarray(inputs["L1_attn_w0"])
        d["w1f"] = np.asarray(inputs["L1_attn_w1"])
        d["w2"] = np.asarray(inputs["L1_attn_w2"])
    if "D" in phases:
        d["W_r"] = np.asarray(inputs["L1_ffn_W_r"])
        d["W_k"] = np.asarray(inputs["L1_ffn_W_k"])
        d["W_v"] = np.asarray(inputs["L1_ffn_W_v"])
        d["mu_r"] = np.asarray(inputs["L1_ffn_mu_r"]).reshape(1, -1)
        d["mu_k"] = np.asarray(inputs["L1_ffn_mu_k"]).reshape(1, -1)
        d["psel"] = const["psel"]
        d["pflag"] = const["pflag"]
    return d


def run_launch(inputs, phases, stream_f=None, stream_b=None, debug_outputs=()):
    nc = build_program(phases=tuple(phases), debug_outputs=debug_outputs)
    consts = [host_constants(c) for c in range(NCORES)]
    in_maps = [host_inputs_for_phases(inputs, phases, c, consts[c], stream_f, stream_b)
               for c in range(NCORES)]
    res = run_bass_kernel_spmd(nc, in_maps, core_ids=list(range(NCORES)))
    return res.results


def kernel(**inputs):
    consts = [host_constants(c) for c in range(NCORES)]
    if _os.environ.get("KERNEL_UNFUSED") == "1":
        groups = [("A",), ("B",), ("C",), ("D",)]
    else:
        groups = [("A", "B", "C", "D")]
    stream_f = stream_b = None
    out = None
    for phases in groups:
        results = run_launch(inputs, phases, stream_f, stream_b)
        last = phases[-1]
        if last != "D":
            stream_f = [np.asarray(results[c][f"xf_{last}"]) for c in range(NCORES)]
            stream_b = [np.asarray(results[c][f"xb_{last}"]) for c in range(NCORES)]
        else:
            out = np.zeros((B, S, D), dtype=np.float32)
            for c in range(NCORES):
                b = c // 2
                out[b][consts[c]["own_pos"]] = np.asarray(results[c]["out_tok"])
    return out


# version 11
# speedup vs baseline: 1.0064x; 1.0010x over previous
import math
import os as _os
from contextlib import ExitStack
A_STAGE = _os.environ.get("KERNEL_A_STAGE", "all")

import numpy as np
import ml_dtypes

import concourse.bass as bass
import concourse.mybir as mybir
from concourse.bass_utils import run_bass_kernel_spmd

F32 = mybir.dt.float32
BF16 = mybir.dt.bfloat16
AF = mybir.ActivationFunctionType
ALU = mybir.AluOpType
AX = mybir.AxisListType

B, S, D = 4, 8192, 1024
H3 = 3 * D
NCORES = 8
RPC = S // 2
NCHUNK = 16
SEG = 256
NH, DH, NPAIR = 16, 64, 8
FF = 4096
EXP_BIAS = 12.0
LN_EPS = 1e-5
LR = 0.001

ENGINES = ("sync", "tensor", "vector", "scalar", "gpsimd")


class Prog:
    def __init__(self, nc):
        self.nc = nc
        self.ops = {e: [] for e in ENGINES}
        self.cnt = {e: 0 for e in ENGINES}
        self.waited = {e: set() for e in ENGINES}
        self.dma_cnt = {}
        self.dma_streams = {}
        self._es = None
        self.sems = {}

    def init_sems(self, es):
        for e in ENGINES:
            self.sems[e] = es.enter_context(self.nc.semaphore(f"sem_{e}"))
        self._es = es

    def dma_stream(self, name, queue_engine):
        if name not in self.dma_streams:
            sem = self._es.enter_context(self.nc.semaphore(f"dma_{name}"))
            self.dma_streams[name] = (queue_engine, sem)
            self.dma_cnt[name] = 0
        return name

    def op(self, engine, fn):
        self.cnt[engine] += 1
        tok = (engine, self.cnt[engine])
        self.ops[engine].append(("op", fn, tok))
        return tok

    def tensor(self, fn):
        return self.op("tensor", fn)

    def vector(self, fn):
        return self.op("vector", fn)

    def scalar(self, fn):
        return self.op("scalar", fn)

    def gpsimd(self, fn):
        return self.op("gpsimd", fn)

    def sync_op(self, fn):
        return self.op("sync", fn)

    def dma(self, stream, out, in_, **kw):
        qe, _sem = self.dma_streams[stream]
        self.dma_cnt[stream] += 16
        fn = lambda eng, out=out, in_=in_, kw=kw: eng.dma_start(out=out, in_=in_, **kw)
        self.ops[qe].append(("dma", fn, stream))
        return self.dma_cnt[stream]

    def wait(self, engine, tok):
        if tok is None:
            return
        (pe, t) = tok
        if pe == engine:
            return
        self.waited[pe].add(t)
        self.ops[engine].append(("wait", "op", pe, t))

    def cc(self, fn):
        if not hasattr(self, "cc_sem"):
            self.cc_sem = self._es.enter_context(self.nc.semaphore("cc_sem"))
            self.cc_cnt = 0
        self.cc_cnt += 1
        self.ops["gpsimd"].append(("cc", fn, self.cc_cnt))
        return self.cc_cnt

    def wait_cc(self, engine):
        if getattr(self, "cc_cnt", 0) == 0:
            return
        self.ops[engine].append(("wait", "cc", None, self.cc_cnt))

    def wait_dma(self, engine, stream, upto=None):
        val = self.dma_cnt[stream] if upto is None else upto
        if val == 0:
            return
        self.ops[engine].append(("wait", "dma", stream, val))

    def barrier(self):
        toks = {e: self.cnt[e] for e in ENGINES if self.cnt[e] > 0}
        dmav = dict(self.dma_cnt)
        for e in ENGINES:
            for pe, t in toks.items():
                if pe != e:
                    self.waited[pe].add(t)
                    self.ops[e].append(("wait", "op", pe, t))
            for s, v in dmav.items():
                if v > 0:
                    self.ops[e].append(("wait", "dma", s, v))
            self.wait_cc(e)

    def emit_all(self, block):
        rank = {}
        for e in ENGINES:
            w = sorted(self.waited[e])
            rank[e] = {t: i + 1 for i, t in enumerate(w)}

        def run_engine(ename):
            def body(eng):
                for entry in self.ops[ename]:
                    kind = entry[0]
                    if kind == "op":
                        _, fn, tok = entry
                        inst = fn(eng)
                        if tok[1] in self.waited[ename]:
                            inst.then_inc(self.sems[ename], 1)
                    elif kind == "dma":
                        _, fn, stream = entry
                        fn(eng).then_inc(self.dma_streams[stream][1], 16)
                    elif kind == "cc":
                        _, fn, _n = entry
                        fn(eng).then_inc(self.cc_sem, 1)
                    else:
                        _, wkind, key, val = entry
                        if wkind == "op":
                            eng.wait_ge(self.sems[key], rank[key][val])
                        elif wkind == "cc":
                            eng.wait_ge(self.cc_sem, val)
                        else:
                            eng.wait_ge(self.dma_streams[key][1], val)
            return body

        block.sync(run_engine("sync"))
        block.tensor(run_engine("tensor"))
        block.vector(run_engine("vector"))
        block.scalar(run_engine("scalar"))
        block.gpsimd(run_engine("gpsimd"))


class Ctx:
    def __init__(self, nc, prog, es):
        self.nc = nc
        self.p = prog
        self.es = es
        self.scopes = [es]
        self.g = {}

    def sb(self, name, shape, dt=F32):
        return self.scopes[-1].enter_context(self.nc.sbuf_tensor(name, shape, dt))

    def psum(self, name, shape, dt=F32):
        return self.scopes[-1].enter_context(self.nc.psum_tensor(name, shape, dt))

    def scope(self):
        ctx = self

        class _Scope:
            def __enter__(self_s):
                self_s.es = ExitStack()
                ctx.scopes.append(self_s.es)
                return ctx

            def __exit__(self_s, *a):
                if a[0] is None:
                    ctx.p.barrier()
                ctx.scopes.pop()
                return self_s.es.__exit__(*a)

        return _Scope()


def bias_chunks(ctx, dram_ap, n, tag, scale=1.0):
    p = ctx.p
    out = ctx.sb(f"bchunks_{tag}", [128, n], F32)
    with ctx.scope():
        row = ctx.sb(f"brow_{tag}", [1, n * 128], F32)
        ps = ctx.psum(f"psb_{tag}", [128, 1], F32)
        p.dma("ld", row[:], dram_ap)
        p.wait_dma("tensor", "ld")
        for j in range(n):
            tmm = p.tensor(lambda e, j=j: e.matmul(
                ps[:], row[0:1, j * 128:(j + 1) * 128], ctx.one11[:], start=True, stop=True))
            p.wait("vector", tmm)
            if scale == 1.0:
                last = p.vector(lambda e, j=j: e.tensor_copy(out[:, j:j + 1], ps[:]))
            else:
                last = p.vector(lambda e, j=j: e.tensor_scalar_mul(out[:, j:j + 1], ps[:], scale))
            p.wait("tensor", last)
    return out


def emit_ln(ctx, lnt, ycol, N, store_xf, store_xb, ps_stat, ps_rep, st_stream="st"):
    p = ctx.p
    y2t, srow, mrow, vrow, rrow, mr2, rep, xbt = lnt

    ps_s, ps_q = ps_stat
    mm_q = None
    for c in range(8):
        p.wait("vector", mm_q)
        tsq = p.vector(lambda e, c=c: e.tensor_mul(
            y2t[:], ycol[:, c * N:(c + 1) * N], ycol[:, c * N:(c + 1) * N]))
        p.tensor(lambda e, c=c: e.matmul(
            ps_s[:], ctx.ones_col[:], ycol[:, c * N:(c + 1) * N],
            start=(c == 0), stop=(c == 7)))
        p.wait("tensor", tsq)
        mm_q = p.tensor(lambda e, c=c: e.matmul(
            ps_q[:], ctx.ones_col[:], y2t[:],
            start=(c == 0), stop=(c == 7)))
    p.wait("vector", mm_q)
    p.vector(lambda e: e.tensor_copy(srow[0:1, 0:N], ps_s[:]))
    tcp = p.vector(lambda e: e.tensor_copy(srow[0:1, N:2 * N], ps_q[:]))
    p.wait("tensor", tcp)
    p.vector(lambda e: e.tensor_scalar_mul(mrow[:], srow[0:1, 0:N], 1.0 / D))
    p.vector(lambda e: e.tensor_scalar_mul(vrow[:], srow[0:1, N:2 * N], 1.0 / D))
    p.vector(lambda e: e.tensor_mul(mr2[:], mrow[:], mrow[:]))
    tv = p.vector(lambda e: e.tensor_sub(vrow[:], vrow[:], mr2[:]))
    p.wait("scalar", tv)
    p.scalar(lambda e: e.activation(rrow[:], vrow[:], AF.Ln, bias=ctx.eps1[:], scale=1.0))
    tr = p.scalar(lambda e: e.activation(rrow[:], rrow[:], AF.Exp, bias=ctx.zero1[:], scale=-0.5))
    p.wait("vector", tr)
    tmr = p.vector(lambda e: e.tensor_mul(mr2[:], mrow[:], rrow[:]))
    p.wait("tensor", tmr)
    bm1 = p.tensor(lambda e: e.matmul(ps_rep[:, 0:N], ctx.ones_row[:], rrow[:], start=True, stop=True))
    bm2 = p.tensor(lambda e: e.matmul(ps_rep[:, N:2 * N], ctx.ones_row[:], mr2[:], start=True, stop=True))
    p.wait("vector", bm1)
    p.wait("vector", bm2)
    tcr = p.vector(lambda e: e.tensor_copy(rep[:], ps_rep[:, 0:2 * N]))
    p.wait("tensor", tcr)
    for c in range(8):
        p.vector(lambda e, c=c: e.tensor_mul(ycol[:, c * N:(c + 1) * N],
                                             ycol[:, c * N:(c + 1) * N], rep[:, 0:N]))
        tfc = p.vector(lambda e, c=c: e.tensor_sub(ycol[:, c * N:(c + 1) * N],
                                                   ycol[:, c * N:(c + 1) * N], rep[:, N:2 * N]))
        p.wait("gpsimd" if st_stream == "st" else "sync", tfc)
        store_xf(c, ycol[:, c * N:(c + 1) * N])
        if store_xb is not None:
            p.wait_dma("vector", st_stream)
            tbc = p.vector(lambda e, c=c: e.tensor_copy(xbt[:], ycol[:, c * N:(c + 1) * N]))
            p.wait("gpsimd" if st_stream == "st" else "sync", tbc)
            store_xb(c, xbt[:])
    p.wait_dma("vector", st_stream)
    return tfc


def make_ln_tiles(ctx, N, tag):
    return (
        ctx.sb(f"ln_y2_{tag}", [128, N], F32),
        ctx.sb(f"ln_srow_{tag}", [1, 2 * N], F32),
        ctx.sb(f"ln_mrow_{tag}", [1, N], F32),
        ctx.sb(f"ln_vrow_{tag}", [1, N], F32),
        ctx.sb(f"ln_rrow_{tag}", [1, N], F32),
        ctx.sb(f"ln_mr2_{tag}", [1, N], F32),
        ctx.sb(f"ln_rep_{tag}", [128, 2 * N], F32),
        ctx.sb(f"ln_xb_{tag}", [128, N], BF16),
    )


def emit_rope(ctx, tmp, ct, st, out_bf, rop1, rop2, N):
    p = ctx.p
    p.vector(lambda e: e.tensor_mul(rop1[:], tmp[:], ct[:]))
    for q in range(4):
        a, b = q * 32, (q ^ 1) * 32
        p.vector(lambda e, a=a, b=b: e.tensor_copy(rop2[a:a + 32, :], tmp[b:b + 32, :]))
    p.vector(lambda e: e.tensor_mul(rop2[:], rop2[:], st[:]))
    return p.vector(lambda e: e.tensor_add(out_bf[:], rop1[:], rop2[:]))


def emit_phase_a(ctx_outer):
    with ctx_outer.scope() as ctx:
        _emit_phase_a(ctx)


def _emit_phase_a(ctx):
    p, g, nc = ctx.p, ctx.g, ctx.nc
    NBLK = 16

    wqkv_b = ctx.sb("wqkv_b", [128, 8 * H3], BF16)
    wout_b = ctx.sb("wout_b", [128, 8 * D], BF16)
    bqk = bias_chunks(ctx, g["b_qkv"][0:1, 0:2048], 16, "bqk")
    bo = bias_chunks(ctx, g["b_out"][0:1, :], 8, "bo")
    if _os.environ.get("KERNEL_DEBUG_BIAS") == "1":
        dbg = nc.dram_tensor("bqk_dbg", [128, 16], F32, kind="ExternalOutput").ap()
        p.barrier()
        p.dma("st", dbg[:], bqk[:])
    bv_b = ctx.sb("bv_b", [1, D], BF16)
    o_all = ctx.sb("o_all", [128, NPAIR * RPC], BF16)
    with ctx.scope():
        wtmp = ctx.sb("wtmp_a", [128, H3], F32)
        for dc in range(8):
            p.dma("ld", wtmp[:], g["w_qkv"][dc * 128:(dc + 1) * 128, :])
            p.wait_dma("vector", "ld")
            tcast = p.vector(lambda e, dc=dc: e.tensor_copy(wqkv_b[:, dc * H3:(dc + 1) * H3], wtmp[:]))
            p.wait("sync", tcast)
        for dc in range(8):
            p.dma("ld", wtmp[:, 0:D], g["w_out"][dc * 128:(dc + 1) * 128, :])
            p.wait_dma("vector", "ld")
            tcast = p.vector(lambda e, dc=dc: e.tensor_copy(wout_b[:, dc * D:(dc + 1) * D], wtmp[:, 0:D]))
            p.wait("sync", tcast)
        p.dma("ld", wtmp[0:1, 0:D], g["b_qkv"][0:1, 2048:3072])
        p.wait_dma("vector", "ld")
        p.vector(lambda e: e.tensor_copy(bv_b[:], wtmp[0:1, 0:D]))

    with ctx.scope():
        sb_ = ctx.sb
        ps_ = ctx.psum
        xrow = sb_("xrow", [128, D])
        xTf = sb_("xTf", [128, 8 * 512])
        xTb = sb_("xTb", [128, 8 * 512], BF16)
        ct = sb_("ct", [128, 512])
        st_t = sb_("st_t", [128, 512])
        ctq = sb_("ctq", [128, 512])
        stq = sb_("stq", [128, 512])
        tmpe = sb_("tmpe", [128, 512])
        rop1 = sb_("rop1", [128, 512])
        rop2 = sb_("rop2", [128, 512])
        kout = sb_("kout", [128, 512], BF16)
        qout = sb_("qout", [128, 512], BF16)
        vout = sb_("vout", [128, 512], BF16)
        psT = ps_("psT", [128, 1024])
        psQK = [ps_("psQK0", [128, 512]), ps_("psQK1", [128, 512])]
        psV = ps_("psV", [128, 512])

        last_rope = None
        tlast_tr = None
        for blk in range(NBLK):
            own = blk < 8
            p.wait_dma("vector", "st")
            tprev_cast = None
            for t in range(4):
                if tlast_tr is not None:
                    p.wait("sync", tlast_tr)
                p.dma("ld", xrow[:], g["xb"][blk * 512 + t * 128: blk * 512 + (t + 1) * 128, :])
                p.wait_dma("tensor", "ld")
                ttr = None
                for dc in range(8):
                    slot = (dc % 2) * 512
                    ttr = p.tensor(lambda e, dc=dc, slot=slot: e.transpose(
                        psT[:, slot:slot + 128], xrow[:, dc * 128:(dc + 1) * 128], ctx.ident_f[:]))
                    p.wait("vector", ttr)
                    tev = p.vector(lambda e, dc=dc, t=t, slot=slot: e.tensor_copy(
                        xTf[:, dc * 512 + t * 128: dc * 512 + (t + 1) * 128],
                        psT[:, slot:slot + 128]))
                    p.wait("tensor", tev)
                tlast_tr = tev
                tcast = p.vector(lambda e, t=t: e.tensor_copy(
                    xTb.rearrange("p (c n) -> p c n", c=8)[:, :, t * 128:(t + 1) * 128],
                    xTf.rearrange("p (c n) -> p c n", c=8)[:, :, t * 128:(t + 1) * 128]))
                tprev_cast = tcast
            if own:
                p.wait("gpsimd", tprev_cast)
                for dc in range(8):
                    p.dma("st", g["XR0"][dc * 128:(dc + 1) * 128, blk * 512:(blk + 1) * 512],
                          xTf[:, dc * 512:(dc + 1) * 512])
            if A_STAGE == "tr":
                continue
            p.wait("scalar", last_rope)
            p.dma("ld2", ct[:], g["cos_k"][:, blk * 512:(blk + 1) * 512])
            p.dma("ld2", st_t[:], g["sin_k"][:, blk * 512:(blk + 1) * 512])
            if own:
                p.dma("ld2", ctq[:], g["cos_q"][:, blk * 512:(blk + 1) * 512])
                p.dma("ld2", stq[:], g["sin_q"][:, blk * 512:(blk + 1) * 512])
            p.wait("tensor", tprev_cast)
            p.wait_dma("vector", "ld2")

            for kc in range(8):
                bank = psQK[kc % 2]
                mm = None
                for dc in range(8):
                    mm = p.tensor(lambda e, kc=kc, dc=dc, bank=bank: e.matmul(
                        bank[:], wqkv_b[:, dc * H3 + D + kc * 128: dc * H3 + D + (kc + 1) * 128],
                        xTb[:, dc * 512:(dc + 1) * 512],
                        start=(dc == 0), stop=(dc == 7)))
                p.wait("vector", mm)
                tb_ = p.vector(lambda e, kc=kc, bank=bank: e.tensor_scalar_add(tmpe[:], bank[:], bqk[:, 8 + kc:9 + kc]))
                p.wait("tensor", tb_)
                tlast = emit_rope(ctx, tmpe, ct, st_t, kout, rop1, rop2, 512)
                last_rope = tlast
                p.wait("gpsimd", tlast)
                p.dma("st", g["KT_s"][kc * 128:(kc + 1) * 128, blk * 512:(blk + 1) * 512], kout[:])
                p.wait_dma("vector", "st")
            if own:
                for qc in range(8):
                    bank = psQK[qc % 2]
                    mm = None
                    for dc in range(8):
                        mm = p.tensor(lambda e, qc=qc, dc=dc, bank=bank: e.matmul(
                            bank[:], wqkv_b[:, dc * H3 + qc * 128: dc * H3 + (qc + 1) * 128],
                            xTb[:, dc * 512:(dc + 1) * 512],
                            start=(dc == 0), stop=(dc == 7)))
                    p.wait("vector", mm)
                    tb_ = p.vector(lambda e, qc=qc, bank=bank: e.tensor_scalar_add(
                        tmpe[:], bank[:], bqk[:, qc:qc + 1]))
                    p.wait("tensor", tb_)
                    tlast = emit_rope(ctx, tmpe, ctq, stq, qout, rop1, rop2, 512)
                    last_rope = tlast
                    p.wait("gpsimd", tlast)
                    p.dma("st", g["QT_s"][qc * 128:(qc + 1) * 128, blk * 512:(blk + 1) * 512], qout[:])
                    p.wait_dma("vector", "st")
            for t in range(4):
                for vc in range(2):
                    mm = None
                    for dc in range(8):
                        mm = p.tensor(lambda e, t=t, vc=vc, dc=dc: e.matmul(
                            psV[:], xTb[:, dc * 512 + t * 128: dc * 512 + (t + 1) * 128],
                            wqkv_b[:, dc * H3 + 2048 + vc * 512: dc * H3 + 2048 + (vc + 1) * 512],
                            start=(dc == 0), stop=False))
                    mm = p.tensor(lambda e, t=t, vc=vc: e.matmul(
                        psV[:], ctx.ones_brow[:], bv_b[0:1, vc * 512:(vc + 1) * 512],
                        start=False, stop=True))
                    p.wait("vector", mm)
                    tev = p.vector(lambda e: e.tensor_copy(vout[:], psV[:]))
                    p.wait("tensor", tev)
                    p.wait("gpsimd", tev)
                    p.dma("st", g["V_s"][blk * 512 + t * 128: blk * 512 + (t + 1) * 128,
                                         vc * 512:(vc + 1) * 512], vout[:])
                    p.wait_dma("vector", "st")

    p.barrier()
    if A_STAGE in ("tr", "qkv"):
        return

    with ctx.scope():
        sb_ = ctx.sb
        ps_ = ctx.psum
        qT = sb_("qT", [128, RPC], BF16)
        kT = sb_("kT", [128, S], BF16)
        vpA = sb_("vpA", [128, 64 * 65], BF16)
        vpB = sb_("vpB", [128, 64 * 65], BF16)
        PP = sb_("pp", [128, 4 * 1024], BF16)
        osA = sb_("osA", [65, 512])
        osB = sb_("osB", [65, 512])
        zrec = sb_("zrec", [1, 1024])
        psS2 = [ps_("psSa", [128, 1024]), ps_("psSb", [128, 1024])]
        poA = ps_("poA", [65, 512])
        poB = ps_("poB", [65, 512])
        psR = ps_("psR", [128, 1024])
        mb = sb_("mb", [128, 1])
        p.vector(lambda e: e.memset(mb[:], -EXP_BIAS))

        prev_smm = None
        prev_pv = None
        prev_tmo = None
        for pair in range(NPAIR):
            if pair > 0:
                for engq in ("sync", "scalar", "gpsimd"):
                    p.wait(engq, prev_smm)
                    p.wait(engq, prev_pv)
                p.wait("vector", prev_pv)
            p.dma("ld", qT[:], g["QT_s"][pair * 128:(pair + 1) * 128, :])
            p.dma("ld2", kT[:], g["KT_s"][pair * 128:(pair + 1) * 128, :])
            p.vector(lambda e: e.memset(vpA.rearrange("p (n c) -> p n c", c=65)[:, :, 64], 1.0))
            tms = p.vector(lambda e: e.memset(vpB.rearrange("p (n c) -> p n c", c=65)[:, :, 64], 1.0))
            p.dma("ld3", vpA.rearrange("p (n c) -> p n c", c=65)[:, :, 0:64],
                  g["V_s"].rearrange("(n p) c -> p n c", p=128)[:, :, pair * 128: pair * 128 + 64])
            p.dma("ld", vpB.rearrange("p (n c) -> p n c", c=65)[:, :, 0:64],
                  g["V_s"].rearrange("(n p) c -> p n c", p=128)[:, :, pair * 128 + 64: pair * 128 + 128])
            p.wait_dma("tensor", "ld")
            p.wait_dma("tensor", "ld2")
            p.wait_dma("tensor", "ld3")
            p.wait("tensor", tms)

            exp_hist = {}
            pv_hist = {}

            def emit_pv(kt, gk):
                p.wait("tensor", exp_hist[gk])
                base = (gk % 4) * 1024
                pv = p.tensor(lambda e, kt=kt, base=base: e.matmul(
                    poA[:], vpA[:, kt * 65:(kt + 1) * 65],
                    PP[:, base:base + 512],
                    start=(kt == 0), stop=(kt == 63)))
                pv = p.tensor(lambda e, kt=kt, base=base: e.matmul(
                    poB[:], vpB[:, kt * 65:(kt + 1) * 65],
                    PP[:, base + 512:base + 1024],
                    start=(kt == 0), stop=(kt == 63)))
                pv_hist[gk] = pv

            for qc in range(8):
                for kt in range(64):
                    gk = qc * 64 + kt
                    if gk >= 2:
                        p.wait("tensor", exp_hist[gk - 2])
                    bank = psS2[gk % 2]
                    smm = p.tensor(lambda e, kt=kt, qc=qc, bank=bank: e.matmul(
                        bank[:, 0:512],
                        kT[0:64, kt * 128:(kt + 1) * 128], qT[0:64, qc * 512:(qc + 1) * 512],
                        start=True, stop=True, tile_position=(0, 0)))
                    smm = p.tensor(lambda e, kt=kt, qc=qc, bank=bank: e.matmul(
                        bank[:, 512:1024],
                        kT[64:128, kt * 128:(kt + 1) * 128], qT[64:128, qc * 512:(qc + 1) * 512],
                        start=True, stop=True, tile_position=(64, 0)))
                    p.wait("scalar", smm)
                    if gk >= 4:
                        p.wait("scalar", pv_hist[gk - 4])
                    texp = p.scalar(lambda e, gk=gk, bank=bank: e.activation(
                        PP[:, (gk % 4) * 1024:(gk % 4 + 1) * 1024], bank[:],
                        AF.Exp, bias=mb[:], scale=1.0))
                    exp_hist[gk] = texp
                    if kt > 0:
                        emit_pv(kt - 1, gk - 1)
                emit_pv(63, qc * 64 + 63)
                p.wait("vector", pv_hist[qc * 64 + 63])
                p.vector(lambda e: e.tensor_copy(osA[:], poA[:]))
                tcb = p.vector(lambda e: e.tensor_copy(osB[:], poB[:]))
                p.vector(lambda e: e.reciprocal(zrec[0:1, 0:512], osA[64:65, :]))
                trc = p.vector(lambda e: e.reciprocal(zrec[0:1, 512:1024], osB[64:65, :]))
                p.wait("tensor", tcb)
                p.wait("tensor", trc)
                p.wait("tensor", prev_tmo)
                z1 = p.tensor(lambda e: e.matmul(psR[0:64, 0:512], ctx.ones_row[0:1, 0:64],
                                                 zrec[0:1, 0:512], start=True, stop=True))
                z2 = p.tensor(lambda e: e.matmul(psR[64:128, 512:1024], ctx.ones_row[0:1, 0:64],
                                                 zrec[0:1, 512:1024], start=True, stop=True))
                p.wait("vector", z1)
                p.wait("vector", z2)
                p.vector(lambda e, pair=pair, qc=qc: e.tensor_mul(
                    o_all[0:64, pair * RPC + qc * 512: pair * RPC + (qc + 1) * 512],
                    osA[0:64, :], psR[0:64, 0:512]))
                tmo = p.vector(lambda e, pair=pair, qc=qc: e.tensor_mul(
                    o_all[64:128, pair * RPC + qc * 512: pair * RPC + (qc + 1) * 512],
                    osB[0:64, :], psR[64:128, 512:1024]))
                prev_tmo = tmo
            prev_smm = smm
            prev_pv = pv_hist[7 * 64 + 63]

    if A_STAGE == "flash":
        return
    with ctx.scope():
        sb_ = ctx.sb
        ps_ = ctx.psum
        ycol = sb_("ycol", [128, 8 * 512])
        xrt_all = sb_("xrt_all", [128, 8 * 512])
        psOb = [ps_("psOa3", [128, 512]), ps_("psOb3", [128, 512])]
        ps_stat = (ps_("ps_stat_a_s", [1, 512]), ps_("ps_stat_a_q", [1, 512]))
        ps_rep = ps_("ps_rep_a", [128, 1024])
        lnt = make_ln_tiles(ctx, 512, "a")
        t_evac = {0: None, 1: None}
        last_res = None
        pending_ln = None
        for qc in range(8):
            p.wait("sync", last_res)
            for dm in range(8):
                p.dma("ld", xrt_all[:, dm * 512:(dm + 1) * 512],
                      g["XR0"][dm * 128:(dm + 1) * 128, qc * 512:(qc + 1) * 512])
            for dm in range(8):
                bank = psOb[dm % 2]
                p.wait("tensor", t_evac[dm % 2])
                mm = None
                for pr in range(8):
                    mm = p.tensor(lambda e, qc=qc, dm=dm, pr=pr, bank=bank: e.matmul(
                        bank[:], wout_b[:, pr * D + dm * 128: pr * D + (dm + 1) * 128],
                        o_all[:, pr * RPC + qc * 512: pr * RPC + (qc + 1) * 512],
                        start=(pr == 0), stop=(pr == 7)))
                if dm == 0 and pending_ln is not None:
                    pending_ln()
                    pending_ln = None
                p.wait("vector", mm)
                if dm == 0:
                    p.wait_dma("vector", "ld")
                tadd = p.vector(lambda e, dm=dm, bank=bank: e.tensor_scalar_add(
                    ycol[:, dm * 512:(dm + 1) * 512], bank[:], bo[:, dm:dm + 1]))
                tres = p.vector(lambda e, dm=dm: e.tensor_add(
                    ycol[:, dm * 512:(dm + 1) * 512], ycol[:, dm * 512:(dm + 1) * 512],
                    xrt_all[:, dm * 512:(dm + 1) * 512]))
                t_evac[dm % 2] = tadd
                last_res = tres
            def _mk_ln_a(qc=qc):
                def _f():
                    emit_ln(ctx, lnt, ycol, 512,
                            lambda c, ap, qc=qc: p.dma("st", g["xf_A"][c * 128:(c + 1) * 128,
                                                                      qc * 512:(qc + 1) * 512], ap),
                            lambda c, ap, qc=qc: p.dma("st", g["xb_A"][c * 128:(c + 1) * 128,
                                                                      qc * 512:(qc + 1) * 512], ap),
                            ps_stat, ps_rep)
                return _f
            pending_ln = _mk_ln_a()
        pending_ln()



def emit_phase_b(ctx_outer, src):
    with ctx_outer.scope() as ctx:
        _emit_phase_b(ctx, src)


def _emit_phase_b(ctx, src):
    p, g, nc = ctx.p, ctx.g, ctx.nc
    xf_src, xb_src = g[src[0]], g[src[1]]

    w1t = ctx.sb("w1t", [128, 8 * FF], BF16)
    w2t = ctx.sb("w2t", [128, 32 * D], BF16)
    b1c = bias_chunks(ctx, g["b1"][0:1, :], 32, "b1")
    b2c = bias_chunks(ctx, g["b2"][0:1, :], 8, "b2")
    er1 = ctx.sb("er1", [1, 4 * 1024], BF16)
    oddc1 = ctx.sb("oddc1", [128, 8 * 4], BF16)
    oddc2 = ctx.sb("oddc2", [128, 8 * 4], BF16)
    lsc = ctx.sb("lsc", [1, 1], F32)
    p.dma("ld", lsc[:], g["lscale"][:])

    evs4 = ctx.sb("evs4", [4, 8 * 128], BF16)

    with ctx.scope():
        wrow = ctx.sb("wrow_b", [128, FF])
        wrow_bf = ctx.sb("wrowbf_b", [128, FF], BF16)
        psTb = ctx.psum("psT_b", [128, 1024], BF16)
        tprev = None
        for hc in range(32):
            p.wait("sync", tprev)
            p.dma("ld", wrow[:, 0:D], g["W1"][hc * 128:(hc + 1) * 128, :])
            p.wait_dma("vector", "ld")
            tc = p.vector(lambda e: e.tensor_copy(wrow_bf[:, 0:D], wrow[:, 0:D]))
            tprev = tc
            p.wait("tensor", tc)
            for dc in range(8):
                slot = (dc % 2) * 512
                tt = p.tensor(lambda e, dc=dc, slot=slot: e.transpose(
                    psTb[:, slot:slot + 128], wrow_bf[:, dc * 128:(dc + 1) * 128], ctx.ident_b[:]))
                p.wait("vector", tt)
                tev = p.vector(lambda e, dc=dc, hc=hc, slot=slot: e.tensor_copy(
                    w1t[:, dc * FF + hc * 128: dc * FF + (hc + 1) * 128], psTb[:, slot:slot + 128]))
                p.wait("tensor", tev)
                tprev = tev
        for dc in range(8):
            p.wait("sync", tprev)
            p.dma("ld", wrow[:], g["W2"][dc * 128:(dc + 1) * 128, :])
            p.wait_dma("vector", "ld")
            tc = p.vector(lambda e: e.tensor_copy(wrow_bf[:], wrow[:]))
            tprev = tc
            p.wait("tensor", tc)
            for hc in range(32):
                slot = (hc % 2) * 512
                tt = p.tensor(lambda e, hc=hc, slot=slot: e.transpose(
                    psTb[:, slot:slot + 128], wrow_bf[:, hc * 128:(hc + 1) * 128], ctx.ident_b[:]))
                p.wait("vector", tt)
                tev = p.vector(lambda e, dc=dc, hc=hc, slot=slot: e.tensor_copy(
                    w2t[:, hc * D + dc * 128: hc * D + (dc + 1) * 128], psTb[:, slot:slot + 128]))
                p.wait("tensor", tev)
                tprev = tev

    with ctx.scope():
        ltcols = ctx.sb("ltcols", [128, 16])
        lt_tok = ctx.sb("lt_tok", [16, 1024])
        psLT = ctx.psum("psLT", [128, 128])
        tprev = None
        for dc in range(8):
            p.wait("sync", tprev)
            p.dma("ld", ltcols[:], xf_src[dc * 128:(dc + 1) * 128, 1:17])
            p.wait_dma("tensor", "ld")
            tt = p.tensor(lambda e, dc=dc: e.transpose(psLT[0:16, :], ltcols[:], ctx.ident_f[:]))
            p.wait("vector", tt)
            tev = p.vector(lambda e, dc=dc: e.tensor_copy(lt_tok[:, dc * 128:(dc + 1) * 128],
                                                          psLT[0:16, :]))
            tprev = tev
            p.wait("tensor", tev)
        p.wait("gpsimd", tprev)
        p.dma("cc", g["ag_lt_in"][:], lt_tok[:])
        p.wait_dma("gpsimd", "cc")
        p.cc(lambda e: e.collective_compute(
            "AllGather", ALU.bypass, replica_groups=[list(range(NCORES))],
            ins=[g["ag_lt_in"].opt()], outs=[g["ag_lt_out"].opt()]))
        p.wait_cc("gpsimd")
        agt = ctx.sb("agt", [16, 8 * 1024])
        bselt = ctx.sb("bselt", [16, 8])
        ltmine = ctx.sb("ltmine", [16, 1024])
        p.dma("ld", bselt[:], g["bsel"][:])
        for j in range(NCORES):
            p.dma("cc", agt[:, j * 1024:(j + 1) * 1024], g["ag_lt_out"][j * 16:(j + 1) * 16, :])
        p.wait_dma("vector", "cc")
        p.wait_dma("vector", "ld")
        p.vector(lambda e: e.tensor_scalar_mul(ltmine[:], agt[:, 0:1024], bselt[:, 0:1]))
        tcomb = None
        for j in range(1, NCORES):
            tcomb = p.vector(lambda e, j=j: e.scalar_tensor_tensor(
                ltmine[:], agt[:, j * 1024:(j + 1) * 1024], bselt[:, j:j + 1], ltmine[:],
                ALU.mult, ALU.add))
        p.wait("gpsimd", tcomb)
        p.dma("st", g["lt_s"][:], ltmine[:])
        ltfeat = ctx.sb("ltfeat", [128, 8 * 16])
        p.wait("tensor", tcomb)
        tprev = None
        for dc in range(8):
            tt = p.tensor(lambda e, dc=dc: e.transpose(
                psLT[:, 0:16], ltmine[:, dc * 128:(dc + 1) * 128], ctx.ident_f[0:16, 0:16]))
            p.wait("vector", tt)
            tev = p.vector(lambda e, dc=dc: e.tensor_copy(ltfeat[:, dc * 16:(dc + 1) * 16],
                                                          psLT[:, 0:16]))
            p.wait("tensor", tev)
            tprev = tev
        if _os.environ.get("KERNEL_DEBUG_B") == "1":
            dbg_lf = nc.dram_tensor("dbg_lf", [128, 128], F32, kind="ExternalOutput").ap()
            p.barrier()
            p.dma("st", dbg_lf[:], ltfeat[:])
            p.barrier()
        for dc in range(8):
            p.vector(lambda e, dc=dc: e.tensor_copy(
                oddc1[:, dc * 4:(dc + 1) * 4],
                ltfeat.rearrange("p (c s) -> p c s", s=16)[:, dc, 1:9:2]))
            p.vector(lambda e, dc=dc: e.tensor_copy(
                oddc2[:, dc * 4:(dc + 1) * 4],
                ltfeat.rearrange("p (c s) -> p c s", s=16)[:, dc, 9:16:2]))
        erow = ctx.sb("erow", [1, 1024])
        er2f = ctx.sb("er2f", [1, 1024])
        evrow = ctx.sb("evrow", [1, 256])
        p.wait_dma("sync", "st")
        p.wait_dma("vector", "ld")
        tprevdma = None
        for j in range(4):
            p.wait("sync", tprevdma)
            p.dma("ld", erow[:], g["lt_s"][2 * j:2 * j + 1, :])
            p.wait_dma("vector", "ld")
            t1 = p.vector(lambda e, j=j: e.tensor_scalar_mul(
                er1[0:1, j * 1024:(j + 1) * 1024], erow[:], lsc[0:1, 0:1]))
            tprevdma = t1
        for j in range(4):
            p.wait("sync", tprevdma)
            p.dma("ld", erow[:], g["lt_s"][8 + 2 * j:9 + 2 * j, :])
            p.wait_dma("vector", "ld")
            t2 = p.vector(lambda e: e.tensor_scalar_mul(er2f[:], erow[:], lsc[0:1, 0:1]))
            p.wait("gpsimd", t2)
            for r in range(4):
                tv = p.vector(lambda e, r=r: e.tensor_copy(evrow[:], er2f[0:1, r:1024:4]))
                p.wait("gpsimd", tv)
                p.dma("st", g["evs_s"][4 * j + r:4 * j + r + 1, :], evrow[:])
                p.wait_dma("vector", "st")
            tprevdma = t2
        evs4f = ctx.sb("evs4f", [4, 8 * 128])
        p.wait_dma("sync", "st")
        for j in range(4):
            for half in range(2):
                p.dma("ld", evs4f[:, (j * 2 + half) * 128:(j * 2 + half + 1) * 128],
                      g["evs_s"].rearrange("a (h b) -> a h b", h=2)[4 * j:4 * j + 4, half, :])
        p.wait_dma("vector", "ld")
        tcv = p.vector(lambda e: e.tensor_copy(evs4[:], evs4f[:]))
        p.wait("tensor", tcv)

    if _os.environ.get("KERNEL_DEBUG_B") == "1":
        dbg_oc = nc.dram_tensor("dbg_oc", [128, 32], BF16, kind="ExternalOutput").ap()
        dbg_oc2 = nc.dram_tensor("dbg_oc2", [128, 32], BF16, kind="ExternalOutput").ap()
        p.barrier()
        p.dma("st", dbg_oc[:], oddc1[:])
        p.dma("st", dbg_oc2[:], oddc2[:])
        dbg_e4 = nc.dram_tensor("dbg_e4", [4, 1024], BF16, kind="ExternalOutput").ap()
        p.dma("st", dbg_e4[:], evs4[:])
        p.barrier()
    TBW = 256
    NTB = RPC // TBW
    with ctx.scope():
        xcolb2 = [ctx.sb("xcolb_b0", [128, 8 * TBW], BF16),
                  ctx.sb("xcolb_b1", [128, 8 * TBW], BF16)]
        xrt = ctx.sb("xrt_b", [128, TBW])
        htile = ctx.sb("htile_b", [128, 32 * TBW], BF16)
        ab = ctx.sb("ab_b", [1, 4 * TBW], BF16)
        c4 = ctx.sb("c4_b", [4, 4 * TBW], BF16)
        crow = ctx.sb("crow_b", [4, TBW], BF16)
        ycol = ctx.sb("ycol_b", [128, 8 * TBW])
        lnt = make_ln_tiles(ctx, TBW, "b")
        psAa = ctx.psum("psAa", [1, TBW])
        bankA = ctx.psum("bankA_b", [128, TBW])
        bankB = ctx.psum("bankB_b", [128, TBW])
        psC = ctx.psum("psC_b", [4, TBW])
        ps_stat = (ctx.psum("ps_stat_b_s", [1, TBW]), ctx.psum("ps_stat_b_q", [1, TBW]))
        ps_rep = ctx.psum("ps_rep_b", [128, 2 * TBW])

        tlast_res = None
        pending_ln = None
        prev_tbh = {}
        buf_use = {0: None, 1: None}
        for tb in range(NTB):
            cols = (tb * TBW, (tb + 1) * TBW)
            xcolb = xcolb2[tb % 2]
            if tb == 0:
                for dc in range(8):
                    p.dma("ld", xcolb[:, dc * TBW:(dc + 1) * TBW],
                          xb_src[dc * 128:(dc + 1) * 128, cols[0]:cols[1]])
            p.wait_dma("tensor", "ld")
            for j in range(4):
                mm = None
                for dc in range(8):
                    mm = p.tensor(lambda e, j=j, dc=dc, xcolb=xcolb: e.matmul(
                        psAa[:], oddc1[:, dc * 4 + j: dc * 4 + j + 1],
                        xcolb[:, dc * TBW:(dc + 1) * TBW],
                        start=(dc == 0), stop=(dc == 7)))
                p.wait("vector", mm)
                tev = p.vector(lambda e, j=j: e.tensor_copy(ab[0:1, j * TBW:(j + 1) * TBW], psAa[:]))
                p.wait("tensor", tev)
            if tb + 1 < NTB:
                p.wait("sync", buf_use[(tb + 1) % 2])
                for dc in range(8):
                    p.dma("ld", xcolb2[(tb + 1) % 2][:, dc * TBW:(dc + 1) * TBW],
                          xb_src[dc * 128:(dc + 1) * 128, (tb + 1) * TBW:(tb + 2) * TBW])
            tg_hist = {}
            tev_c = None
            for hc in range(32):
                bank = bankA if hc % 2 == 0 else bankB
                j = hc // 8
                if hc >= 2:
                    p.wait("tensor", tg_hist[hc - 2])
                else:
                    p.wait("tensor", prev_tbh.get(6 + hc))
                for dc in range(8):
                    p.tensor(lambda e, hc=hc, dc=dc, bank=bank, xcolb=xcolb: e.matmul(
                        bank[:], w1t[:, dc * FF + hc * 128: dc * FF + (hc + 1) * 128],
                        xcolb[:, dc * TBW:(dc + 1) * TBW],
                        start=(dc == 0), stop=False))
                mm = p.tensor(lambda e, hc=hc, j=j, bank=bank: e.matmul(
                    bank[:], er1[0:1, j * 1024 + (hc % 8) * 128: j * 1024 + ((hc % 8) + 1) * 128],
                    ab[0:1, j * TBW:(j + 1) * TBW],
                    start=False, stop=True))
                p.wait("scalar", mm)
                tg = p.scalar(lambda e, hc=hc, bank=bank: e.activation(
                    htile[:, hc * TBW:(hc + 1) * TBW], bank[:], AF.Gelu,
                    bias=b1c[:, hc:hc + 1], scale=1.0))
                tg_hist[hc] = tg
                if hc % 8 == 7:
                    r = hc // 8
                    p.wait("tensor", tg)
                    p.wait("tensor", tev_c)
                    mmc = None
                    for qc in range(8):
                        mmc = p.tensor(lambda e, r=r, qc=qc: e.matmul(
                            psC[:], oddc2[:, qc * 4:(qc + 1) * 4],
                            htile[:, (r * 8 + qc) * TBW:(r * 8 + qc + 1) * TBW],
                            start=(qc == 0), stop=(qc == 7)))
                    p.wait("vector", mmc)
                    p.wait_dma("vector", "st")
                    tev_c = p.vector(lambda e, r=r: e.tensor_copy(crow[:], psC[:]))
                    p.wait("gpsimd", tev_c)
                    p.dma("st", g["cscr"][r, :, :], crow[:])
            buf_use[tb % 2] = mm
            p.wait_dma("scalar", "st")
            for j in range(4):
                p.dma("ld2", c4[:, j * TBW:(j + 1) * TBW], g["cscr"][:, j, :])
            if pending_ln is not None:
                pending_ln()
                pending_ln = None
            tbh = {}
            for oc in range(8):
                bank = bankA if oc % 2 == 0 else bankB
                j, half = oc // 2, oc % 2
                if oc >= 2:
                    p.wait("tensor", tbh[oc - 2])
                for hc in range(32):
                    p.tensor(lambda e, oc=oc, hc=hc, bank=bank: e.matmul(
                        bank[:], w2t[:, hc * D + oc * 128: hc * D + (oc + 1) * 128],
                        htile[:, hc * TBW:(hc + 1) * TBW],
                        start=(hc == 0), stop=False))
                if oc == 0:
                    p.wait_dma("tensor", "ld2")
                mm = p.tensor(lambda e, oc=oc, j=j, half=half, bank=bank: e.matmul(
                    bank[:], evs4[:, (j * 2 + half) * 128:(j * 2 + half + 1) * 128],
                    c4[:, j * TBW:(j + 1) * TBW],
                    start=False, stop=True))
                if oc > 0 or tb > 0:
                    p.wait("scalar", tlast_res)
                p.dma("ld2", xrt[:], xf_src[oc * 128:(oc + 1) * 128, cols[0]:cols[1]])
                p.wait("vector", mm)
                tb_ = p.vector(lambda e, oc=oc, bank=bank: e.tensor_scalar_add(
                    ycol[:, oc * TBW:(oc + 1) * TBW], bank[:], b2c[:, oc:oc + 1]))
                p.wait_dma("vector", "ld2")
                tres = p.vector(lambda e, oc=oc: e.tensor_add(
                    ycol[:, oc * TBW:(oc + 1) * TBW], ycol[:, oc * TBW:(oc + 1) * TBW],
                    xrt[:]))
                tbh[oc] = tb_
                tlast_use = mm
                tlast_res = tres
            prev_tbh = tbh
            if tb == 0 and _os.environ.get("KERNEL_DEBUG_B") == "1":
                dbg_h = nc.dram_tensor("dbg_h", [128, 32 * TBW], BF16, kind="ExternalOutput").ap()
                dbg_ab = nc.dram_tensor("dbg_ab", [1, 4 * TBW], BF16, kind="ExternalOutput").ap()
                dbg_y = nc.dram_tensor("dbg_y", [128, 8 * TBW], F32, kind="ExternalOutput").ap()
                p.barrier()
                p.dma("st", dbg_h[:], htile[:])
                p.dma("st", dbg_ab[:], ab[:])
                p.dma("st", dbg_y[:], ycol[:])
                p.barrier()
            def _mk_ln(tb=tb):
                def _f():
                    emit_ln(ctx, lnt, ycol, TBW,
                            lambda c, ap, tb=tb: p.dma("st", g["xf_B"][c * 128:(c + 1) * 128,
                                                                       tb * TBW:(tb + 1) * TBW], ap),
                            lambda c, ap, tb=tb: p.dma("st", g["xb_B"][c * 128:(c + 1) * 128,
                                                                       tb * TBW:(tb + 1) * TBW], ap),
                            ps_stat, ps_rep)
                return _f
            pending_ln = _mk_ln()
        pending_ln()

def emit_phase_c(ctx_outer, src):
    with ctx_outer.scope() as ctx:
        _emit_phase_c(ctx, src)


def _emit_phase_c(ctx, src):
    p, g, nc = ctx.p, ctx.g, ctx.nc
    xf_src, xb_src = g[src[0]], g[src[1]]
    TOT = float(2048 * 1024)

    w0b = ctx.sb("w0b", [128, 8 * FF], BF16)
    w1b = ctx.sb("w1b", [128, 32 * D], BF16)
    Dsc = ctx.sb("Dsc", [1, 1])
    Dvec = ctx.sb("Dvec", [128, 1])
    D2vec = ctx.sb("D2vec", [128, 1])
    with ctx.scope():
        wtmp = ctx.sb("wtmp_c", [128, FF])
        wbtmp = ctx.sb("wbtmp_c", [128, FF], BF16)
        tprev = None
        for dc in range(8):
            p.wait("sync", tprev)
            p.dma("ld", wtmp[:], g["w0"][dc * 128:(dc + 1) * 128, :])
            p.wait_dma("vector", "ld")
            tprev = p.vector(lambda e, dc=dc: e.tensor_copy(w0b[:, dc * FF:(dc + 1) * FF], wtmp[:]))
        for fc in range(32):
            p.wait("sync", tprev)
            p.dma("ld", wtmp[:, 0:D], g["w1f"][fc * 128:(fc + 1) * 128, :])
            p.wait_dma("vector", "ld")
            tprev = p.vector(lambda e, fc=fc: e.tensor_copy(w1b[:, fc * D:(fc + 1) * D], wtmp[:, 0:D]))
        for dc in range(8):
            p.wait("sync", tprev)
            p.dma("ld", wtmp[:], g["w2"][dc * 128:(dc + 1) * 128, :])
            p.wait_dma("vector", "ld")
            tc = p.vector(lambda e: e.tensor_copy(wbtmp[:], wtmp[:]))
            tprev = tc
            p.wait("gpsimd", tc)
            p.dma("st", g["w2b_s"][dc * 128:(dc + 1) * 128, :], wbtmp[:])
            p.wait_dma("vector", "st")
        p.vector(lambda e: e.memset(Dsc[:], 1.0))
        p.vector(lambda e: e.memset(Dvec[:], 1.0))
        p.vector(lambda e: e.memset(D2vec[:], 1.0))
        zrow = ctx.sb("zrow_c", [1, 8])
        tzz = p.vector(lambda e: e.memset(zrow[:], 0.0))
        p.wait("gpsimd", tzz)
        p.dma("cc", g["cc_in"][:], zrow[:])
        p.wait_dma("gpsimd", "cc")

    with ctx.scope():
        TBW = 256
        xcb = ctx.sb("xcb_c", [128, 8 * TBW], BF16)
        xcf = ctx.sb("xcf_c", [128, 8 * TBW])
        w2fc = [ctx.sb("w2fc0", [128, 8 * 128], BF16), ctx.sb("w2fc1", [128, 8 * 128], BF16)]
        htile = ctx.sb("htile_c", [128, 32 * TBW], BF16)
        stile = ctx.sb("stile_c", [128, TBW])
        vtile = ctx.sb("vtile_c", [128, TBW])
        ocol = ctx.sb("ocol_c", [128, 8 * TBW])
        dtile = ctx.sb("dtile_c", [128, TBW])
        lrow = ctx.sb("lrow_c", [1, TBW])
        ls = ctx.sb("ls_c", [1, 1])
        lg = ctx.sb("lg_c", [1, 1])
        dec = ctx.sb("dec_c", [1, 1])
        lnt = make_ln_tiles(ctx, TBW, "c")
        psU = ctx.psum("psU_c", [128, TBW])
        psV2 = ctx.psum("psV_c", [128, TBW])
        psO2 = [ctx.psum("psO2a_c", [128, TBW]), ctx.psum("psO2b_c", [128, TBW])]
        psM = ctx.psum("psM_c", [128, TBW])
        ps_stat = (ctx.psum("ps_st_c_s", [1, TBW]), ctx.psum("ps_st_c_q", [1, TBW]))
        ps_rep = ctx.psum("ps_rep_c", [128, 2 * TBW])

        t_u_evac = None
        t_v_evac = None
        t_x_use = None
        t_res = None
        last_th = None
        t_w2_use = {0: None, 1: None}
        for c in range(NCHUNK):
            cols = (c * TBW, (c + 1) * TBW)
            p.wait("sync", t_x_use)
            p.wait("sync", t_res)
            for dc in range(8):
                p.dma("ld", xcb[:, dc * TBW:(dc + 1) * TBW],
                      xb_src[dc * 128:(dc + 1) * 128, cols[0]:cols[1]])
                p.dma("ld", xcf[:, dc * TBW:(dc + 1) * TBW],
                      xf_src[dc * 128:(dc + 1) * 128, cols[0]:cols[1]])
            p.wait_dma("tensor", "ld")
            p.wait_dma("vector", "ld")
            for fc in range(32):
                buf = fc % 2
                p.wait("scalar", t_w2_use[buf])
                p.wait("gpsimd", t_w2_use[buf])
                p.dma("ld2", w2fc[buf][:, 0:512],
                      g["w2b_s"].rearrange("(cc pp) f -> pp cc f", pp=128)[:, 0:4, fc * 128:(fc + 1) * 128])
                p.dma("ld3", w2fc[buf][:, 512:1024],
                      g["w2b_s"].rearrange("(cc pp) f -> pp cc f", pp=128)[:, 4:8, fc * 128:(fc + 1) * 128])
                mmu = None
                p.wait("tensor", t_u_evac)
                for dc in range(8):
                    mmu = p.tensor(lambda e, fc=fc, dc=dc: e.matmul(
                        psU[:], w0b[:, dc * FF + fc * 128: dc * FF + (fc + 1) * 128],
                        xcb[:, dc * TBW:(dc + 1) * TBW],
                        start=(dc == 0), stop=(dc == 7)))
                p.wait_dma("tensor", "ld2")
                p.wait_dma("tensor", "ld3")
                mmv = None
                p.wait("tensor", t_v_evac)
                for dc in range(8):
                    mmv = p.tensor(lambda e, fc=fc, dc=dc, buf=buf: e.matmul(
                        psV2[:], w2fc[buf][:, dc * 128:(dc + 1) * 128],
                        xcb[:, dc * TBW:(dc + 1) * TBW],
                        start=(dc == 0), stop=(dc == 7)))
                t_w2_use[buf] = mmv
                t_x_use = mmv
                p.wait("scalar", mmu)
                p.wait("scalar", last_th)
                tsl = p.scalar(lambda e: e.activation(stile[:], psU[:], AF.Silu,
                                                      bias=ctx.zero128[:], scale=Dvec[:]))
                t_u_evac = tsl
                p.wait("vector", mmv)
                tvc = p.vector(lambda e: e.tensor_copy(vtile[:], psV2[:]))
                t_v_evac = tvc
                p.wait("vector", tsl)
                th = p.vector(lambda e, fc=fc: e.tensor_mul(
                    htile[:, fc * TBW:(fc + 1) * TBW], stile[:], vtile[:]))
                last_th = th
            p.wait("tensor", last_th)
            t_o_evac = {0: None, 1: None}
            for dm in range(8):
                bank = psO2[dm % 2]
                mm = None
                p.wait("tensor", t_o_evac[dm % 2])
                for fc in range(32):
                    mm = p.tensor(lambda e, dm=dm, fc=fc, bank=bank: e.matmul(
                        bank[:], w1b[:, fc * D + dm * 128: fc * D + (dm + 1) * 128],
                        htile[:, fc * TBW:(fc + 1) * TBW],
                        start=(fc == 0), stop=(fc == 31)))
                p.wait("vector", mm)
                tev = p.vector(lambda e, dm=dm, bank=bank: e.tensor_scalar(
                    ocol[:, dm * TBW:(dm + 1) * TBW], bank[:], D2vec[:], None, ALU.mult))
                t_o_evac[dm % 2] = tev
            mml = None
            for dm in range(8):
                tsub = p.vector(lambda e, dm=dm: e.tensor_sub(
                    dtile[:], ocol[:, dm * TBW:(dm + 1) * TBW], xcf[:, dm * TBW:(dm + 1) * TBW]))
                tsq = p.vector(lambda e: e.tensor_mul(dtile[:], dtile[:], dtile[:]))
                p.wait("tensor", tsq)
                mml = p.tensor(lambda e, dm=dm: e.matmul(
                    psM[0:1, 0:TBW], ctx.ones_col[:], dtile[:],
                    start=(dm == 0), stop=(dm == 7)))
                if dm < 7:
                    p.wait("vector", mml)
            p.wait("vector", mml)
            tlr = p.vector(lambda e: e.tensor_copy(lrow[:], psM[0:1, 0:TBW]))
            p.wait("tensor", tlr)
            tls = p.vector(lambda e: e.reduce_sum(ls[:], lrow[:], AX.X))
            p.wait("gpsimd", tls)
            p.dma("cc", g["cc_in"][0:1, 0:1], ls[:])
            p.wait_dma("gpsimd", "cc")
            p.cc(lambda e: e.collective_compute(
                "AllReduce", ALU.add, replica_groups=[list(range(NCORES))],
                ins=[g["cc_in"].opt()], outs=[g["cc_out"].opt()]))
            p.wait_cc("gpsimd")
            p.dma("cc", lg[:], g["cc_out"][0:1, 0:1])
            for dm in range(8):
                t_res = p.vector(lambda e, dm=dm: e.tensor_add(
                    ocol[:, dm * TBW:(dm + 1) * TBW], ocol[:, dm * TBW:(dm + 1) * TBW],
                    xcf[:, dm * TBW:(dm + 1) * TBW]))
            emit_ln(ctx, lnt, ocol, TBW,
                    lambda ch, ap, c=c: p.dma("st2", g["xf_C"][ch * 128:(ch + 1) * 128,
                                                               c * TBW:(c + 1) * TBW], ap),
                    lambda ch, ap, c=c: p.dma("st2", g["xb_C"][ch * 128:(ch + 1) * 128,
                                                               c * TBW:(c + 1) * TBW], ap),
                    ps_stat, ps_rep, st_stream="st2")
            p.wait_dma("vector", "cc")
            if _os.environ.get("KERNEL_DEBUG_C") == "1":
                if "dbg_loss" not in g:
                    g["dbg_loss"] = nc.dram_tensor("dbg_loss", [1, NCHUNK], F32,
                                                   kind="ExternalOutput").ap()
                    g["dbg_o"] = nc.dram_tensor("dbg_o", [128, 8 * 256], F32,
                                                kind="ExternalOutput").ap()
                p.wait_dma("gpsimd", "cc")
                p.dma("st", g["dbg_loss"][0:1, c:c + 1], lg[:])
            tdec = p.vector(lambda e: e.tensor_scalar(dec[:], lg[:], -LR / TOT, 1.0,
                                                      ALU.mult, ALU.add))
            p.wait("vector", t_u_evac)
            tds = p.vector(lambda e: e.tensor_mul(Dsc[:], Dsc[:], dec[:]))
            p.wait("tensor", tds)
            mmd = p.tensor(lambda e: e.matmul(psM[:, 0:1], ctx.ones_row[:], Dsc[:],
                                              start=True, stop=True))
            p.wait("vector", mmd)
            tdv = p.vector(lambda e: e.tensor_copy(Dvec[:], psM[:, 0:1]))
            p.wait("tensor", tdv)
            td2 = p.vector(lambda e: e.tensor_mul(D2vec[:], Dvec[:], Dvec[:]))
            p.wait("scalar", td2)
            if _os.environ.get("KERNEL_DEBUG_C") == "1" and c == NCHUNK - 1:
                p.barrier()
                p.dma("st", g["dbg_o"][:], ocol[:])
                p.barrier()



def emit_phase_d(ctx_outer, src):
    with ctx_outer.scope() as ctx:
        _emit_phase_d(ctx, src)


def _emit_phase_d(ctx, src):
    p, g, nc = ctx.p, ctx.g, ctx.nc
    xf_src = g[src[0]]

    wrt = ctx.sb("wrt", [128, 8 * D], BF16)
    wkt = ctx.sb("wkt", [128, 8 * D], BF16)
    wvt = ctx.sb("wvt", [128, 8 * D], BF16)
    murc = bias_chunks(ctx, g["mu_r"][0:1, :], 8, "mur")
    mukc = bias_chunks(ctx, g["mu_k"][0:1, :], 8, "muk")
    mur1c = ctx.sb("mur1c", [128, 8])
    muk1c = ctx.sb("muk1c", [128, 8])
    bd_use = ctx.sb("bd_use", [128, 128])
    p.vector(lambda e: e.tensor_scalar(mur1c[:], murc[:], -1.0, 1.0, ALU.mult, ALU.add))
    p.vector(lambda e: e.tensor_scalar(muk1c[:], mukc[:], -1.0, 1.0, ALU.mult, ALU.add))

    with ctx.scope():
        wrow = ctx.sb("wrow_d", [128, D])
        wrow_bf = ctx.sb("wrowbf_d", [128, D], BF16)
        psTd0 = ctx.psum("psTd0", [128, 1024], BF16)
        tprev = None
        for wname, wt in (("W_r", wrt), ("W_k", wkt), ("W_v", wvt)):
            for oc in range(8):
                p.wait("sync", tprev)
                p.dma("ld", wrow[:], g[wname][oc * 128:(oc + 1) * 128, :])
                p.wait_dma("vector", "ld")
                tc = p.vector(lambda e: e.tensor_copy(wrow_bf[:], wrow[:]))
                tprev = tc
                p.wait("tensor", tc)
                for cc in range(8):
                    slot = (cc % 2) * 512
                    tt = p.tensor(lambda e, cc=cc, slot=slot: e.transpose(
                        psTd0[:, slot:slot + 128], wrow_bf[:, cc * 128:(cc + 1) * 128],
                        ctx.ident_b[:]))
                    p.wait("vector", tt)
                    tev = p.vector(lambda e, cc=cc, oc=oc, wt=wt, slot=slot: e.tensor_copy(
                        wt[:, cc * D + oc * 128: cc * D + (oc + 1) * 128],
                        psTd0[:, slot:slot + 128]))
                    p.wait("tensor", tev)
                    tprev = tev

    with ctx.scope():
        bd = ctx.sb("bd", [128, 128])
        agt = ctx.sb("agt_d", [128, 8 * 128])
        pflagt = ctx.sb("pflagt", [128, 2])
        pbd1 = ctx.sb("pbd1", [128, 128])
        for dc in range(8):
            p.dma("ld", bd[:, dc * 16:(dc + 1) * 16],
                  xf_src[dc * 128:(dc + 1) * 128, 255:RPC:256],
                  allow_slow_non_contiguous=True)
        p.dma("ld", pflagt[:], g["pflag"][:])
        p.wait_dma("gpsimd", "ld")
        p.dma("cc", g["ag_bd_in"][:], bd[:])
        p.wait_dma("gpsimd", "cc")
        p.cc(lambda e: e.collective_compute(
            "AllGather", ALU.bypass, replica_groups=[list(range(NCORES))],
            ins=[g["ag_bd_in"].opt()], outs=[g["ag_bd_out"].opt()]))
        p.wait_cc("gpsimd")
        for j in range(NCORES):
            p.dma("cc", agt[:, j * 128:(j + 1) * 128], g["ag_bd_out"][j * 128:(j + 1) * 128, :])
        p.wait_dma("vector", "cc")
        tm0 = p.vector(lambda e: e.tensor_scalar_mul(bd_use[:], agt[:, 0:128], ctx.psel128[:, 0:1]))
        tcomb = tm0
        for j in range(1, NCORES):
            tcomb = p.vector(lambda e, j=j: e.scalar_tensor_tensor(
                bd_use[:], agt[:, j * 128:(j + 1) * 128], ctx.psel128[:, j:j + 1], bd_use[:],
                ALU.mult, ALU.add))
        for dc in range(8):
            p.vector(lambda e, dc=dc: e.memset(pbd1[:, dc * 16:dc * 16 + 1], 0.0))
            p.vector(lambda e, dc=dc: e.tensor_copy(
                pbd1[:, dc * 16 + 1:(dc + 1) * 16], bd_use[:, dc * 16:(dc + 1) * 16 - 1]))
        p.vector(lambda e: e.tensor_scalar_mul(bd_use[:], bd_use[:], pflagt[:, 0:1]))
        tbd = p.vector(lambda e: e.scalar_tensor_tensor(
            bd_use[:], pbd1[:], pflagt[:, 1:2], bd_use[:], ALU.mult, ALU.add))
        p.wait("tensor", tbd)
        p.wait("sync", tbd)

    with ctx.scope():
        TBW = 256
        xseg2 = [ctx.sb("xseg_d0", [128, 8 * 257]), ctx.sb("xseg_d1", [128, 8 * 257])]
        xr_b = ctx.sb("xr_d", [128, 8 * TBW], BF16)
        xk_b = ctx.sb("xk_d", [128, 8 * TBW], BF16)
        rsig = ctx.sb("rsig_d", [128, 8 * TBW])
        kk = ctx.sb("kk_d", [128, TBW])
        kact = ctx.sb("kact_d", [128, 8 * TBW], BF16)
        ycol = ctx.sb("ycol_d", [128, 8 * TBW])
        otok = ctx.sb("otok_d", [128, 8 * 128])
        tmpd = ctx.sb("tmp_d", [128, TBW])
        lnt = make_ln_tiles(ctx, TBW, "d")
        bankA = ctx.psum("bankA_d", [128, TBW])
        bankB = ctx.psum("bankB_d", [128, TBW])
        psTd = ctx.psum("psTd", [128, 1024])
        ps_stat = (ctx.psum("ps_st_d_s", [1, TBW]), ctx.psum("ps_st_d_q", [1, TBW]))
        ps_rep = ctx.psum("ps_rep_d", [128, 2 * TBW])

        def emit_loadmix(sg, wait_toks):
            xseg = xseg2[sg % 2]
            for dc in range(8):
                p.dma("ld", xseg[:, dc * 257 + 1:(dc + 1) * 257],
                      xf_src[dc * 128:(dc + 1) * 128, sg * TBW:(sg + 1) * TBW])
            p.wait_dma("vector", "ld")
            p.vector(lambda e, sg=sg, xseg=xseg: e.tensor_copy(
                xseg.rearrange("p (c s) -> p c s", s=257)[:, :, 0:1],
                bd_use.rearrange("p (c s) -> p c s", s=16)[:, :, sg:sg + 1]))
            for tok in wait_toks:
                p.wait("vector", tok)
            tmix = None
            for dc in range(8):
                p.vector(lambda e, dc=dc, xseg=xseg: e.tensor_scalar_mul(
                    tmpd[:], xseg[:, dc * 257 + 1:(dc + 1) * 257], murc[:, dc:dc + 1]))
                p.vector(lambda e, dc=dc, xseg=xseg: e.scalar_tensor_tensor(
                    xr_b[:, dc * TBW:(dc + 1) * TBW], xseg[:, dc * 257:(dc + 1) * 257 - 1],
                    mur1c[:, dc:dc + 1], tmpd[:], ALU.mult, ALU.add))
                p.vector(lambda e, dc=dc, xseg=xseg: e.tensor_scalar_mul(
                    tmpd[:], xseg[:, dc * 257 + 1:(dc + 1) * 257], mukc[:, dc:dc + 1]))
                tmix = p.vector(lambda e, dc=dc, xseg=xseg: e.scalar_tensor_tensor(
                    xk_b[:, dc * TBW:(dc + 1) * TBW], xseg[:, dc * 257:(dc + 1) * 257 - 1],
                    muk1c[:, dc:dc + 1], tmpd[:], ALU.mult, ALU.add))
            return tmix

        res_buf = {0: None, 1: None}
        tmix_next = None
        for sg in range(NCHUNK):
            xseg = xseg2[sg % 2]
            tmix = emit_loadmix(0, ()) if sg == 0 else tmix_next
            p.wait("tensor", tmix)
            t_ev = {0: None, 1: None}
            for dd in range(8):
                bank = bankA if dd % 2 == 0 else bankB
                p.wait("tensor", t_ev[dd % 2])
                mm = None
                for cc in range(8):
                    mm = p.tensor(lambda e, dd=dd, cc=cc, bank=bank: e.matmul(
                        bank[:], wrt[:, cc * D + dd * 128: cc * D + (dd + 1) * 128],
                        xr_b[:, cc * TBW:(cc + 1) * TBW],
                        start=(cc == 0), stop=(cc == 7)))
                p.wait("scalar", mm)
                tsg = p.scalar(lambda e, dd=dd, bank=bank: e.activation(
                    rsig[:, dd * TBW:(dd + 1) * TBW], bank[:], AF.Sigmoid,
                    bias=ctx.zero128[:], scale=1.0))
                t_ev[dd % 2] = tsg
            mm_r_last = mm
            for dd in range(8):
                bank = bankA if dd % 2 == 0 else bankB
                p.wait("tensor", t_ev[dd % 2])
                mm = None
                for cc in range(8):
                    mm = p.tensor(lambda e, dd=dd, cc=cc, bank=bank: e.matmul(
                        bank[:], wkt[:, cc * D + dd * 128: cc * D + (dd + 1) * 128],
                        xk_b[:, cc * TBW:(cc + 1) * TBW],
                        start=(cc == 0), stop=(cc == 7)))
                p.wait("scalar", mm)
                trl = p.scalar(lambda e, dd=dd, bank=bank: e.activation(
                    kk[:], bank[:], AF.Relu, bias=ctx.zero128[:], scale=1.0))
                p.wait("vector", trl)
                tk = p.vector(lambda e, dd=dd: e.tensor_mul(
                    kact[:, dd * TBW:(dd + 1) * TBW], kk[:], kk[:]))
                p.wait("scalar", tk)
                t_ev[dd % 2] = tk
            mm_k_last = mm
            if sg + 1 < NCHUNK:
                p.wait("sync", res_buf[(sg + 1) % 2])
                tmix_next = emit_loadmix(sg + 1, (mm_r_last, mm_k_last))
            p.wait("tensor", tk)
            t_evv = {0: None, 1: None}
            for dd in range(8):
                bank = bankA if dd % 2 == 0 else bankB
                p.wait("tensor", t_evv[dd % 2])
                mm = None
                for cc in range(8):
                    mm = p.tensor(lambda e, dd=dd, cc=cc, bank=bank: e.matmul(
                        bank[:], wvt[:, cc * D + dd * 128: cc * D + (dd + 1) * 128],
                        kact[:, cc * TBW:(cc + 1) * TBW],
                        start=(cc == 0), stop=(cc == 7)))
                p.wait("vector", mm)
                tmul = p.vector(lambda e, dd=dd, bank=bank: e.tensor_tensor(
                    ycol[:, dd * TBW:(dd + 1) * TBW], rsig[:, dd * TBW:(dd + 1) * TBW],
                    bank[:], ALU.mult))
                tres = p.vector(lambda e, dd=dd, xseg=xseg: e.tensor_add(
                    ycol[:, dd * TBW:(dd + 1) * TBW], ycol[:, dd * TBW:(dd + 1) * TBW],
                    xseg[:, dd * 257 + 1:(dd + 1) * 257]))
                t_evv[dd % 2] = tres
                res_buf[sg % 2] = tres
            tln = emit_ln(ctx, lnt, ycol, TBW,
                          lambda ch, ap: None, None, ps_stat, ps_rep)
            p.wait("tensor", tln)
            for t in range(2):
                ttr = None
                for dc in range(8):
                    ttr = p.tensor(lambda e, t=t, dc=dc: e.transpose(
                        psTd[:, dc * 128:(dc + 1) * 128],
                        ycol[:, dc * TBW + t * 128: dc * TBW + (t + 1) * 128],
                        ctx.ident_f[:]))
                p.wait("vector", ttr)
                p.wait_dma("vector", "st")
                tev = p.vector(lambda e: e.tensor_copy(otok[:], psTd[:]))
                p.wait("tensor", tev)
                p.wait("gpsimd", tev)
                p.dma("st", g["out_tok"][sg * TBW + t * 128: sg * TBW + (t + 1) * 128, :],
                      otok[:])

def build_program(phases=("A", "B", "C", "D"), debug_outputs=()):
    nc = bass.Bass("TRN2", target_bir_lowering=False)
    p = Prog(nc)
    es = ExitStack()
    ctx = Ctx(nc, p, es)
    g = ctx.g

    def dram(name, shape, dt=F32, kind="Internal"):
        if name in debug_outputs and kind == "Internal":
            kind = "ExternalOutput"
        g[name] = nc.dram_tensor(name, shape, dt, kind=kind).ap()
        return g[name]

    first, last = phases[0], phases[-1]

    if "A" in phases:
        dram("xb", [S, D], F32, kind="ExternalInput")
        dram("w_qkv", [D, H3], F32, kind="ExternalInput")
        dram("b_qkv", [1, H3], F32, kind="ExternalInput")
        dram("w_out", [D, D], F32, kind="ExternalInput")
        dram("b_out", [1, D], F32, kind="ExternalInput")
        dram("cos_q", [128, RPC], F32, kind="ExternalInput")
        dram("sin_q", [128, RPC], F32, kind="ExternalInput")
        dram("cos_k", [128, S], F32, kind="ExternalInput")
        dram("sin_k", [128, S], F32, kind="ExternalInput")
        dram("QT_s", [D, RPC], BF16)
        dram("KT_s", [D, S], BF16)
        dram("V_s", [S, D], BF16)
        dram("XR0", [D, RPC], F32)
    dram("ident_bf", [128, 128], BF16, kind="ExternalInput")
    dram("ident_f32", [128, 128], F32, kind="ExternalInput")
    if "B" in phases:
        dram("W1", [FF, D], F32, kind="ExternalInput")
        dram("b1", [1, FF], F32, kind="ExternalInput")
        dram("W2", [D, FF], F32, kind="ExternalInput")
        dram("b2", [1, D], F32, kind="ExternalInput")
        dram("lscale", [1, 1], F32, kind="ExternalInput")
        dram("bsel", [16, 8], F32, kind="ExternalInput")
        dram("ag_lt_in", [16, D], F32)
        dram("ag_lt_out", [128, D], F32)
        dram("lt_s", [16, D], F32)
        dram("evs_s", [16, 256], F32)
        dram("cscr", [4, 4, 256], BF16)
    if "C" in phases:
        dram("w0", [D, FF], F32, kind="ExternalInput")
        dram("w1f", [FF, D], F32, kind="ExternalInput")
        dram("w2", [D, FF], F32, kind="ExternalInput")
        dram("w2b_s", [D, FF], BF16)
        dram("cc_in", [1, 8], F32)
        dram("cc_out", [1, 8], F32)
    if "D" in phases:
        dram("W_r", [D, D], F32, kind="ExternalInput")
        dram("W_k", [D, D], F32, kind="ExternalInput")
        dram("W_v", [D, D], F32, kind="ExternalInput")
        dram("mu_r", [1, D], F32, kind="ExternalInput")
        dram("mu_k", [1, D], F32, kind="ExternalInput")
        dram("psel", [128, 8], F32, kind="ExternalInput")
        dram("pflag", [128, 2], F32, kind="ExternalInput")
        dram("ag_bd_in", [128, 128], F32)
        dram("ag_bd_out", [1024, 128], F32)
    if first != "A":
        dram("xT_in_f", [D, RPC], F32, kind="ExternalInput")
        dram("xT_in_b", [D, RPC], BF16, kind="ExternalInput")
    for ph in phases:
        if ph == "D":
            continue
        kind = "ExternalOutput" if ph == last else "Internal"
        dram(f"xf_{ph}", [D, RPC], F32, kind=kind)
        dram(f"xb_{ph}", [D, RPC], BF16, kind=kind)
    if "D" in phases:
        dram("out_tok", [RPC, D], F32, kind="ExternalOutput")

    with es:
        p.init_sems(es)
        p.dma_stream("ld", "sync")
        p.dma_stream("ld2", "scalar")
        p.dma_stream("ld3", "gpsimd")
        p.dma_stream("st2", "sync")
        p.dma_stream("st", "gpsimd")
        p.dma_stream("cc", "gpsimd")

        ctx.ident_b = ctx.sb("identb", [128, 128], BF16)
        ctx.ident_f = ctx.sb("identf", [128, 128], F32)
        ctx.ones_col = ctx.sb("ones_col", [128, 1], F32)
        ctx.ones_row = ctx.sb("ones_row", [1, 128], F32)
        ctx.ones_brow = ctx.sb("ones_brow", [1, 128], BF16)
        ctx.one11 = ctx.sb("one11", [1, 1], F32)
        ctx.eps1 = ctx.sb("eps1", [1, 1], F32)
        ctx.zero1 = ctx.sb("zero1", [1, 1], F32)
        ctx.zero128 = ctx.sb("zero128", [128, 1], F32)
        p.dma("ld", ctx.ident_b[:], g["ident_bf"][:])
        p.dma("ld", ctx.ident_f[:], g["ident_f32"][:])
        p.vector(lambda e: e.memset(ctx.ones_col[:], 1.0))
        p.vector(lambda e: e.memset(ctx.ones_row[:], 1.0))
        p.vector(lambda e: e.memset(ctx.ones_brow[:], 1.0))
        p.vector(lambda e: e.memset(ctx.one11[:], 1.0))
        if "D" in phases:
            ctx.psel128 = ctx.sb("psel128", [128, 8], F32)
            p.dma("ld", ctx.psel128[:], g["psel"][:])
        p.vector(lambda e: e.memset(ctx.eps1[:], LN_EPS))
        p.vector(lambda e: e.memset(ctx.zero1[:], 0.0))
        tz = p.vector(lambda e: e.memset(ctx.zero128[:], 0.0))
        p.wait("tensor", tz)
        p.wait("scalar", tz)
        p.wait_dma("tensor", "ld")

        for ph in phases:
            if ph == "A":
                emit_phase_a(ctx)
            elif ph == "B":
                emit_phase_b(ctx, src=("xf_A", "xb_A") if "A" in phases else ("xT_in_f", "xT_in_b"))
            elif ph == "C":
                idx = phases.index("C")
                src = (f"xf_{phases[idx-1]}", f"xb_{phases[idx-1]}") if idx > 0 else ("xT_in_f", "xT_in_b")
                emit_phase_c(ctx, src=src)
            elif ph == "D":
                idx = phases.index("D")
                src = (f"xf_{phases[idx-1]}", f"xb_{phases[idx-1]}") if idx > 0 else ("xT_in_f", "xT_in_b")
                emit_phase_d(ctx, src=src)
            p.barrier()

        with nc.Block() as block:
            p.emit_all(block)
    return nc


def host_constants(core):
    b, pp = core // 2, core % 2
    own_pos = np.concatenate(
        [np.arange(c * 512 + pp * 256, c * 512 + pp * 256 + 256) for c in range(NCHUNK)])
    other_pos = np.concatenate(
        [np.arange(c * 512 + (1 - pp) * 256, c * 512 + (1 - pp) * 256 + 256) for c in range(NCHUNK)])
    perm = np.concatenate([own_pos, other_pos])
    half = 32
    freqs = 1.0 / (500.0 ** (np.arange(half, dtype=np.float64) / half))

    def tables(pos, scale):
        ang = pos[None, :].astype(np.float64) * freqs[:, None]
        cos = np.cos(ang) * scale
        sin = np.sin(ang) * scale
        cos128 = np.tile(cos, (4, 1))
        sinS = np.concatenate([-sin, sin, -sin, sin], axis=0)
        return cos128.astype(np.float32), sinS.astype(np.float32)

    kscale = 1.0 / math.sqrt(DH)
    cq, sq = tables(own_pos, 1.0)
    ck, sk = tables(perm, kscale)
    bsel = np.zeros((16, 8), dtype=np.float32)
    bsel[:, 2 * b] = 1.0
    psel = np.zeros((128, 8), dtype=np.float32)
    psel[:, 2 * b + (1 - pp)] = 1.0
    pflag = np.zeros((128, 2), dtype=np.float32)
    pflag[:, 0] = 1.0 if pp == 1 else 0.0
    pflag[:, 1] = 1.0 if pp == 0 else 0.0
    return dict(own_pos=own_pos, perm=perm, cos_q=cq, sin_q=sq, cos_k=ck, sin_k=sk,
                bsel=bsel, psel=psel, pflag=pflag,
                ident_bf=np.eye(128, dtype=ml_dtypes.bfloat16),
                ident_f32=np.eye(128, dtype=np.float32))


def host_inputs_for_phases(inputs, phases, core, const, stream_f=None, stream_b=None):
    b = core // 2
    d = {}
    d["ident_bf"] = const["ident_bf"]
    d["ident_f32"] = const["ident_f32"]
    if "A" in phases:
        x = np.asarray(inputs["L0_attn_x"])
        d["xb"] = np.ascontiguousarray(x[b][const["perm"]])
        d["w_qkv"] = np.asarray(inputs["L0_attn_w_qkv"])
        d["b_qkv"] = np.asarray(inputs["L0_attn_b_qkv"]).reshape(1, -1)
        d["w_out"] = np.asarray(inputs["L0_attn_w_out"])
        d["b_out"] = np.asarray(inputs["L0_attn_b_out"]).reshape(1, -1)
        d["cos_q"], d["sin_q"] = const["cos_q"], const["sin_q"]
        d["cos_k"], d["sin_k"] = const["cos_k"], const["sin_k"]
    else:
        d["xT_in_f"] = stream_f[core]
        d["xT_in_b"] = stream_b[core]
    if "B" in phases:
        d["W1"] = np.asarray(inputs["L0_ffn_W1"])
        d["b1"] = np.asarray(inputs["L0_ffn_b1"]).reshape(1, -1)
        d["W2"] = np.asarray(inputs["L0_ffn_W2"])
        d["b2"] = np.asarray(inputs["L0_ffn_b2"]).reshape(1, -1)
        d["lscale"] = np.asarray(inputs["L0_ffn_scale"]).reshape(1, 1)
        d["bsel"] = const["bsel"]
    if "C" in phases:
        d["w0"] = np.asarray(inputs["L1_attn_w0"])
        d["w1f"] = np.asarray(inputs["L1_attn_w1"])
        d["w2"] = np.asarray(inputs["L1_attn_w2"])
    if "D" in phases:
        d["W_r"] = np.asarray(inputs["L1_ffn_W_r"])
        d["W_k"] = np.asarray(inputs["L1_ffn_W_k"])
        d["W_v"] = np.asarray(inputs["L1_ffn_W_v"])
        d["mu_r"] = np.asarray(inputs["L1_ffn_mu_r"]).reshape(1, -1)
        d["mu_k"] = np.asarray(inputs["L1_ffn_mu_k"]).reshape(1, -1)
        d["psel"] = const["psel"]
        d["pflag"] = const["pflag"]
    return d


def run_launch(inputs, phases, stream_f=None, stream_b=None, debug_outputs=()):
    nc = build_program(phases=tuple(phases), debug_outputs=debug_outputs)
    consts = [host_constants(c) for c in range(NCORES)]
    in_maps = [host_inputs_for_phases(inputs, phases, c, consts[c], stream_f, stream_b)
               for c in range(NCORES)]
    res = run_bass_kernel_spmd(nc, in_maps, core_ids=list(range(NCORES)))
    return res.results


def kernel(**inputs):
    consts = [host_constants(c) for c in range(NCORES)]
    if _os.environ.get("KERNEL_UNFUSED") == "1":
        groups = [("A",), ("B",), ("C",), ("D",)]
    else:
        groups = [("A", "B", "C", "D")]
    stream_f = stream_b = None
    out = None
    for phases in groups:
        results = run_launch(inputs, phases, stream_f, stream_b)
        last = phases[-1]
        if last != "D":
            stream_f = [np.asarray(results[c][f"xf_{last}"]) for c in range(NCORES)]
            stream_b = [np.asarray(results[c][f"xb_{last}"]) for c in range(NCORES)]
        else:
            out = np.zeros((B, S, D), dtype=np.float32)
            for c in range(NCORES):
                b = c // 2
                out[b][consts[c]["own_pos"]] = np.asarray(results[c]["out_tok"])
    return out
